# version 10
# baseline (speedup 1.0000x reference)
import contextlib
import numpy as np
import concourse.bass as bass
import concourse.mybir as mybir
from concourse.bass_utils import run_bass_kernel_spmd

F32 = mybir.dt.float32
F16 = mybir.dt.float16
BF16 = mybir.dt.bfloat16
I32 = mybir.dt.int32
AF = mybir.ActivationFunctionType
ALU = mybir.AluOpType
AX = mybir.AxisListType

T = 2048
E = 1024
H = 16
D = 64
I3 = 3072
NT = T // 128
NQ = T // 512
NE = E // 128
NI = I3 // 128
QMAX = 32767.0
EXPOFF = float(np.log(1024.0))


class Sem:
    __slots__ = ("h", "n", "name")

    def __init__(self, h, name):
        self.h = h
        self.n = 0
        self.name = name


class Em:

    CHAINED = ("vector", "scalar", "gpsimd")

    def __init__(self, nc, es):
        self.nc = nc
        self.es = es
        self.steps = []
        self.chains = {}

    def sem(self, name):
        return Sem(self.es.enter_context(self.nc.semaphore(name)), name)

    def set_chains(self, v, a, p):
        self.chains = {"vector": v, "scalar": a, "gpsimd": p}

    def emit(self, eng, fn, waits=(), inc=None, k=1, no_chain=False):
        chain_prev = -1
        if eng in self.CHAINED and not no_chain:
            ch = self.chains[eng]
            if inc is None:
                inc = ch
            assert inc is ch
            chain_prev = ch.n
        incs = ()
        if inc is not None:
            inc.n += k
            incs = ((inc, k),)
        self.steps.append((eng, fn, tuple(waits), incs, chain_prev))

    def dma(self, eng, out, in_, sem, waits=(), **kw):
        self.emit(eng, lambda e, o=out, i=in_, kw=kw: e.dma_start(out=o, in_=i, **kw),
                  waits=waits, inc=sem, k=16, no_chain=True)

    def finalize(self, block):
        final = {}
        for _, _, _, incs, _ in self.steps:
            for s, k in incs:
                final[s.name] = final.get(s.name, 0) + k
        for _, _, waits, _, _ in self.steps:
            for s, v in waits:
                assert v <= final.get(s.name, 0), \
                    f"wait {s.name}>={v} but final={final.get(s.name, 0)}"

        def make(name):
            def body(eng):
                last_w = {}
                chain = self.chains.get(name)
                for ename, fn, waits, incs, chain_prev in self.steps:
                    if ename != name:
                        continue
                    for s, v in waits:
                        if v > 0 and last_w.get(s.name, -1) < v:
                            eng.wait_ge(s.h, v)
                            last_w[s.name] = v
                    inst = fn(eng)
                    if chain_prev > 0:
                        inst._wait_ge(chain.h, chain_prev)
                    for s, k in incs:
                        inst.then_inc(s.h, k)
            return body
        block.sync(make("sync"))
        block.scalar(make("scalar"))
        block.vector(make("vector"))
        block.tensor(make("tensor"))
        block.gpsimd(make("gpsimd"))


def build(dbg=False):
    nc = bass.Bass("TRN2", target_bir_lowering=False,
                   dynamic_dma_scratch_size=2048)

    x_d = nc.dram_tensor("x", [T, E], F32, kind="ExternalInput").ap()
    wqkv_d = nc.dram_tensor("wqkv", [E, I3], F32, kind="ExternalInput").ap()
    bqkv_d = nc.dram_tensor("bqkv", [1, I3], F32, kind="ExternalInput").ap()
    wout_d = nc.dram_tensor("wout", [E, E], F32, kind="ExternalInput").ap()
    bout_d = nc.dram_tensor("bout", [1, E], F32, kind="ExternalInput").ap()
    wg_d = nc.dram_tensor("wg", [I3, E], F32, kind="ExternalInput").ap()
    wu_d = nc.dram_tensor("wu", [I3, E], F32, kind="ExternalInput").ap()
    wd_d = nc.dram_tensor("wd", [E, I3], F32, kind="ExternalInput").ap()
    y_d = nc.dram_tensor("y", [T, E], F32, kind="ExternalOutput").ap()

    def dscr(name, shape, dt):
        if dbg:
            return nc.dram_tensor(name, shape, dt, kind="ExternalOutput").ap()
        return nc.dram_tensor(name, shape, dt).ap()

    xn_h = dscr("xn_h", [T, E], F16)
    v_h = dscr("v_h", [E, T], BF16)
    wqg_h = dscr("wqg_h", [I3, E], F16)
    wqu_h = dscr("wqu_h", [I3, E], F16)
    wqd_h = dscr("wqd_h", [E, I3], F16)
    spg_h = dscr("spg_h", [I3, T], F16)
    spu_h = dscr("spu_h", [I3, T], F16)
    spg2_h = dscr("spg2_h", [I3, T], F16)
    spo_h = dscr("spo_h", [I3, T], F16)
    spq_h = dscr("spq_h", [I3, T], F16)
    cc_i = [nc.dram_tensor(f"cci{i}", [1, 2], F32).ap() for i in range(4)]
    cc_o = [nc.dram_tensor(f"cco{i}", [1, 2], F32).ap() for i in range(4)]
    id128_d = nc.inline_tensor(np.eye(128, dtype=np.float32), "id128d").ap()
    xnf_dbg = dscr("xnf_dbg", [128, 16384], F16) if dbg else None

    es = contextlib.ExitStack()
    with es:
        sb = lambda n, s, d: es.enter_context(nc.sbuf_tensor(n, s, d))
        arA = sb("arA", [128, 16384], F16)
        arB = sb("arB", [128, 16384], F16)
        arQ = sb("arQ", [128, 16384], F16)
        arK = sb("arK", [128, 16384], F16)
        arV = sb("arV", [128, 16640], F16)
        arW = sb("arW", [128, 11392], F16)
        st = sb("st", [128, 32], F32)
        nbE = sb("nbE", [128, 8], F32)
        nbO = sb("nbO", [128, 8], F32)
        bcol = sb("bcol", [128, 24], F32)
        bq8 = sb("bq8", [128, 24], F32)
        ob16 = sb("ob16", [1, E], F16)
        id128 = sb("id128", [128, 128], F32)
        ones32 = sb("ones32", [1, 128], F32)
        ones16 = sb("ones16", [1, 128], F16)
        sel2 = sb("sel2", [128, 2], F32)
        selE = sb("selE", [2, 128], F32)
        selO = sb("selO", [2, 128], F32)
        qn2 = sb("qn2", [2, 8], F32)
        kn2 = sb("kn2", [2, 8], F32)
        nbsrc = sb("nbsrc", [2, 8], F32)
        mcol = sb("mcol", [128, 8], F32)
        cst = sb("cst", [128, 2], F32)
        vland = sb("vland", [128, 256], F16)
        nc.const_aps.register(F32, 1e-5, cst[:, 0:1])

        COL = {k: i for i, k in enumerate(
            ["m", "var", "rstd", "t0", "t1", "t2",
             "s1", "i1", "s2", "i2", "s3", "i3", "hs3", "q3",
             "s4", "i4", "s42", "s5", "i5"])}

        def stc(name, p0=0, p1=128):
            c = COL[name]
            return st[p0:p1, c:c + 1]

        MC = {"xn": 0, "g": 1, "u": 2, "g2": 3, "o": 4, "g2b": 5, "ob": 6}

        def mc(name):
            return mcol[:, MC[name]:MC[name] + 1]

        pb = [es.enter_context(nc.psum_tensor(f"pb{i}", [128, 512], F32))
              for i in range(8)]

        block = es.enter_context(nc.Block())
        em = Em(nc, es)
        sv = em.sem("sv")
        sa = em.sem("sa")
        spl = em.sem("spl")
        stn = em.sem("stn")
        em.set_chains(sv, sa, spl)
        dxa = [em.sem("dxa0"), em.sem("dxa1")]
        dxn = em.sem("dxn")
        dxf = em.sem("dxf")
        dwl = [em.sem("dwl0"), em.sem("dwl1")]
        dlq = [em.sem("dlq0"), em.sem("dlq1")]
        dqw = em.sem("dqw")
        dgt = [em.sem("dgt0"), em.sem("dgt1")]
        dvw = em.sem("dvw")
        dvr = em.sem("dvr")
        dsp = em.sem("dsp")
        dgr = [em.sem("dgr0"), em.sem("dgr1")]
        dur = [em.sem("dur0"), em.sem("dur1")]
        dcc = em.sem("dcc")
        scc = em.sem("scc")
        doq = [em.sem("doq0"), em.sem("doq1")]
        dwt = em.sem("dwt")
        dxr = [em.sem("dxr0"), em.sem("dxr1")]
        dy = em.sem("dy")
        dmi = em.sem("dmi")

        V, A, P, TN, SY = "vector", "scalar", "gpsimd", "tensor", "sync"

        f32c = lambda ar, c0, n: ar[:, c0:c0 + 2 * n].bitcast(F32)
        i32c = lambda ar, c0, n: ar[:, c0:c0 + 2 * n].bitcast(I32)
        bf16c = lambda ar, c0, n: ar[:, c0:c0 + n].bitcast(BF16)

        xn_f = arA
        sq_f32 = f32c(arA, 10240, 2048)
        pt_bf = [bf16c(arA, 14336 + i * 512, 512) for i in range(4)]
        m_gin = [arA[:, 0:512], arA[:, 512:1024]]
        m_tf = [f32c(arA, 1024, 512), f32c(arA, 2048, 512)]
        m_i32 = [i32c(arA, 3072, 512), i32c(arA, 4096, 512)]
        m_gf = [f32c(arA, 5120, 512), f32c(arA, 6144, 512)]
        m_th = [f32c(arA, 7168, 512), f32c(arA, 8192, 512)]
        m_out = [arA[:, 9216:9728], arA[:, 9728:10240]]
        m_uf = [f32c(arA, 10240, 512), f32c(arA, 11264, 512)]
        m_ui32 = [i32c(arA, 12288, 512), i32c(arA, 13312, 512)]
        m_uin = [arA[:, 14336:14848], arA[:, 14848:15360]]

        wst = [f32c(arB, 0, 1024), f32c(arB, 2048, 1024)]
        wh = [arB[:, 4096:5120], arB[:, 5120:6144]]
        vst = [arB[:, 6144 + i * 512:6656 + i * 512].bitcast(BF16)
               for i in range(4)]
        xq_f = arB
        O_f = arB

        Q_f = arQ
        wdTh = arQ[:, 0:12288]
        wouth = arQ[:, 12288:16384]

        K_f = arK
        oqt = [arK[:, 0:3072], arK[:, 3072:6144]]
        xrr = [f32c(arK, 6144, 512), f32c(arK, 7168, 512)]
        yt = [f32c(arK, 8192, 512), f32c(arK, 9216, 512)]

        v_tok = arV

        lw_f32 = f32c(arW, 0, 1024)
        wgT = [arW[:, 2048:3072], arW[:, 3072:4096]]
        lw_tf = f32c(arW, 4096, 1024)
        lw_i32 = i32c(arW, 4096, 1024)
        lw_int16 = arW[:, 6144:7168]
        lw_q = [arW[:, 7168:8192], arW[:, 8192:9216]]
        lw_s = f32c(arW, 9216, 64)
        wo_st = f32c(arW, 0, 512)
        xqf32_t = f32c(arW, 9344, 512)
        xqi32_t = i32c(arW, 10368, 512)
        rbb_t = xqf32_t
        rb = xqf32_t[0:1, :]
        brow = lw_f32[0:24, 0:128]
        obf = lw_tf[0:1, :]
        xa_t = f32c(arQ, 0, 2048)
        xsq_t = f32c(arQ, 4096, 1024)
        xnt_t = arQ[:, 6144:8192]

        em.emit(V, lambda e: e.memset(cst[:, 0:1], 1e-5))
        em.emit(V, lambda e: e.memset(ones32[:, :], 1.0))
        em.emit(V, lambda e: e.memset(ones16[:, :], 1.0))
        em.emit(V, lambda e: e.memset(st[:, :], 0.0))
        em.emit(V, lambda e: e.memset(mcol[:, :], 0.0))
        em.emit(V, lambda e: e.memset(qn2[:, :], 0.0))
        em.emit(V, lambda e: e.memset(kn2[:, :], 0.0))
        em.emit(V, lambda e: e.memset(sel2[:, :], 0.0))
        em.emit(V, lambda e: e.memset(sel2[0:64, 0:1], 1.0))
        em.emit(V, lambda e: e.memset(sel2[64:128, 1:2], 1.0))

        def vtok_ones(e):
            ap = v_tok[:, :].bitcast(BF16).rearrange("p (n c) -> p n c", c=65)
            return e.memset(ap[:, :, 64:65], 1.0)
        em.emit(V, vtok_ones)
        em.emit(V, lambda e: e.memset(selE[:, :], 1.0))
        em.emit(V, lambda e: e.memset(selO[:, :], 1.0))
        SELV = sv.n
        em.emit(P, lambda e: e.affine_select(
            selE[:, :], selE[:, :], [[1, 128]], ALU.is_ge, 0.0,
            base=0, channel_multiplier=-128), waits=[(sv, SELV)])
        em.emit(P, lambda e: e.affine_select(
            selO[:, :], selO[:, :], [[1, 128]], ALU.is_ge, 0.0,
            base=-128, channel_multiplier=128))
        INIT_V = sv.n
        INIT_P = spl.n
        em.dma(A, id128[:, :], id128_d, dmi)
        em.dma(A, brow[:, :], bqkv_d.rearrange("a (c p) -> (a c) p", p=128), dmi)
        em.dma(A, obf[:, :], bout_d, dmi)
        INIT_D = dmi.n
        em.emit(TN, lambda e: e.transpose(pb[7][:, 0:24], brow[:, :],
                                          id128[0:24, 0:24]),
                waits=[(dmi, INIT_D), (sv, INIT_V)], inc=stn)
        BT = stn.n
        em.emit(A, lambda e: e.activation(bcol[:, :], pb[7][:, 0:24],
                                          AF.Identity), waits=[(stn, BT)])
        em.emit(A, lambda e: e.activation(bq8[:, :], pb[7][:, 0:24],
                                          AF.Identity, scale=0.125))
        BC_A = sa.n
        em.emit(A, lambda e: e.activation(ob16[:, :], obf[:, :], AF.Identity),
                waits=[(dmi, INIT_D)])
        OB_A = sa.n

        xn_done_v = []
        xn_done_a = []
        for it in range(NT):
            bi = it % 2
            xt = xa_t[:, bi * 1024:(bi + 1) * 1024]
            xnt = xnt_t[:, bi * 1024:(bi + 1) * 1024]
            w = []
            if it >= 2:
                w.append((sa, xn_done_a[it - 2]))
            em.dma(A, xt, x_d[it * 128:(it + 1) * 128, :], dxa[bi], waits=w)
            DX = dxa[bi].n
            wv = [(dxa[bi], DX)]
            em.emit(A, lambda e, xt=xt: e.activation(xsq_t[:, :], xt, AF.Square),
                    waits=wv + ([(sv, xn_done_v[it - 1])] if it >= 1 else []))
            SQ = sa.n
            em.emit(V, lambda e, xt=xt: e.tensor_reduce(
                stc("m"), xt, AX.X, ALU.add), waits=[(dxa[bi], DX)])
            em.emit(V, lambda e: e.tensor_scalar_mul(stc("m"), stc("m"), 1.0 / E))
            em.emit(V, lambda e: e.tensor_reduce(
                stc("var"), xsq_t[:, :], AX.X, ALU.add), waits=[(sa, SQ)])
            em.emit(V, lambda e: e.tensor_scalar_mul(stc("var"), stc("var"),
                                                     1.0 / E))
            em.emit(V, lambda e: e.scalar_tensor_tensor(
                stc("t0"), stc("m"), stc("m"), stc("var"),
                ALU.mult, ALU.subtract))
            VW = sv.n
            em.emit(A, lambda e: e.activation(
                stc("t1"), stc("t0"), AF.Sqrt, bias=1e-5, scale=-1.0),
                waits=[(sv, VW)])
            SD = sa.n
            em.emit(V, lambda e: e.reciprocal(stc("rstd"), stc("t1")),
                    waits=[(sa, SD)])
            em.emit(V, lambda e, xt=xt: e.tensor_scalar(
                xsq_t[:, :], xt, stc("m"), None, ALU.subtract))
            em.emit(V, lambda e: e.tensor_scalar(
                xsq_t[:, :], xsq_t[:, :], stc("rstd"), None, ALU.mult))
            XN = sv.n
            em.emit(V, lambda e: e.tensor_reduce(
                stc("t1"), xsq_t[:, :], AX.X, ALU.max,
                apply_absolute_value=True))
            em.emit(V, lambda e: e.tensor_tensor(
                mc("xn"), mc("xn"), stc("t1"), ALU.max))
            xn_done_v.append(sv.n)
            em.emit(A, lambda e, xnt=xnt: e.activation(
                xnt, xsq_t[:, :], AF.Identity),
                waits=[(sv, XN)] + ([(dxn, 16 * (it - 1))] if it >= 2 else []))
            xn_done_a.append(sa.n)
            em.dma(SY, xn_h[it * 128:(it + 1) * 128, :], xnt, dxn,
                   waits=[(sa, xn_done_a[-1])])
            for c in range(NE):
                em.dma(SY, xn_f[:, c * T + it * 128:c * T + (it + 1) * 128],
                       xn_h[it * 128:(it + 1) * 128, c * 128:(c + 1) * 128],
                       dxf, waits=[(dxn, dxn.n)], transpose=True)
        LN_V = sv.n
        XNF_D = dxf.n

        def fold_scalar(src_col, dst, waits):
            em.emit(TN, lambda e: e.transpose(
                pb[7][0:1, 0:128], src_col, id128[:, :]),
                waits=list(waits) + [(dmi, INIT_D), (sv, sv.n), (sa, sa.n)],
                inc=stn)
            TW = stn.n
            em.emit(V, lambda e, dst=dst: e.tensor_reduce(
                stc(dst, 0, 1), pb[7][0:1, 0:128], AX.X, ALU.max),
                waits=[(stn, TW)])
            return sv.n

        MX_V = fold_scalar(mc("xn"), "t0", [(sv, LN_V)])

        def collective(ci, waits):
            em.dma(P, cc_i[ci], st[0:1, COL["t0"]:COL["t0"] + 2], dcc,
                   waits=waits)
            em.emit(P, lambda e, ci=ci: e.collective_compute(
                "AllReduce", ALU.max, [[0, 1, 2, 3, 4, 5, 6, 7]],
                [cc_i[ci].opt()], [cc_o[ci].opt()]),
                waits=[(dcc, dcc.n)], inc=scc, no_chain=True)
            return scc.n

        CC1 = collective(0, [(sv, MX_V)])
        em.dma(P, st[0:1, COL["t0"]:COL["t0"] + 2], cc_o[0], dcc,
               waits=[(scc, CC1)])
        CC1D = dcc.n

        def bcast_scale(srccol, scol, icol, waits, extra=()):
            em.emit(TN, lambda e: e.matmul(
                pb[7][:, 0:1], ones32[0:1, 0:128],
                st[0:1, COL[srccol]:COL[srccol] + 1], start=True, stop=True),
                waits=list(waits) + [(sv, sv.n), (sa, sa.n)], inc=stn)
            TW = stn.n
            em.emit(V, lambda e: e.tensor_scalar(
                stc(scol), pb[7][:, 0:1], 1.0 / QMAX, 1e-12, ALU.mult, ALU.max),
                waits=[(stn, TW)])
            em.emit(V, lambda e: e.reciprocal(stc(icol), stc(scol)))
            for f in extra:
                f()
            return sv.n

        S1_V = bcast_scale("t0", "s1", "i1", [(dcc, CC1D)])

        lqst = {"n": 0, "a": 0, "p": 0, "wr": 0, "q": [0, 0]}

        def lpbq_tile(src_ap, dst_ap):
            k = lqst["n"]
            lqst["n"] += 1
            w = [(dmi, INIT_D)]
            if k == 0:
                w += [(stn, BT)]
            if k >= 1:
                w += [(spl, lqst["p"])]
            em.dma(A, lw_f32, src_ap, dlq[k % 2], waits=w)
            RD = dlq[k % 2].n
            wv3 = lw_f32.rearrange("p (nb e) -> p nb e", e=32)
            war = [(spl, lqst["p"])] if k >= 1 else []
            em.emit(V, lambda e, wv3=wv3: e.tensor_reduce(
                lw_s[:, 0:32], wv3, AX.X, ALU.max, apply_absolute_value=True),
                waits=[(dlq[k % 2], RD)] + war)
            em.emit(V, lambda e: e.tensor_scalar(
                lw_s[:, 32:64], lw_s[:, 0:32], 1.0 / 7.0, 1e-12,
                ALU.mult, ALU.max))
            em.emit(V, lambda e: e.reciprocal(lw_s[:, 0:32], lw_s[:, 32:64]))
            SV_ = sv.n
            inv_b = lw_s[:, 0:32].unsqueeze(2).broadcast_to([128, 32, 32])
            s_b = lw_s[:, 32:64].unsqueeze(2).broadcast_to([128, 32, 32])
            tfv = lw_tf.rearrange("p (nb e) -> p nb e", e=32)
            war_t = [(sa, lqst["a"])] if k >= 1 else []
            war_t2 = [(sa, OB_A)] if k == 0 else []
            em.emit(P, lambda e, wv3=wv3, inv_b=inv_b, tfv=tfv: e.tensor_tensor(
                tfv, wv3, inv_b, ALU.mult), waits=[(sv, SV_)] + war_t + war_t2)
            TM = spl.n
            lqst["p"] = TM
            em.emit(A, lambda e: e.activation(lw_i32[:, :], lw_tf, AF.Identity),
                    waits=[(spl, TM)])
            em.emit(A, lambda e: e.activation(lw_int16[:, :], lw_i32[:, :],
                                              AF.Identity))
            IA = sa.n
            lqst["a"] = IA
            qs = lw_q[k % 2]
            war_q = [(dqw, lqst["q"][k % 2])]
            em.emit(P, lambda e, qs=qs, s_b=s_b: e.tensor_tensor(
                qs.rearrange("p (nb e) -> p nb e", e=32),
                lw_int16.rearrange("p (nb e) -> p nb e", e=32),
                s_b, ALU.mult), waits=[(sa, IA)] + war_q)
            QP = spl.n
            em.dma(SY, dst_ap, qs, dqw, waits=[(spl, QP)])
            lqst["wr"] = dqw.n
            lqst["q"][k % 2] = dqw.n

        def lpbq_jobs(w_src, w_dst, nrows, ncols):
            out = []
            for rc in range(nrows // 128):
                for sc in range(ncols // 1024):
                    out.append((w_src[rc * 128:(rc + 1) * 128,
                                      sc * 1024:(sc + 1) * 1024],
                                w_dst[rc * 128:(rc + 1) * 128,
                                      sc * 1024:(sc + 1) * 1024]))
            return out

        side_jobs = (lpbq_jobs(wg_d, wqg_h, I3, E) + lpbq_jobs(wu_d, wqu_h, I3, E)
                     + lpbq_jobs(wd_d, wqd_h, E, I3))
        side_it = iter(side_jobs)

        def pop_side(n=1):
            for _ in range(n):
                job = next(side_it, None)
                if job is not None:
                    lpbq_tile(*job)

        wqkv_r = wqkv_d.rearrange("(c p) f -> p c f", p=128)
        qkv_conv_a = [0, 0]
        whmm = [0, 0]
        evq = [0, 0]
        vwd = [0, 0, 0, 0]
        vst_a = [0, 0, 0, 0]

        for m in range(24):
            b = m % 2
            j, r3 = divmod(m, 3)
            w = [(sa, qkv_conv_a[b])] if m >= 2 else []
            for k in range(NE):
                em.dma(A, wst[b][:, k * 128:(k + 1) * 128],
                       wqkv_r[:, k, m * 128:(m + 1) * 128], dwl[b],
                       waits=(w if k == 0 else []))
            DW = dwl[b].n
            wconv = [(dwl[b], DW)]
            if m >= 2:
                wconv.append((stn, whmm[b]))
            em.emit(A, lambda e, b=b: e.activation(wh[b][:, :], wst[b][:, :],
                                                   AF.Identity), waits=wconv)
            WH = sa.n
            qkv_conv_a[b] = WH
            whv = wh[b].rearrange("p (c f) -> p c f", c=NE)
            for t in range(NQ):
                bk = (m * NQ + t) % 2
                wl = [(sa, WH), (dxf, XNF_D)]
                if m * NQ + t >= 2:
                    wl.append((sa, evq[bk]))
                for k in range(NE):
                    em.emit(TN, lambda e, bk=bk, k=k, t=t, whv=whv: e.matmul(
                        pb[bk][:, :], whv[:, k, :],
                        xn_f[:, k * T + t * 512:k * T + (t + 1) * 512],
                        start=(k == 0), stop=(k == NE - 1)),
                        waits=(wl if k == 0 else ()),
                        inc=(stn if k == NE - 1 else None))
                MM = stn.n
                if r3 == 0:
                    em.emit(A, lambda e, bk=bk, j=j, t=t, m=m: e.activation(
                        Q_f[0:64, j * T + t * 512:j * T + (t + 1) * 512],
                        pb[bk][0:64, :], AF.Identity,
                        bias=bq8[0:64, m:m + 1], scale=0.125),
                        waits=[(stn, MM), (sa, BC_A)])
                    em.emit(A, lambda e, bk=bk, j=j, t=t, m=m: e.activation(
                        K_f[0:64, j * T + t * 512:j * T + (t + 1) * 512],
                        pb[bk][64:128, :], AF.Identity,
                        bias=bcol[64:128, m:m + 1]))
                elif r3 == 1:
                    em.emit(A, lambda e, bk=bk, t=t, m=m: e.activation(
                        vst[t][0:64, :], pb[bk][0:64, :], AF.Identity,
                        bias=bcol[0:64, m:m + 1]),
                        waits=[(stn, MM), (sa, BC_A), (dvw, vwd[t])])
                    em.emit(A, lambda e, bk=bk, j=j, t=t, m=m: e.activation(
                        Q_f[64:128, j * T + t * 512:j * T + (t + 1) * 512],
                        pb[bk][64:128, :], AF.Identity,
                        bias=bq8[64:128, m:m + 1], scale=0.125))
                else:
                    em.emit(A, lambda e, bk=bk, j=j, t=t, m=m: e.activation(
                        K_f[64:128, j * T + t * 512:j * T + (t + 1) * 512],
                        pb[bk][0:64, :], AF.Identity,
                        bias=bcol[0:64, m:m + 1]),
                        waits=[(stn, MM), (sa, BC_A)])
                    em.emit(A, lambda e, bk=bk, t=t, m=m: e.activation(
                        vst[t][64:128, :], pb[bk][64:128, :], AF.Identity,
                        bias=bcol[64:128, m:m + 1]), waits=[(dvw, vwd[t])])
                    vst_a[t] = sa.n
                    em.dma(SY, v_h[j * 128:(j + 1) * 128,
                                   t * 512:(t + 1) * 512], vst[t], dvw,
                           waits=[(sa, vst_a[t])])
                    vwd[t] = dvw.n
                evq[bk] = sa.n
            whmm[b] = stn.n
            if m % 2 == 1:
                pop_side(2)
        QKV_MM = stn.n
        QKV_A = sa.n
        VW_D = dvw.n
        if dbg:
            em.dma(SY, xnf_dbg, arA[:, :], dmi, waits=[(stn, QKV_MM)])

        vcp = [0, 0]
        for idx in range(NT * NE):
            kt, j = divmod(idx, NE)
            r = idx % 2
            land = vland[:, r * 128:(r + 1) * 128].bitcast(BF16)
            w = [(dvw, VW_D)]
            if idx >= 2:
                w.append((spl, vcp[r]))
            em.dma(A, land, v_h[j * 128:(j + 1) * 128,
                                kt * 128:(kt + 1) * 128],
                   dvr, waits=w, transpose=True)
            DVT = dvr.n
            s0 = (kt * H + 2 * j) * 65
            s1 = (kt * H + 2 * j + 1) * 65
            em.emit(P, lambda e, land=land, s0=s0: e.tensor_copy(
                v_tok[:, s0:s0 + 64].bitcast(BF16), land[:, 0:64]),
                waits=[(dvr, DVT), (sv, INIT_V)])
            em.emit(P, lambda e, land=land, s1=s1: e.tensor_copy(
                v_tok[:, s1:s1 + 64].bitcast(BF16), land[:, 64:128]))
            vcp[r] = spl.n
        VT_P = spl.n
        VT_D = dvr.n

        xq_a = [0]
        first_xq = True
        for c in range(NE):
            for t in range(NQ):
                wv = [(sv, S1_V)]
                if first_xq:
                    wv += [(stn, QKV_MM), (sa, QKV_A), (dvw, VW_D)]
                    first_xq = False
                if xq_a[0]:
                    wv.append((sa, xq_a[0]))
                em.emit(V, lambda e, c=c, t=t: e.tensor_scalar(
                    xqf32_t, xn_f[:, c * T + t * 512:c * T + (t + 1) * 512],
                    stc("i1"), -32768.0, ALU.mult, ALU.max), waits=wv)
                em.emit(V, lambda e: e.tensor_scalar(
                    xqi32_t, xqf32_t, 32767.0, None, ALU.min))
                I32V = sv.n
                em.emit(A, lambda e, c=c, t=t: e.activation(
                    xq_f[:, c * T + t * 512:c * T + (t + 1) * 512],
                    xqi32_t, AF.Identity), waits=[(sv, I32V)])
                xq_a[0] = sa.n
        XQ_A = sa.n
        XQ_V = sv.n

        spill_cnt = {"g": 0, "u": 0}
        gumax_v = {"g": 0, "u": 0}
        gu_ev = {"ev0": 0, "ev1": 0}
        gumm = [0, 0]
        gusp = [0, 0]

        def gu_load(wq_h, i, wq_wr_need):
            b = i % 2
            w = [(dqw, wq_wr_need)]
            if i >= 2:
                w.append((stn, gumm[b]))
            for k in range(NE):
                em.dma(SY, wgT[b][:, k * 128:(k + 1) * 128],
                       wq_h[i * 128:(i + 1) * 128, k * 128:(k + 1) * 128],
                       dgt[b], waits=(w if k == 0 else []), transpose=True)
            return dgt[b].n

        def gu_stream(wq_h, sp_h, mx, wq_wr_need):
            gl = [0, 0]
            gl[0] = gu_load(wq_h, 0, wq_wr_need)
            for i in range(NI):
                b = i % 2
                if i + 1 < NI:
                    gl[(i + 1) % 2] = gu_load(wq_h, i + 1, wq_wr_need)
                DG = gl[b]
                for t in range(NQ):
                    bk = (i * NQ + t) % 2
                    wl = [(dgt[b], DG), (sa, XQ_A)]
                    if gu_ev[f"ev{bk}"]:
                        wl.append((sa, gu_ev[f"ev{bk}"]))
                    for k in range(NE):
                        em.emit(TN, lambda e, bk=bk, b=b, k=k, t=t: e.matmul(
                            pb[bk][:, :], wgT[b][:, k * 128:(k + 1) * 128],
                            xq_f[:, k * T + t * 512:k * T + (t + 1) * 512],
                            start=(k == 0), stop=(k == NE - 1)),
                            waits=(wl if k == 0 else ()),
                            inc=(stn if k == NE - 1 else None))
                    MM = stn.n
                    sl = m_out[bk]
                    em.emit(A, lambda e, sl=sl, bk=bk: e.activation(
                        sl, pb[bk][:, :], AF.Identity, scale=stc("s1")),
                        waits=[(stn, MM), (dsp, gusp[bk])])
                    EVA = sa.n
                    gu_ev[f"ev{bk}"] = EVA
                    em.emit(V, lambda e, sl=sl: e.tensor_reduce(
                        stc("t1"), sl, AX.X, ALU.max,
                        apply_absolute_value=True), waits=[(sa, EVA)])
                    em.emit(V, lambda e, mx=mx: e.tensor_tensor(
                        mc(mx), mc(mx), stc("t1"), ALU.max))
                    gumax_v[mx] = sv.n
                    em.dma(SY, sp_h[i * 128:(i + 1) * 128,
                                    t * 512:(t + 1) * 512], sl, dsp,
                           waits=[(sv, gumax_v[mx])])
                    gusp[bk] = dsp.n
                    spill_cnt[mx] += 1
                gumm[b] = stn.n
                if i % 2 == 1:
                    pop_side(1)

        pop_side(26)
        gu_stream(wqg_h, spg_h, "g", 16 * 24)
        pop_side(12)
        gu_stream(wqu_h, spu_h, "u", 16 * 48)
        pop_side(len(side_jobs))
        GU_MM = stn.n
        GU_SPILLS = 16 * (spill_cnt["g"] + spill_cnt["u"])
        LPBQ_WR = lqst["wr"]
        assert lqst["n"] == len(side_jobs)

        G_V = fold_scalar(mc("g"), "t0", [(sv, gumax_v["g"])])
        U_V = fold_scalar(mc("u"), "t1", [(sv, gumax_v["u"])])
        CC2 = collective(1, [(sv, U_V)])
        em.dma(P, st[0:1, COL["t0"]:COL["t0"] + 2], cc_o[1], dcc,
               waits=[(scc, CC2)])
        CC2D = dcc.n

        def s3_extra():
            em.emit(V, lambda e: e.tensor_scalar_mul(stc("hs3"), stc("s3"), 0.5))
            em.emit(V, lambda e: e.tensor_scalar_mul(stc("q3"), stc("s3"),
                                                     1.0 / 65536.0))
        S3_V = bcast_scale("t0", "s3", "i3", [(dcc, CC2D)], extra=(s3_extra,))
        S2_V = bcast_scale("t1", "s2", "i2", [(dcc, CC2D)])

        def norms(src, acc):
            for hc in range(NE):
                em.emit(A, lambda e, hc=hc, src=src: e.activation(
                    sq_f32[:, :], src[:, hc * T:(hc + 1) * T], AF.Square),
                    waits=[(stn, stn.n), (sv, XQ_V)])
                SQA = sa.n
                for t in range(NQ):
                    em.emit(TN, lambda e, t=t: e.matmul(
                        pb[6][0:2, :], sel2[:, :],
                        sq_f32[:, t * 512:(t + 1) * 512],
                        start=True, stop=True),
                        waits=[(sa, SQA), (sv, sv.n)], inc=stn)
                    TW = stn.n
                    em.emit(V, lambda e: e.tensor_reduce(
                        stc("t1", 0, 2), pb[6][0:2, :], AX.X, ALU.max),
                        waits=[(stn, TW)])
                    em.emit(V, lambda e, hc=hc, acc=acc: e.tensor_tensor(
                        acc[0:2, hc:hc + 1], acc[0:2, hc:hc + 1],
                        stc("t1", 0, 2), ALU.max))
            return sv.n

        QN_V = norms(Q_f, qn2)
        KN_V = norms(K_f, kn2)
        em.emit(V, lambda e: e.tensor_tensor(nbsrc[0:2, :], qn2[0:2, :],
                                             kn2[0:2, :], ALU.mult),
                waits=[(sv, KN_V)])
        NB1 = sv.n
        em.emit(A, lambda e: e.activation(nbsrc[0:2, :], nbsrc[0:2, :],
                                          AF.Sqrt), waits=[(sv, NB1)])
        NB2 = sa.n
        em.emit(V, lambda e: e.tensor_scalar(
            nbsrc[0:2, :], nbsrc[0:2, :], -1.0, EXPOFF, ALU.mult, ALU.add),
            waits=[(sa, NB2)])
        NB3 = sv.n
        em.emit(TN, lambda e: e.matmul(pb[6][:, 0:8], selE[:, :],
                                       nbsrc[0:2, :], start=True, stop=True),
                waits=[(sv, NB3), (spl, INIT_P), (sv, sv.n)], inc=stn)
        NBT1 = stn.n
        em.emit(A, lambda e: e.activation(nbE[:, :], pb[6][:, 0:8],
                                          AF.Identity), waits=[(stn, NBT1)])
        NBA1 = sa.n
        em.emit(TN, lambda e: e.matmul(pb[6][:, 0:8], selO[:, :],
                                       nbsrc[0:2, :], start=True, stop=True),
                waits=[(sa, NBA1)], inc=stn)
        NBT2 = stn.n
        em.emit(A, lambda e: e.activation(nbO[:, :], pb[6][:, 0:8],
                                          AF.Identity), waits=[(stn, NBT2)])
        NEGB_A = sa.n

        m1 = {"n": 0, "ec": [0, 0], "sp": [0, 0], "wr": 0, "rv": [0, 0]}

        def m1_tile():
            jn = m1["n"]
            m1["n"] += 1
            i, t = divmod(jn, NQ)
            r = jn % 2
            EE = V if r == 0 else P
            ES = sv if r == 0 else spl
            acc, tmp = ("g2", "t1") if r == 0 else ("g2b", "t2")
            w = [(dsp, GU_SPILLS)]
            if m1["ec"][r]:
                w.append((ES, m1["ec"][r]))
            if m1["sp"][r]:
                w.append((dsp, m1["sp"][r]))
            em.dma(SY, m_gin[r], spg_h[i * 128:(i + 1) * 128,
                                       t * 512:(t + 1) * 512], dgr[r], waits=w)
            RD = dgr[r].n
            em.emit(EE, lambda e, r=r: e.tensor_scalar(
                m_tf[r], m_gin[r], stc("i3"), -32768.0, ALU.mult, ALU.max),
                waits=[(dgr[r], RD), (sv, S3_V)])
            em.emit(EE, lambda e, r=r: e.tensor_scalar(
                m_i32[r], m_tf[r], 32767.0, None, ALU.min))
            em.emit(EE, lambda e, r=r: e.tensor_copy(m_gf[r], m_i32[r]))
            GF = ES.n
            em.emit(A, lambda e, r=r: e.activation(m_th[r], m_gf[r], AF.Tanh,
                                                   scale=stc("hs3")),
                    waits=[(ES, GF)])
            TH = sa.n
            em.emit(EE, lambda e, r=r: e.tensor_scalar(
                m_tf[r], m_th[r], 32768.0, 32768.0, ALU.mult, ALU.add),
                waits=[(sa, TH)])
            em.emit(EE, lambda e, r=r: e.tensor_scalar(
                m_i32[r], m_tf[r], 65535.0, None, ALU.min))
            em.emit(EE, lambda e, r=r: e.tensor_copy(m_th[r], m_i32[r]))
            em.emit(EE, lambda e, r=r: e.tensor_tensor(
                m_tf[r], m_gf[r], m_th[r], ALU.mult))
            em.emit(EE, lambda e, r=r: e.tensor_scalar(
                m_out[r], m_tf[r], stc("q3"), None, ALU.mult),
                waits=([(sv, m1["rv"][1])] if r == 1 and m1["rv"][1] else []))
            m1["ec"][r] = ES.n
            em.emit(V, lambda e, r=r, tmp=tmp: e.tensor_reduce(
                stc(tmp), m_out[r], AX.X, ALU.max, apply_absolute_value=True),
                waits=[(ES, m1["ec"][r])])
            em.emit(V, lambda e, acc=acc, tmp=tmp: e.tensor_tensor(
                mc(acc), mc(acc), stc(tmp), ALU.max))
            m1["rv"][r] = sv.n
            em.dma(SY, spg2_h[i * 128:(i + 1) * 128, t * 512:(t + 1) * 512],
                   m_out[r], dsp, waits=[(ES, m1["ec"][r])])
            m1["sp"][r] = dsp.n
            m1["wr"] = dsp.n

        ps_free = {0: 0, 1: 0, 2: 0, 3: 0}
        pt_free = [0, 0, 0, 0]
        o_ev = {4: 0, 5: 0}
        ep_a = [NEGB_A]
        si_ctr = [0]

        def attn_pair(hc):
            for qc in range(NQ):
                nkt = 4 * qc + 4
                for kt in range(nkt):
                    diag = kt >= 4 * qc
                    for par, (sbk0, obk, nb) in enumerate(
                            [(2, 4, nbE), (3, 5, nbO)]):
                        sbk = sbk0 if kt % 2 == 0 else par
                        si = si_ctr[0]
                        si_ctr[0] += 1
                        p0 = par * 64
                        w = [(sa, NEGB_A)]
                        if ps_free[sbk]:
                            w.append((sa, ps_free[sbk]))
                        em.emit(TN, lambda e, sbk=sbk, p0=p0, hc=hc, kt=kt,
                                qc=qc: e.matmul(
                            pb[sbk][:, :],
                            K_f[p0:p0 + 64,
                                hc * T + kt * 128:hc * T + (kt + 1) * 128],
                            Q_f[p0:p0 + 64,
                                hc * T + qc * 512:hc * T + (qc + 1) * 512],
                            start=True, stop=True), waits=w, inc=stn)
                        MMW = stn.n
                        slot = si % 4
                        we = [(stn, MMW)]
                        if pt_free[slot]:
                            we.append((stn, pt_free[slot]))
                        em.emit(A, lambda e, sbk=sbk, slot=slot, nb=nb, hc=hc:
                                e.activation(pt_bf[slot], pb[sbk][:, :],
                                             AF.Exp, bias=nb[:, hc:hc + 1]),
                                waits=we)
                        EXA = sa.n
                        ps_free[sbk] = EXA
                        if diag:
                            s = kt - 4 * qc
                            em.emit(P, lambda e, slot=slot, s=s:
                                    e.affine_select(
                                pt_bf[slot], pt_bf[slot], [[1, 512]],
                                ALU.is_ge, 0.0, base=-128 * s,
                                channel_multiplier=-1), waits=[(sa, EXA)])
                            rdy = (spl, spl.n)
                        else:
                            rdy = (sa, EXA)
                        h = 2 * hc + par
                        wpv = [(spl, VT_P), rdy]
                        if kt == 0 and o_ev[obk]:
                            wpv.append((sv, o_ev[obk]))
                        em.emit(TN, lambda e, obk=obk, kt=kt, h=h, slot=slot,
                                nkt=nkt: e.matmul(
                            pb[obk][0:65, :],
                            v_tok[:, (kt * H + h) * 65:(kt * H + h + 1) * 65]
                            .bitcast(BF16),
                            pt_bf[slot], start=(kt == 0), stop=(kt == nkt - 1)),
                            waits=wpv, inc=stn)
                        pt_free[slot] = stn.n
                for par, obk in ((0, 4), (1, 5)):
                    PVM = stn.n
                    em.emit(V, lambda e, obk=obk: e.tensor_scalar(
                        rb, pb[obk][64:65, :], 1e-12, None, ALU.add),
                        waits=[(stn, PVM)])
                    em.emit(V, lambda e: e.reciprocal(rb, rb))
                    RC = sv.n
                    em.emit(TN, lambda e: e.matmul(
                        pb[6][0:64, :], ones32[0:1, 0:64], rb,
                        start=True, stop=True),
                        waits=[(sv, RC), (sa, ep_a[0])], inc=stn)
                    BCW = stn.n
                    em.emit(A, lambda e: e.activation(
                        rbb_t[0:64, :], pb[6][0:64, :], AF.Identity),
                        waits=[(stn, BCW)])
                    ep_a[0] = sa.n
                    em.emit(V, lambda e, obk=obk, par=par, hc=hc, qc=qc:
                            e.tensor_tensor(
                        O_f[par * 64:par * 64 + 64,
                            hc * T + qc * 512:hc * T + (qc + 1) * 512],
                        pb[obk][0:64, :], rbb_t[0:64, :], ALU.mult),
                        waits=[(sa, ep_a[0])])
                    o_ev[obk] = sv.n

        for hc in range(NE):
            attn_pair(hc)
            if hc >= 2:
                for _ in range(16):
                    if m1["n"] < NI * NQ:
                        m1_tile()
        while m1["n"] < NI * NQ:
            m1_tile()
        ATT_MM = stn.n
        O_V = sv.n
        M1_WR = m1["wr"]

        em.emit(V, lambda e: e.tensor_tensor(
            mc("g2"), mc("g2"), mc("g2b"), ALU.max),
            waits=[(sv, m1["rv"][0]), (sv, m1["rv"][1])])
        G2C_V = sv.n
        G2_V = fold_scalar(mc("g2"), "t0", [(sv, G2C_V)])
        CC3 = collective(2, [(sv, G2_V)])
        em.dma(P, st[0:1, COL["t0"]:COL["t0"] + 2], cc_o[2], dcc,
               waits=[(scc, CC3)])
        CC3D = dcc.n

        def s4_extra():
            em.emit(V, lambda e: e.tensor_tensor(
                stc("s42"), stc("s4"), stc("s2"), ALU.mult))
        S4_V = bcast_scale("t0", "s4", "i4", [(dcc, CC3D), (sv, S2_V)],
                           extra=(s4_extra,))

        m2 = {"ec": [0, 0], "sp": [0, 0], "rv": [0, 0]}
        m2_pend = [None, None]
        for jn in range(NI * NQ):
            i, t = divmod(jn, NQ)
            r = jn % 2
            EE = V if r == 0 else P
            ES = sv if r == 0 else spl
            DQ = SY if r == 0 else A
            acc, tmp = ("o", "t1") if r == 0 else ("ob", "t2")
            w = [(dsp, M1_WR)]
            if m2["ec"][r]:
                w.append((ES, m2["ec"][r]))
            if m2["sp"][r]:
                w.append((dsp, m2["sp"][r]))
            em.dma(DQ, m_gin[r], spg2_h[i * 128:(i + 1) * 128,
                                        t * 512:(t + 1) * 512], dgr[r],
                   waits=w)
            RDg = dgr[r].n
            em.dma(DQ, m_uin[r], spu_h[i * 128:(i + 1) * 128,
                                       t * 512:(t + 1) * 512], dur[r],
                   waits=([(sv, O_V)] if jn < 2 else []))
            RDu = dur[r].n
            if m2_pend[r] is not None:
                m2_pend[r]()
                m2_pend[r] = None
            em.emit(EE, lambda e, r=r: e.tensor_scalar(
                m_tf[r], m_gin[r], stc("i4"), -32768.0, ALU.mult, ALU.max),
                waits=[(dgr[r], RDg), (sv, S4_V)])
            em.emit(EE, lambda e, r=r: e.tensor_scalar(
                m_i32[r], m_tf[r], 32767.0, None, ALU.min))
            em.emit(EE, lambda e, r=r: e.tensor_copy(m_gf[r], m_i32[r]))
            em.emit(EE, lambda e, r=r: e.tensor_scalar(
                m_tf[r], m_uin[r], stc("i2"), -32768.0, ALU.mult, ALU.max),
                waits=[(dur[r], RDu)])
            em.emit(EE, lambda e, r=r: e.tensor_scalar(
                m_ui32[r], m_tf[r], 32767.0, None, ALU.min))
            em.emit(EE, lambda e, r=r: e.tensor_copy(m_uf[r], m_ui32[r]))
            em.emit(EE, lambda e, r=r: e.tensor_tensor(
                m_tf[r], m_gf[r], m_uf[r], ALU.mult))
            em.emit(EE, lambda e, r=r: e.tensor_scalar(
                m_out[r], m_tf[r], stc("s42"), None, ALU.mult),
                waits=([(sv, m2["rv"][1])] if r == 1 and m2["rv"][1] else []))
            m2["ec"][r] = ES.n
            em.emit(V, lambda e, r=r, tmp=tmp: e.tensor_reduce(
                stc(tmp), m_out[r], AX.X, ALU.max, apply_absolute_value=True),
                waits=[(ES, m2["ec"][r])])
            em.emit(V, lambda e, acc=acc, tmp=tmp: e.tensor_tensor(
                mc(acc), mc(acc), stc(tmp), ALU.max))
            m2["rv"][r] = sv.n
            EC = m2["ec"][r]

            def spill(r=r, i=i, t=t, EC=EC, ES=ES, DQ=DQ):
                em.dma(DQ, spo_h[i * 128:(i + 1) * 128,
                                 t * 512:(t + 1) * 512],
                       m_out[r], dsp, waits=[(ES, EC)])
                m2["sp"][r] = dsp.n
            m2_pend[r] = spill
        for r in range(2):
            if m2_pend[r] is not None:
                m2_pend[r]()
                m2_pend[r] = None
        M2_WR = dsp.n

        em.emit(V, lambda e: e.tensor_tensor(
            mc("o"), mc("o"), mc("ob"), ALU.max),
            waits=[(sv, m2["rv"][0]), (sv, m2["rv"][1])])
        O5C_V = sv.n
        O5_V = fold_scalar(mc("o"), "t0", [(sv, O5C_V)])
        CC4 = collective(3, [(sv, O5_V)])
        em.dma(P, st[0:1, COL["t0"]:COL["t0"] + 2], cc_o[3], dcc,
               waits=[(scc, CC4)])
        CC4D = dcc.n
        S5_V = bcast_scale("t0", "s5", "i5", [(dcc, CC4D)])

        m3 = {"ec": [0, 0], "sp": [0, 0]}
        m3_pend = [None, None]
        for jn in range(NI * NQ):
            i, t = divmod(jn, NQ)
            r = jn % 2
            EE = V if r == 0 else P
            ES = sv if r == 0 else spl
            DQ = SY if r == 0 else A
            w = [(dsp, M2_WR)]
            if m3["ec"][r]:
                w.append((ES, m3["ec"][r]))
            if m3["sp"][r]:
                w.append((dsp, m3["sp"][r]))
            em.dma(DQ, m_gin[r], spo_h[i * 128:(i + 1) * 128,
                                       t * 512:(t + 1) * 512], dgr[r], waits=w)
            RD = dgr[r].n
            if m3_pend[r] is not None:
                m3_pend[r]()
                m3_pend[r] = None
            em.emit(EE, lambda e, r=r: e.tensor_scalar(
                m_tf[r], m_gin[r], stc("i5"), -32768.0, ALU.mult, ALU.max),
                waits=[(dgr[r], RD), (sv, S5_V)])
            em.emit(EE, lambda e, r=r: e.tensor_scalar(
                m_i32[r], m_tf[r], 32767.0, None, ALU.min))
            em.emit(EE, lambda e, r=r: e.tensor_copy(m_gf[r], m_i32[r]))
            em.emit(EE, lambda e, r=r: e.tensor_scalar(
                m_out[r], m_gf[r], stc("s5"), None, ALU.mult))
            m3["ec"][r] = ES.n
            EC = m3["ec"][r]

            def spill3(r=r, i=i, t=t, EC=EC, ES=ES, DQ=DQ):
                em.dma(DQ, spq_h[i * 128:(i + 1) * 128,
                                 t * 512:(t + 1) * 512],
                       m_out[r], dsp, waits=[(ES, EC)])
                m3["sp"][r] = dsp.n
            m3_pend[r] = spill3
        for r in range(2):
            if m3_pend[r] is not None:
                m3_pend[r]()
                m3_pend[r] = None
        M3_WR = dsp.n

        WH_A = [0]
        p8mm = [0, 0]
        p8y = [0, 0]
        ydma = []
        for eh in range(2):
            for i in range(NI):
                wfirst = ([(stn, ATT_MM), (dqw, LPBQ_WR)]
                          if (i == 0 and eh == 0) else
                          ([(sa, WH_A[0]), (stn, stn.n)]
                           if i == 0 and eh == 1 else []))
                em.dma(SY, wdTh[:, i * 512:(i + 1) * 512],
                       wqd_h[eh * 512:(eh + 1) * 512,
                             i * 128:(i + 1) * 128], dwt,
                       waits=wfirst, transpose=True)
            for k in range(NE):
                wld = [(spl, spl.n), (sv, sv.n)] if (k == 0 and eh == 0) \
                    else ([(sa, WH_A[0]), (stn, stn.n)] if k == 0
                          else [(sa, WH_A[0])])
                em.dma(A, wo_st[:, :],
                       wout_d[k * 128:(k + 1) * 128,
                              eh * 512:(eh + 1) * 512], dwl[0], waits=wld)
                WD = dwl[0].n
                wca = [(dwl[0], WD)]
                if eh == 0 and k == 0:
                    wca.append((stn, ATT_MM))
                em.emit(A, lambda e, k=k: e.activation(
                    wouth[:, k * 512:(k + 1) * 512], wo_st[:, :],
                    AF.Identity), waits=wca)
                WH_A[0] = sa.n
            WDT = dwt.n
            WHA = sa.n

            def p8_load(t):
                r = t % 2
                n8 = eh * NT + t
                w = [(dsp, M3_WR)]
                if n8 == 0:
                    w.append((stn, ATT_MM))
                if n8 >= 2:
                    w.append((stn, p8mm[r]))
                for i in range(NI):
                    em.dma(SY, oqt[r][:, i * 128:(i + 1) * 128],
                           spq_h[i * 128:(i + 1) * 128,
                                 t * 128:(t + 1) * 128], doq[r],
                           waits=(w if i == 0 else []))
                wx = [(sv, p8y[r])] if n8 >= 2 else [(stn, ATT_MM)]
                em.dma(A, xrr[r], x_d[t * 128:(t + 1) * 128,
                                      eh * 512:(eh + 1) * 512], dxr[r],
                       waits=wx)
                return doq[r].n, dxr[r].n

            pend_y = [None]
            lded = {0: p8_load(0)}
            for t in range(NT):
                r = t % 2
                n8 = eh * NT + t
                if t + 1 < NT:
                    lded[t + 1] = p8_load(t + 1)
                DOQ, DXR = lded.pop(t)
                bk = r
                wl = [(sv, O_V), (sa, WHA), (dwt, WDT), (doq[r], DOQ),
                      (sa, OB_A)]
                if n8 >= 2:
                    wl.append((sv, p8y[r]))
                for k in range(NE):
                    em.emit(TN, lambda e, bk=bk, k=k, t=t: e.matmul(
                        pb[bk][:, :],
                        O_f[:, k * T + t * 128:k * T + (t + 1) * 128],
                        wouth[:, k * 512:(k + 1) * 512],
                        start=(k == 0), stop=False),
                        waits=(wl if k == 0 else ()))
                for i in range(NI):
                    em.emit(TN, lambda e, bk=bk, i=i, r=r: e.matmul(
                        pb[bk][:, :], oqt[r][:, i * 128:(i + 1) * 128],
                        wdTh[:, i * 512:(i + 1) * 512],
                        start=False, stop=False))
                em.emit(TN, lambda e, bk=bk, eh=eh: e.matmul(
                    pb[bk][:, :], ones16[0:1, 0:128],
                    ob16[0:1, eh * 512:(eh + 1) * 512],
                    start=False, stop=True), inc=stn)
                MM = stn.n
                p8mm[r] = MM
                wy = [(stn, MM), (dxr[r], DXR)]
                if len(ydma) >= 2:
                    wy.append((dy, ydma[-2]))
                em.emit(V, lambda e, bk=bk, r=r: e.tensor_tensor(
                    yt[r], pb[bk][:, :], xrr[r], ALU.add), waits=wy)
                YV = sv.n
                p8y[r] = YV
                em.dma(SY, y_d[t * 128:(t + 1) * 128,
                               eh * 512:(eh + 1) * 512], yt[r], dy,
                       waits=[(sv, YV)])
                ydma.append(dy.n)

        em.emit(P, lambda e: e.engine_nop(), waits=[(dy, dy.n)], no_chain=True)

        n_inst = len(em.steps)
        em.finalize(block)
    ends = []
    for a in nc.m.functions[0].allocations:
        if not hasattr(a, "memorylocations"):
            continue
        for ml in a.memorylocations:
            nm = getattr(ml, "name", "")
            if nm in ("arA", "arB", "arQ", "arK", "arV", "arW", "st", "nbE",
                      "nbO", "bcol", "bq8", "ob16", "id128", "ones32",
                      "ones16", "sel2", "selE", "selO", "qn2", "kn2",
                      "nbsrc", "mcol", "cst", "vland"):
                ends.append(ml.addr + ml.size() // 128)
    assert max(ends) <= 196608, f"SBUF overflow: {max(ends)}"
    if dbg:
        print(f"[build] {n_inst} emitted steps")
    return nc


_NC_CACHE = {}


def kernel(L0_attn_x, L0_attn_qkv_w, L0_attn_qkv_b, L0_attn_out_w,
           L0_attn_out_b, L0_ffn_w_gate, L0_ffn_w_up, L0_ffn_w_down):
    x = np.ascontiguousarray(L0_attn_x, dtype=np.float32)
    wqkv = np.ascontiguousarray(L0_attn_qkv_w, dtype=np.float32).reshape(E, I3)
    bqkv = np.ascontiguousarray(L0_attn_qkv_b, dtype=np.float32).reshape(1, I3)
    wout = np.ascontiguousarray(L0_attn_out_w, dtype=np.float32).reshape(E, E)
    bout = np.ascontiguousarray(L0_attn_out_b, dtype=np.float32).reshape(1, E)
    wg = np.ascontiguousarray(L0_ffn_w_gate, dtype=np.float32)
    wu = np.ascontiguousarray(L0_ffn_w_up, dtype=np.float32)
    wd = np.ascontiguousarray(L0_ffn_w_down, dtype=np.float32)

    if "nc" not in _NC_CACHE:
        _NC_CACHE["nc"] = build()
    nc = _NC_CACHE["nc"]
    in_maps = [{"x": np.ascontiguousarray(x[i]), "wqkv": wqkv, "bqkv": bqkv,
                "wout": wout, "bout": bout, "wg": wg, "wu": wu, "wd": wd}
               for i in range(8)]
    res = run_bass_kernel_spmd(nc, in_maps, core_ids=list(range(8)))
    return np.stack([res.results[i]["y"] for i in range(8)], axis=0)


# version 11
# speedup vs baseline: 1.6046x; 1.6046x over previous
import contextlib
import numpy as np
import concourse.bass as bass
import concourse.mybir as mybir
from concourse.bass_utils import run_bass_kernel_spmd

F32 = mybir.dt.float32
F16 = mybir.dt.float16
BF16 = mybir.dt.bfloat16
I32 = mybir.dt.int32
AF = mybir.ActivationFunctionType
ALU = mybir.AluOpType
AX = mybir.AxisListType

T = 2048
E = 1024
H = 16
D = 64
I3 = 3072
NT = T // 128
NQ = T // 512
NE = E // 128
NI = I3 // 128
QMAX = 32767.0
EXPOFF = float(np.log(1024.0))


class Sem:
    __slots__ = ("h", "n", "name")

    def __init__(self, h, name):
        self.h = h
        self.n = 0
        self.name = name


class Em:

    CHAINED = ("vector", "scalar", "gpsimd")

    def __init__(self, nc, es):
        self.nc = nc
        self.es = es
        self.steps = []
        self.chains = {}

    def sem(self, name):
        return Sem(self.es.enter_context(self.nc.semaphore(name)), name)

    def set_chains(self, v, a, p):
        self.chains = {"vector": v, "scalar": a, "gpsimd": p}

    def emit(self, eng, fn, waits=(), inc=None, k=1, no_chain=False):
        chain_prev = -1
        if eng in self.CHAINED and not no_chain:
            ch = self.chains[eng]
            if inc is None:
                inc = ch
            assert inc is ch
            chain_prev = ch.n
        incs = ()
        if inc is not None:
            inc.n += k
            incs = ((inc, k),)
        self.steps.append((eng, fn, tuple(waits), incs, chain_prev))

    def dma(self, eng, out, in_, sem, waits=(), **kw):
        self.emit(eng, lambda e, o=out, i=in_, kw=kw: e.dma_start(out=o, in_=i, **kw),
                  waits=waits, inc=sem, k=16, no_chain=True)

    def finalize(self, block):
        final = {}
        for _, _, _, incs, _ in self.steps:
            for s, k in incs:
                final[s.name] = final.get(s.name, 0) + k
        for _, _, waits, _, _ in self.steps:
            for s, v in waits:
                assert v <= final.get(s.name, 0), \
                    f"wait {s.name}>={v} but final={final.get(s.name, 0)}"

        def make(name):
            def body(eng):
                last_w = {}
                chain = self.chains.get(name)
                for ename, fn, waits, incs, chain_prev in self.steps:
                    if ename != name:
                        continue
                    for s, v in waits:
                        if v > 0 and last_w.get(s.name, -1) < v:
                            eng.wait_ge(s.h, v)
                            last_w[s.name] = v
                    inst = fn(eng)
                    if chain_prev > 0:
                        inst._wait_ge(chain.h, chain_prev)
                    for s, k in incs:
                        inst.then_inc(s.h, k)
            return body
        block.sync(make("sync"))
        block.scalar(make("scalar"))
        block.vector(make("vector"))
        block.tensor(make("tensor"))
        block.gpsimd(make("gpsimd"))


def build(dbg=False):
    nc = bass.Bass("TRN2", target_bir_lowering=False,
                   dynamic_dma_scratch_size=2048)

    x_d = nc.dram_tensor("x", [T, E], F32, kind="ExternalInput").ap()
    wqkv_d = nc.dram_tensor("wqkv", [E, I3], F32, kind="ExternalInput").ap()
    bqkv_d = nc.dram_tensor("bqkv", [1, I3], F32, kind="ExternalInput").ap()
    wout_d = nc.dram_tensor("wout", [E, E], F32, kind="ExternalInput").ap()
    bout_d = nc.dram_tensor("bout", [1, E], F32, kind="ExternalInput").ap()
    wg_d = nc.dram_tensor("wg", [I3, E], F32, kind="ExternalInput").ap()
    wu_d = nc.dram_tensor("wu", [I3, E], F32, kind="ExternalInput").ap()
    wd_d = nc.dram_tensor("wd", [E, I3], F32, kind="ExternalInput").ap()
    y_d = nc.dram_tensor("y", [T, E], F32, kind="ExternalOutput").ap()

    def dscr(name, shape, dt):
        if dbg:
            return nc.dram_tensor(name, shape, dt, kind="ExternalOutput").ap()
        return nc.dram_tensor(name, shape, dt).ap()

    xn_h = dscr("xn_h", [T, E], F16)
    v_h = dscr("v_h", [E, T], BF16)
    wqg_h = dscr("wqg_h", [I3, E], F16)
    wqu_h = dscr("wqu_h", [I3, E], F16)
    wqd_h = dscr("wqd_h", [E, I3], F16)
    spg_h = dscr("spg_h", [I3, T], F16)
    spu_h = dscr("spu_h", [I3, T], F16)
    spg2_h = dscr("spg2_h", [I3, T], F16)
    spo_h = dscr("spo_h", [I3, T], F16)
    spq_h = dscr("spq_h", [I3, T], F16)
    cc_i = [nc.dram_tensor(f"cci{i}", [1, 2], F32).ap() for i in range(4)]
    cc_o = [nc.dram_tensor(f"cco{i}", [1, 2], F32).ap() for i in range(4)]
    id128_d = nc.inline_tensor(np.eye(128, dtype=np.float32), "id128d").ap()
    xnf_dbg = dscr("xnf_dbg", [128, 16384], F16) if dbg else None

    es = contextlib.ExitStack()
    with es:
        sb = lambda n, s, d: es.enter_context(nc.sbuf_tensor(n, s, d))
        arA = sb("arA", [128, 16384], F16)
        arB = sb("arB", [128, 16384], F16)
        arQ = sb("arQ", [128, 16384], F16)
        arK = sb("arK", [128, 16384], F16)
        arV = sb("arV", [128, 16640], F16)
        arW = sb("arW", [128, 11392], F16)
        st = sb("st", [128, 32], F32)
        nbE = sb("nbE", [128, 8], F32)
        nbO = sb("nbO", [128, 8], F32)
        bcol = sb("bcol", [128, 24], F32)
        bq8 = sb("bq8", [128, 24], F32)
        ob16 = sb("ob16", [1, E], F16)
        id128 = sb("id128", [128, 128], F32)
        ones32 = sb("ones32", [1, 128], F32)
        ones16 = sb("ones16", [1, 128], F16)
        sel2 = sb("sel2", [128, 2], F32)
        selE = sb("selE", [2, 128], F32)
        selO = sb("selO", [2, 128], F32)
        qn2 = sb("qn2", [2, 8], F32)
        kn2 = sb("kn2", [2, 8], F32)
        nbsrc = sb("nbsrc", [2, 8], F32)
        mcol = sb("mcol", [128, 8], F32)
        cst = sb("cst", [128, 2], F32)
        vland = sb("vland", [128, 256], F16)
        nc.const_aps.register(F32, 1e-5, cst[:, 0:1])

        COL = {k: i for i, k in enumerate(
            ["m", "var", "rstd", "t0", "t1", "t2",
             "s1", "i1", "s2", "i2", "s3", "i3", "hs3", "q3",
             "s4", "i4", "s42", "s5", "i5"])}

        def stc(name, p0=0, p1=128):
            c = COL[name]
            return st[p0:p1, c:c + 1]

        MC = {"xn": 0, "g": 1, "u": 2, "g2": 3, "o": 4, "g2b": 5, "ob": 6}

        def mc(name):
            return mcol[:, MC[name]:MC[name] + 1]

        pb = [es.enter_context(nc.psum_tensor(f"pb{i}", [128, 512], F32))
              for i in range(8)]

        block = es.enter_context(nc.Block())
        em = Em(nc, es)
        sv = em.sem("sv")
        sa = em.sem("sa")
        spl = em.sem("spl")
        stn = em.sem("stn")
        em.set_chains(sv, sa, spl)
        dxa = [em.sem("dxa0"), em.sem("dxa1")]
        dxn = em.sem("dxn")
        dxf = em.sem("dxf")
        dwl = [em.sem("dwl0"), em.sem("dwl1")]
        dlq = [em.sem("dlq0"), em.sem("dlq1")]
        dqw = em.sem("dqw")
        dgt = [em.sem("dgt0"), em.sem("dgt1")]
        dvw = em.sem("dvw")
        dvr = em.sem("dvr")
        dsp = em.sem("dsp")
        dgr = [em.sem("dgr0"), em.sem("dgr1")]
        dur = [em.sem("dur0"), em.sem("dur1")]
        dcc = em.sem("dcc")
        scc = em.sem("scc")
        doq = [em.sem("doq0"), em.sem("doq1")]
        dwt = em.sem("dwt")
        dxr = [em.sem("dxr0"), em.sem("dxr1")]
        dy = em.sem("dy")
        dmi = em.sem("dmi")

        V, A, P, TN, SY = "vector", "scalar", "gpsimd", "tensor", "sync"

        f32c = lambda ar, c0, n: ar[:, c0:c0 + 2 * n].bitcast(F32)
        i32c = lambda ar, c0, n: ar[:, c0:c0 + 2 * n].bitcast(I32)
        bf16c = lambda ar, c0, n: ar[:, c0:c0 + n].bitcast(BF16)

        xn_f = arA
        sq_f32 = f32c(arA, 10240, 2048)
        pt_bf = [bf16c(arA, 14336 + i * 512, 512) for i in range(4)]
        m_gin = [arA[:, 0:512], arA[:, 512:1024]]
        m_tf = [f32c(arA, 1024, 512), f32c(arA, 2048, 512)]
        m_i32 = [i32c(arA, 3072, 512), i32c(arA, 4096, 512)]
        m_gf = [f32c(arA, 5120, 512), f32c(arA, 6144, 512)]
        m_th = [f32c(arA, 7168, 512), f32c(arA, 8192, 512)]
        m_out = [arA[:, 9216:9728], arA[:, 9728:10240]]
        m_uf = [f32c(arA, 10240, 512), f32c(arA, 11264, 512)]
        m_ui32 = [i32c(arA, 12288, 512), i32c(arA, 13312, 512)]
        m_uin = [arA[:, 14336:14848], arA[:, 14848:15360]]

        wst = [f32c(arB, 0, 1024), f32c(arB, 2048, 1024)]
        wh = [arB[:, 4096:5120], arB[:, 5120:6144]]
        vst = [arB[:, 6144 + i * 512:6656 + i * 512].bitcast(BF16)
               for i in range(4)]
        xq_f = arB
        O_f = arB

        Q_f = arQ
        wdTh = arQ[:, 0:12288]
        wouth = arQ[:, 12288:16384]

        K_f = arK
        oqt = [arK[:, 0:3072], arK[:, 3072:6144]]
        xrr = [f32c(arK, 6144, 512), f32c(arK, 7168, 512)]
        yt = [f32c(arK, 8192, 512), f32c(arK, 9216, 512)]

        v_tok = arV

        lw_f32 = f32c(arW, 0, 1024)
        wgT = [arW[:, 2048:3072], arW[:, 3072:4096]]
        lw_tf = f32c(arW, 4096, 1024)
        lw_i32 = i32c(arW, 4096, 1024)
        lw_int16 = arW[:, 6144:7168]
        lw_q = [arW[:, 7168:8192], arW[:, 8192:9216]]
        lw_s = f32c(arW, 9216, 64)
        wo_st = f32c(arW, 0, 512)
        xqf32_t = f32c(arW, 9344, 512)
        xqi32_t = i32c(arW, 10368, 512)
        rbb_t = xqf32_t
        rb = xqf32_t[0:1, :]
        brow = lw_f32[0:24, 0:128]
        obf = lw_tf[0:1, :]
        xa_t = f32c(arQ, 0, 2048)
        xsq_t = f32c(arQ, 4096, 1024)
        xnt_t = arQ[:, 6144:8192]

        em.emit(V, lambda e: e.memset(cst[:, 0:1], 1e-5))
        em.emit(V, lambda e: e.memset(ones32[:, :], 1.0))
        em.emit(V, lambda e: e.memset(ones16[:, :], 1.0))
        em.emit(V, lambda e: e.memset(st[:, :], 0.0))
        em.emit(V, lambda e: e.memset(mcol[:, :], 0.0))
        em.emit(V, lambda e: e.memset(qn2[:, :], 0.0))
        em.emit(V, lambda e: e.memset(kn2[:, :], 0.0))
        em.emit(V, lambda e: e.memset(sel2[:, :], 0.0))
        em.emit(V, lambda e: e.memset(sel2[0:64, 0:1], 1.0))
        em.emit(V, lambda e: e.memset(sel2[64:128, 1:2], 1.0))

        def vtok_ones(e):
            ap = v_tok[:, :].bitcast(BF16).rearrange("p (n c) -> p n c", c=65)
            return e.memset(ap[:, :, 64:65], 1.0)
        em.emit(V, vtok_ones)
        em.emit(V, lambda e: e.memset(selE[:, :], 1.0))
        em.emit(V, lambda e: e.memset(selO[:, :], 1.0))
        SELV = sv.n
        em.emit(P, lambda e: e.affine_select(
            selE[:, :], selE[:, :], [[1, 128]], ALU.is_ge, 0.0,
            base=0, channel_multiplier=-128), waits=[(sv, SELV)])
        em.emit(P, lambda e: e.affine_select(
            selO[:, :], selO[:, :], [[1, 128]], ALU.is_ge, 0.0,
            base=-128, channel_multiplier=128))
        INIT_V = sv.n
        INIT_P = spl.n
        em.dma(A, id128[:, :], id128_d, dmi)
        em.dma(A, brow[:, :], bqkv_d.rearrange("a (c p) -> (a c) p", p=128), dmi)
        em.dma(A, obf[:, :], bout_d, dmi)
        INIT_D = dmi.n
        em.emit(TN, lambda e: e.transpose(pb[7][:, 0:24], brow[:, :],
                                          id128[0:24, 0:24]),
                waits=[(dmi, INIT_D), (sv, INIT_V)], inc=stn)
        BT = stn.n
        em.emit(A, lambda e: e.activation(bcol[:, :], pb[7][:, 0:24],
                                          AF.Identity), waits=[(stn, BT)])
        em.emit(A, lambda e: e.activation(bq8[:, :], pb[7][:, 0:24],
                                          AF.Identity, scale=0.125))
        BC_A = sa.n
        em.emit(A, lambda e: e.activation(ob16[:, :], obf[:, :], AF.Identity),
                waits=[(dmi, INIT_D)])
        OB_A = sa.n

        xn_done_v = []
        xn_done_a = []
        for it in range(NT):
            bi = it % 2
            xt = xa_t[:, bi * 1024:(bi + 1) * 1024]
            xnt = xnt_t[:, bi * 1024:(bi + 1) * 1024]
            w = []
            if it >= 2:
                w.append((sa, xn_done_a[it - 2]))
            em.dma(A, xt, x_d[it * 128:(it + 1) * 128, :], dxa[bi], waits=w)
            DX = dxa[bi].n
            wv = [(dxa[bi], DX)]
            em.emit(A, lambda e, xt=xt: e.activation(xsq_t[:, :], xt, AF.Square),
                    waits=wv + ([(sv, xn_done_v[it - 1])] if it >= 1 else []))
            SQ = sa.n
            em.emit(V, lambda e, xt=xt: e.tensor_reduce(
                stc("m"), xt, AX.X, ALU.add), waits=[(dxa[bi], DX)])
            em.emit(V, lambda e: e.tensor_scalar_mul(stc("m"), stc("m"), 1.0 / E))
            em.emit(V, lambda e: e.tensor_reduce(
                stc("var"), xsq_t[:, :], AX.X, ALU.add), waits=[(sa, SQ)])
            em.emit(V, lambda e: e.tensor_scalar_mul(stc("var"), stc("var"),
                                                     1.0 / E))
            em.emit(V, lambda e: e.scalar_tensor_tensor(
                stc("t0"), stc("m"), stc("m"), stc("var"),
                ALU.mult, ALU.subtract))
            VW = sv.n
            em.emit(A, lambda e: e.activation(
                stc("t1"), stc("t0"), AF.Sqrt, bias=1e-5, scale=-1.0),
                waits=[(sv, VW)])
            SD = sa.n
            em.emit(V, lambda e: e.reciprocal(stc("rstd"), stc("t1")),
                    waits=[(sa, SD)])
            em.emit(V, lambda e, xt=xt: e.tensor_scalar(
                xsq_t[:, :], xt, stc("m"), None, ALU.subtract))
            em.emit(V, lambda e: e.tensor_scalar(
                xsq_t[:, :], xsq_t[:, :], stc("rstd"), None, ALU.mult))
            XN = sv.n
            em.emit(V, lambda e: e.tensor_reduce(
                stc("t1"), xsq_t[:, :], AX.X, ALU.max,
                apply_absolute_value=True))
            em.emit(V, lambda e: e.tensor_tensor(
                mc("xn"), mc("xn"), stc("t1"), ALU.max))
            xn_done_v.append(sv.n)
            em.emit(A, lambda e, xnt=xnt: e.activation(
                xnt, xsq_t[:, :], AF.Identity),
                waits=[(sv, XN)] + ([(dxn, 16 * (it - 1))] if it >= 2 else []))
            xn_done_a.append(sa.n)
            em.dma(SY, xn_h[it * 128:(it + 1) * 128, :], xnt, dxn,
                   waits=[(sa, xn_done_a[-1])])
            for c in range(NE):
                em.dma(SY, xn_f[:, c * T + it * 128:c * T + (it + 1) * 128],
                       xn_h[it * 128:(it + 1) * 128, c * 128:(c + 1) * 128],
                       dxf, waits=[(dxn, dxn.n)], transpose=True)
        LN_V = sv.n
        XNF_D = dxf.n

        def fold_scalar(src_col, dst, waits):
            em.emit(TN, lambda e: e.transpose(
                pb[7][0:1, 0:128], src_col, id128[:, :]),
                waits=list(waits) + [(dmi, INIT_D), (sv, sv.n), (sa, sa.n)],
                inc=stn)
            TW = stn.n
            em.emit(V, lambda e, dst=dst: e.tensor_reduce(
                stc(dst, 0, 1), pb[7][0:1, 0:128], AX.X, ALU.max),
                waits=[(stn, TW)])
            return sv.n

        MX_V = fold_scalar(mc("xn"), "t0", [(sv, LN_V)])

        def collective(ci, waits):
            em.dma(P, cc_i[ci], st[0:1, COL["t0"]:COL["t0"] + 2], dcc,
                   waits=waits)
            em.emit(P, lambda e, ci=ci: e.collective_compute(
                "AllReduce", ALU.max, [[0, 1, 2, 3, 4, 5, 6, 7]],
                [cc_i[ci].opt()], [cc_o[ci].opt()]),
                waits=[(dcc, dcc.n)], inc=scc, no_chain=True)
            return scc.n

        CC1 = collective(0, [(sv, MX_V)])
        em.dma(P, st[0:1, COL["t0"]:COL["t0"] + 2], cc_o[0], dcc,
               waits=[(scc, CC1)])
        CC1D = dcc.n

        def bcast_scale(srccol, scol, icol, waits, extra=()):
            em.emit(TN, lambda e: e.matmul(
                pb[7][:, 0:1], ones32[0:1, 0:128],
                st[0:1, COL[srccol]:COL[srccol] + 1], start=True, stop=True),
                waits=list(waits) + [(sv, sv.n), (sa, sa.n)], inc=stn)
            TW = stn.n
            em.emit(V, lambda e: e.tensor_scalar(
                stc(scol), pb[7][:, 0:1], 1.0 / QMAX, 1e-12, ALU.mult, ALU.max),
                waits=[(stn, TW)])
            em.emit(V, lambda e: e.reciprocal(stc(icol), stc(scol)))
            for f in extra:
                f()
            return sv.n

        S1_V = bcast_scale("t0", "s1", "i1", [(dcc, CC1D)])

        lqst = {"n": 0, "a": 0, "p": 0, "wr": 0, "q": [0, 0]}

        def lpbq_tile(src_ap, dst_ap):
            k = lqst["n"]
            lqst["n"] += 1
            w = [(dmi, INIT_D)]
            if k == 0:
                w += [(stn, BT)]
            if k >= 1:
                w += [(spl, lqst["p"])]
            em.dma(A, lw_f32, src_ap, dlq[k % 2], waits=w)
            RD = dlq[k % 2].n
            wv3 = lw_f32.rearrange("p (nb e) -> p nb e", e=32)
            war = [(spl, lqst["p"])] if k >= 1 else []
            em.emit(V, lambda e, wv3=wv3: e.tensor_reduce(
                lw_s[:, 0:32], wv3, AX.X, ALU.max, apply_absolute_value=True),
                waits=[(dlq[k % 2], RD)] + war)
            em.emit(V, lambda e: e.tensor_scalar(
                lw_s[:, 32:64], lw_s[:, 0:32], 1.0 / 7.0, 1e-12,
                ALU.mult, ALU.max))
            em.emit(V, lambda e: e.reciprocal(lw_s[:, 0:32], lw_s[:, 32:64]))
            SV_ = sv.n
            inv_b = lw_s[:, 0:32].unsqueeze(2).broadcast_to([128, 32, 32])
            s_b = lw_s[:, 32:64].unsqueeze(2).broadcast_to([128, 32, 32])
            tfv = lw_tf.rearrange("p (nb e) -> p nb e", e=32)
            war_t = [(sa, lqst["a"])] if k >= 1 else []
            war_t2 = [(sa, OB_A)] if k == 0 else []
            em.emit(P, lambda e, wv3=wv3, inv_b=inv_b, tfv=tfv: e.tensor_tensor(
                tfv, wv3, inv_b, ALU.mult), waits=[(sv, SV_)] + war_t + war_t2)
            TM = spl.n
            lqst["p"] = TM
            em.emit(A, lambda e: e.activation(lw_i32[:, :], lw_tf, AF.Identity),
                    waits=[(spl, TM)])
            em.emit(A, lambda e: e.activation(lw_int16[:, :], lw_i32[:, :],
                                              AF.Identity))
            IA = sa.n
            lqst["a"] = IA
            qs = lw_q[k % 2]
            war_q = [(dqw, lqst["q"][k % 2])]
            em.emit(P, lambda e, qs=qs, s_b=s_b: e.tensor_tensor(
                qs.rearrange("p (nb e) -> p nb e", e=32),
                lw_int16.rearrange("p (nb e) -> p nb e", e=32),
                s_b, ALU.mult), waits=[(sa, IA)] + war_q)
            QP = spl.n
            em.dma(SY, dst_ap, qs, dqw, waits=[(spl, QP)])
            lqst["wr"] = dqw.n
            lqst["q"][k % 2] = dqw.n

        def lpbq_jobs(w_src, w_dst, nrows, ncols):
            out = []
            for rc in range(nrows // 128):
                for sc in range(ncols // 1024):
                    out.append((w_src[rc * 128:(rc + 1) * 128,
                                      sc * 1024:(sc + 1) * 1024],
                                w_dst[rc * 128:(rc + 1) * 128,
                                      sc * 1024:(sc + 1) * 1024]))
            return out

        side_jobs = (lpbq_jobs(wg_d, wqg_h, I3, E) + lpbq_jobs(wu_d, wqu_h, I3, E)
                     + lpbq_jobs(wd_d, wqd_h, E, I3))
        side_it = iter(side_jobs)

        def pop_side(n=1):
            for _ in range(n):
                job = next(side_it, None)
                if job is not None:
                    lpbq_tile(*job)

        wqkv_r = wqkv_d.rearrange("(c p) f -> p c f", p=128)
        qkv_conv_a = [0, 0]
        whmm = [0, 0]
        evq = [0, 0]
        vwd = [0, 0, 0, 0]
        vst_a = [0, 0, 0, 0]

        for m in range(24):
            b = m % 2
            j, r3 = divmod(m, 3)
            w = [(sa, qkv_conv_a[b])] if m >= 2 else []
            for k in range(NE):
                em.dma(A, wst[b][:, k * 128:(k + 1) * 128],
                       wqkv_r[:, k, m * 128:(m + 1) * 128], dwl[b],
                       waits=(w if k == 0 else []))
            DW = dwl[b].n
            wconv = [(dwl[b], DW)]
            if m >= 2:
                wconv.append((stn, whmm[b]))
            em.emit(A, lambda e, b=b: e.activation(wh[b][:, :], wst[b][:, :],
                                                   AF.Identity), waits=wconv)
            WH = sa.n
            qkv_conv_a[b] = WH
            whv = wh[b].rearrange("p (c f) -> p c f", c=NE)
            for t in range(NQ):
                bk = (m * NQ + t) % 2
                wl = [(sa, WH), (dxf, XNF_D)]
                if m * NQ + t >= 2:
                    wl.append((sa, evq[bk]))
                for k in range(NE):
                    em.emit(TN, lambda e, bk=bk, k=k, t=t, whv=whv: e.matmul(
                        pb[bk][:, :], whv[:, k, :],
                        xn_f[:, k * T + t * 512:k * T + (t + 1) * 512],
                        start=(k == 0), stop=(k == NE - 1)),
                        waits=(wl if k == 0 else ()),
                        inc=(stn if k == NE - 1 else None))
                MM = stn.n
                if r3 == 0:
                    em.emit(A, lambda e, bk=bk, j=j, t=t, m=m: e.activation(
                        Q_f[0:64, j * T + t * 512:j * T + (t + 1) * 512],
                        pb[bk][0:64, :], AF.Identity,
                        bias=bq8[0:64, m:m + 1], scale=0.125),
                        waits=[(stn, MM), (sa, BC_A)])
                    em.emit(A, lambda e, bk=bk, j=j, t=t, m=m: e.activation(
                        K_f[0:64, j * T + t * 512:j * T + (t + 1) * 512],
                        pb[bk][64:128, :], AF.Identity,
                        bias=bcol[64:128, m:m + 1]))
                elif r3 == 1:
                    em.emit(A, lambda e, bk=bk, t=t, m=m: e.activation(
                        vst[t][0:64, :], pb[bk][0:64, :], AF.Identity,
                        bias=bcol[0:64, m:m + 1]),
                        waits=[(stn, MM), (sa, BC_A), (dvw, vwd[t])])
                    em.emit(A, lambda e, bk=bk, j=j, t=t, m=m: e.activation(
                        Q_f[64:128, j * T + t * 512:j * T + (t + 1) * 512],
                        pb[bk][64:128, :], AF.Identity,
                        bias=bq8[64:128, m:m + 1], scale=0.125))
                else:
                    em.emit(A, lambda e, bk=bk, j=j, t=t, m=m: e.activation(
                        K_f[64:128, j * T + t * 512:j * T + (t + 1) * 512],
                        pb[bk][0:64, :], AF.Identity,
                        bias=bcol[0:64, m:m + 1]),
                        waits=[(stn, MM), (sa, BC_A)])
                    em.emit(A, lambda e, bk=bk, t=t, m=m: e.activation(
                        vst[t][64:128, :], pb[bk][64:128, :], AF.Identity,
                        bias=bcol[64:128, m:m + 1]), waits=[(dvw, vwd[t])])
                    vst_a[t] = sa.n
                    em.dma(SY, v_h[j * 128:(j + 1) * 128,
                                   t * 512:(t + 1) * 512], vst[t], dvw,
                           waits=[(sa, vst_a[t])])
                    vwd[t] = dvw.n
                evq[bk] = sa.n
            whmm[b] = stn.n
            if m % 2 == 1:
                pop_side(2)
        QKV_MM = stn.n
        QKV_A = sa.n
        VW_D = dvw.n
        if dbg:
            em.dma(SY, xnf_dbg, arA[:, :], dmi, waits=[(stn, QKV_MM)])

        vcp = [0, 0]
        for idx in range(NT * NE):
            kt, j = divmod(idx, NE)
            r = idx % 2
            land = vland[:, r * 128:(r + 1) * 128].bitcast(BF16)
            w = [(dvw, VW_D)]
            if idx >= 2:
                w.append((spl, vcp[r]))
            em.dma(A, land, v_h[j * 128:(j + 1) * 128,
                                kt * 128:(kt + 1) * 128],
                   dvr, waits=w, transpose=True)
            DVT = dvr.n
            s0 = (kt * H + 2 * j) * 65
            s1 = (kt * H + 2 * j + 1) * 65
            em.emit(P, lambda e, land=land, s0=s0: e.tensor_copy(
                v_tok[:, s0:s0 + 64].bitcast(BF16), land[:, 0:64]),
                waits=[(dvr, DVT), (sv, INIT_V)])
            em.emit(P, lambda e, land=land, s1=s1: e.tensor_copy(
                v_tok[:, s1:s1 + 64].bitcast(BF16), land[:, 64:128]))
            vcp[r] = spl.n
        VT_P = spl.n
        VT_D = dvr.n

        xq_a = [0]
        first_xq = True
        for c in range(NE):
            for t in range(NQ):
                wv = [(sv, S1_V)]
                if first_xq:
                    wv += [(stn, QKV_MM), (sa, QKV_A), (dvw, VW_D)]
                    first_xq = False
                if xq_a[0]:
                    wv.append((sa, xq_a[0]))
                em.emit(V, lambda e, c=c, t=t: e.tensor_scalar(
                    xqf32_t, xn_f[:, c * T + t * 512:c * T + (t + 1) * 512],
                    stc("i1"), -32768.0, ALU.mult, ALU.max), waits=wv)
                em.emit(V, lambda e: e.tensor_scalar(
                    xqi32_t, xqf32_t, 32767.0, None, ALU.min))
                I32V = sv.n
                em.emit(A, lambda e, c=c, t=t: e.activation(
                    xq_f[:, c * T + t * 512:c * T + (t + 1) * 512],
                    xqi32_t, AF.Identity), waits=[(sv, I32V)])
                xq_a[0] = sa.n
        XQ_A = sa.n
        XQ_V = sv.n

        spill_cnt = {"g": 0, "u": 0}
        gumax_v = {"g": 0, "u": 0}
        gu_ev = {"ev0": 0, "ev1": 0}
        gumm = [0, 0]
        gusp = [0, 0]

        def gu_load(wq_h, i, wq_wr_need):
            b = i % 2
            w = [(dqw, wq_wr_need)]
            if i >= 2:
                w.append((stn, gumm[b]))
            for k in range(NE):
                em.dma(SY, wgT[b][:, k * 128:(k + 1) * 128],
                       wq_h[i * 128:(i + 1) * 128, k * 128:(k + 1) * 128],
                       dgt[b], waits=(w if k == 0 else []), transpose=True)
            return dgt[b].n

        def gu_stream(wq_h, sp_h, mx, wq_wr_need):
            gl = [0, 0]
            gl[0] = gu_load(wq_h, 0, wq_wr_need)
            for i in range(NI):
                b = i % 2
                if i + 1 < NI:
                    gl[(i + 1) % 2] = gu_load(wq_h, i + 1, wq_wr_need)
                DG = gl[b]
                for t in range(NQ):
                    bk = (i * NQ + t) % 2
                    wl = [(dgt[b], DG), (sa, XQ_A)]
                    if gu_ev[f"ev{bk}"]:
                        wl.append((sa, gu_ev[f"ev{bk}"]))
                    for k in range(NE):
                        em.emit(TN, lambda e, bk=bk, b=b, k=k, t=t: e.matmul(
                            pb[bk][:, :], wgT[b][:, k * 128:(k + 1) * 128],
                            xq_f[:, k * T + t * 512:k * T + (t + 1) * 512],
                            start=(k == 0), stop=(k == NE - 1)),
                            waits=(wl if k == 0 else ()),
                            inc=(stn if k == NE - 1 else None))
                    MM = stn.n
                    sl = m_out[bk]
                    em.emit(A, lambda e, sl=sl, bk=bk: e.activation(
                        sl, pb[bk][:, :], AF.Identity, scale=stc("s1")),
                        waits=[(stn, MM), (dsp, gusp[bk])])
                    EVA = sa.n
                    gu_ev[f"ev{bk}"] = EVA
                    em.emit(V, lambda e, sl=sl: e.tensor_reduce(
                        stc("t1"), sl, AX.X, ALU.max,
                        apply_absolute_value=True), waits=[(sa, EVA)])
                    em.emit(V, lambda e, mx=mx: e.tensor_tensor(
                        mc(mx), mc(mx), stc("t1"), ALU.max))
                    gumax_v[mx] = sv.n
                    em.dma(SY, sp_h[i * 128:(i + 1) * 128,
                                    t * 512:(t + 1) * 512], sl, dsp,
                           waits=[(sv, gumax_v[mx])])
                    gusp[bk] = dsp.n
                    spill_cnt[mx] += 1
                gumm[b] = stn.n
                if i % 2 == 1:
                    pop_side(1)

        pop_side(26)
        gu_stream(wqg_h, spg_h, "g", 16 * 24)
        pop_side(12)
        gu_stream(wqu_h, spu_h, "u", 16 * 48)
        pop_side(len(side_jobs))
        GU_MM = stn.n
        GU_SPILLS = 16 * (spill_cnt["g"] + spill_cnt["u"])
        LPBQ_WR = lqst["wr"]
        assert lqst["n"] == len(side_jobs)

        G_V = fold_scalar(mc("g"), "t0", [(sv, gumax_v["g"])])
        U_V = fold_scalar(mc("u"), "t1", [(sv, gumax_v["u"])])
        CC2 = collective(1, [(sv, U_V)])
        em.dma(P, st[0:1, COL["t0"]:COL["t0"] + 2], cc_o[1], dcc,
               waits=[(scc, CC2)])
        CC2D = dcc.n

        def s3_extra():
            em.emit(V, lambda e: e.tensor_scalar_mul(stc("hs3"), stc("s3"), 0.5))
            em.emit(V, lambda e: e.tensor_scalar_mul(stc("q3"), stc("s3"),
                                                     1.0 / 65536.0))
        S3_V = bcast_scale("t0", "s3", "i3", [(dcc, CC2D)], extra=(s3_extra,))
        S2_V = bcast_scale("t1", "s2", "i2", [(dcc, CC2D)])

        def norms(src, acc):
            for hc in range(NE):
                em.emit(A, lambda e, hc=hc, src=src: e.activation(
                    sq_f32[:, :], src[:, hc * T:(hc + 1) * T], AF.Square),
                    waits=[(stn, stn.n), (sv, XQ_V)])
                SQA = sa.n
                for t in range(NQ):
                    em.emit(TN, lambda e, t=t: e.matmul(
                        pb[6][0:2, :], sel2[:, :],
                        sq_f32[:, t * 512:(t + 1) * 512],
                        start=True, stop=True),
                        waits=[(sa, SQA), (sv, sv.n)], inc=stn)
                    TW = stn.n
                    em.emit(V, lambda e: e.tensor_reduce(
                        stc("t1", 0, 2), pb[6][0:2, :], AX.X, ALU.max),
                        waits=[(stn, TW)])
                    em.emit(V, lambda e, hc=hc, acc=acc: e.tensor_tensor(
                        acc[0:2, hc:hc + 1], acc[0:2, hc:hc + 1],
                        stc("t1", 0, 2), ALU.max))
            return sv.n

        QN_V = norms(Q_f, qn2)
        KN_V = norms(K_f, kn2)
        em.emit(V, lambda e: e.tensor_tensor(nbsrc[0:2, :], qn2[0:2, :],
                                             kn2[0:2, :], ALU.mult),
                waits=[(sv, KN_V)])
        NB1 = sv.n
        em.emit(A, lambda e: e.activation(nbsrc[0:2, :], nbsrc[0:2, :],
                                          AF.Sqrt), waits=[(sv, NB1)])
        NB2 = sa.n
        em.emit(V, lambda e: e.tensor_scalar(
            nbsrc[0:2, :], nbsrc[0:2, :], -1.0, EXPOFF, ALU.mult, ALU.add),
            waits=[(sa, NB2)])
        NB3 = sv.n
        em.emit(TN, lambda e: e.matmul(pb[6][:, 0:8], selE[:, :],
                                       nbsrc[0:2, :], start=True, stop=True),
                waits=[(sv, NB3), (spl, INIT_P), (sv, sv.n)], inc=stn)
        NBT1 = stn.n
        em.emit(A, lambda e: e.activation(nbE[:, :], pb[6][:, 0:8],
                                          AF.Identity), waits=[(stn, NBT1)])
        NBA1 = sa.n
        em.emit(TN, lambda e: e.matmul(pb[6][:, 0:8], selO[:, :],
                                       nbsrc[0:2, :], start=True, stop=True),
                waits=[(sa, NBA1)], inc=stn)
        NBT2 = stn.n
        em.emit(A, lambda e: e.activation(nbO[:, :], pb[6][:, 0:8],
                                          AF.Identity), waits=[(stn, NBT2)])
        NEGB_A = sa.n

        m1 = {"n": 0, "ec": [0, 0], "sp": [0, 0], "wr": 0, "th": [0, 0],
              "pend": [None, None]}

        def m1_pre(jn):
            i, t = divmod(jn, NQ)
            r = jn % 2
            w = [(dsp, GU_SPILLS)]
            if m1["ec"][r]:
                w.append((sv, m1["ec"][r]))
            if m1["sp"][r]:
                w.append((dsp, m1["sp"][r]))
            em.dma(SY, m_gin[r], spg_h[i * 128:(i + 1) * 128,
                                       t * 512:(t + 1) * 512], dgr[r], waits=w)
            RD = dgr[r].n
            em.emit(V, lambda e, r=r: e.tensor_scalar(
                m_tf[r], m_gin[r], stc("i3"), -32768.0, ALU.mult, ALU.max),
                waits=[(dgr[r], RD), (sv, S3_V)])
            em.emit(V, lambda e, r=r: e.tensor_scalar(
                m_i32[r], m_tf[r], 32767.0, None, ALU.min))
            em.emit(V, lambda e, r=r: e.tensor_copy(m_gf[r], m_i32[r]))
            GF = sv.n
            em.emit(A, lambda e, r=r: e.activation(m_th[r], m_gf[r], AF.Tanh,
                                                   scale=stc("hs3")),
                    waits=[(sv, GF)])
            m1["th"][r] = sa.n

        def m1_post(jn):
            i, t = divmod(jn, NQ)
            r = jn % 2
            em.emit(V, lambda e, r=r: e.tensor_scalar(
                m_tf[r], m_th[r], 32768.0, 32768.0, ALU.mult, ALU.add),
                waits=[(sa, m1["th"][r])])
            em.emit(V, lambda e, r=r: e.tensor_scalar(
                m_i32[r], m_tf[r], 65535.0, None, ALU.min))
            em.emit(V, lambda e, r=r: e.tensor_copy(m_th[r], m_i32[r]))
            em.emit(V, lambda e, r=r: e.tensor_tensor(
                m_tf[r], m_gf[r], m_th[r], ALU.mult))
            em.emit(V, lambda e, r=r: e.tensor_scalar(
                m_out[r], m_tf[r], stc("q3"), None, ALU.mult))
            em.emit(V, lambda e, r=r: e.tensor_reduce(
                stc("t1"), m_out[r], AX.X, ALU.max, apply_absolute_value=True))
            em.emit(V, lambda e: e.tensor_tensor(
                mc("g2"), mc("g2"), stc("t1"), ALU.max))
            m1["ec"][r] = sv.n
            em.dma(SY, spg2_h[i * 128:(i + 1) * 128, t * 512:(t + 1) * 512],
                   m_out[r], dsp, waits=[(sv, m1["ec"][r])])
            m1["sp"][r] = dsp.n
            m1["wr"] = dsp.n

        def m1_tile():
            jn = m1["n"]
            m1["n"] += 1
            m1_pre(jn)
            if jn >= 1:
                m1_post(jn - 1)
            if jn == NI * NQ - 1:
                m1_post(jn)

        ps_free = {0: 0, 1: 0, 2: 0, 3: 0}
        pt_free = [0, 0, 0, 0]
        o_ev = {4: 0, 5: 0}
        ep_a = [NEGB_A]
        si_ctr = [0]

        def attn_pair(hc):
            for qc in range(NQ):
                nkt = 4 * qc + 4
                for kt in range(nkt):
                    diag = kt >= 4 * qc
                    for par, (sbk0, obk, nb) in enumerate(
                            [(2, 4, nbE), (3, 5, nbO)]):
                        sbk = sbk0 if kt % 2 == 0 else par
                        si = si_ctr[0]
                        si_ctr[0] += 1
                        p0 = par * 64
                        w = [(sa, NEGB_A)]
                        if ps_free[sbk]:
                            w.append((sa, ps_free[sbk]))
                        em.emit(TN, lambda e, sbk=sbk, p0=p0, hc=hc, kt=kt,
                                qc=qc: e.matmul(
                            pb[sbk][:, :],
                            K_f[p0:p0 + 64,
                                hc * T + kt * 128:hc * T + (kt + 1) * 128],
                            Q_f[p0:p0 + 64,
                                hc * T + qc * 512:hc * T + (qc + 1) * 512],
                            start=True, stop=True), waits=w, inc=stn)
                        MMW = stn.n
                        slot = si % 4
                        we = [(stn, MMW)]
                        if pt_free[slot]:
                            we.append((stn, pt_free[slot]))
                        em.emit(A, lambda e, sbk=sbk, slot=slot, nb=nb, hc=hc:
                                e.activation(pt_bf[slot], pb[sbk][:, :],
                                             AF.Exp, bias=nb[:, hc:hc + 1]),
                                waits=we)
                        EXA = sa.n
                        ps_free[sbk] = EXA
                        if diag:
                            s = kt - 4 * qc
                            em.emit(P, lambda e, slot=slot, s=s:
                                    e.affine_select(
                                pt_bf[slot], pt_bf[slot], [[1, 512]],
                                ALU.is_ge, 0.0, base=-128 * s,
                                channel_multiplier=-1), waits=[(sa, EXA)])
                            rdy = (spl, spl.n)
                        else:
                            rdy = (sa, EXA)
                        h = 2 * hc + par
                        wpv = [(spl, VT_P), rdy]
                        if kt == 0 and o_ev[obk]:
                            wpv.append((sv, o_ev[obk]))
                        em.emit(TN, lambda e, obk=obk, kt=kt, h=h, slot=slot,
                                nkt=nkt: e.matmul(
                            pb[obk][0:65, :],
                            v_tok[:, (kt * H + h) * 65:(kt * H + h + 1) * 65]
                            .bitcast(BF16),
                            pt_bf[slot], start=(kt == 0), stop=(kt == nkt - 1)),
                            waits=wpv, inc=stn)
                        pt_free[slot] = stn.n
                for par, obk in ((0, 4), (1, 5)):
                    PVM = stn.n
                    em.emit(V, lambda e, obk=obk: e.tensor_scalar(
                        rb, pb[obk][64:65, :], 1e-12, None, ALU.add),
                        waits=[(stn, PVM)])
                    em.emit(V, lambda e: e.reciprocal(rb, rb))
                    RC = sv.n
                    em.emit(TN, lambda e: e.matmul(
                        pb[6][0:64, :], ones32[0:1, 0:64], rb,
                        start=True, stop=True),
                        waits=[(sv, RC), (sa, ep_a[0])], inc=stn)
                    BCW = stn.n
                    em.emit(A, lambda e: e.activation(
                        rbb_t[0:64, :], pb[6][0:64, :], AF.Identity),
                        waits=[(stn, BCW)])
                    ep_a[0] = sa.n
                    em.emit(V, lambda e, obk=obk, par=par, hc=hc, qc=qc:
                            e.tensor_tensor(
                        O_f[par * 64:par * 64 + 64,
                            hc * T + qc * 512:hc * T + (qc + 1) * 512],
                        pb[obk][0:64, :], rbb_t[0:64, :], ALU.mult),
                        waits=[(sa, ep_a[0])])
                    o_ev[obk] = sv.n

        for hc in range(NE):
            attn_pair(hc)
            if hc >= 2:
                for _ in range(16):
                    if m1["n"] < NI * NQ:
                        m1_tile()
        while m1["n"] < NI * NQ:
            m1_tile()
        ATT_MM = stn.n
        O_V = sv.n
        M1_WR = m1["wr"]

        G2_V = fold_scalar(mc("g2"), "t0",
                           [(sv, m1["ec"][0]), (sv, m1["ec"][1])])
        CC3 = collective(2, [(sv, G2_V)])
        em.dma(P, st[0:1, COL["t0"]:COL["t0"] + 2], cc_o[2], dcc,
               waits=[(scc, CC3)])
        CC3D = dcc.n

        def s4_extra():
            em.emit(V, lambda e: e.tensor_tensor(
                stc("s42"), stc("s4"), stc("s2"), ALU.mult))
        S4_V = bcast_scale("t0", "s4", "i4", [(dcc, CC3D), (sv, S2_V)],
                           extra=(s4_extra,))

        m2 = {"ec": [0, 0], "sp": [0, 0]}
        m2_pend = [None, None]
        for jn in range(NI * NQ):
            i, t = divmod(jn, NQ)
            r = jn % 2
            w = [(dsp, M1_WR)]
            if m2["ec"][r]:
                w.append((sv, m2["ec"][r]))
            if m2["sp"][r]:
                w.append((dsp, m2["sp"][r]))
            em.dma(SY, m_gin[r], spg2_h[i * 128:(i + 1) * 128,
                                        t * 512:(t + 1) * 512], dgr[r],
                   waits=w)
            RDg = dgr[r].n
            em.dma(SY, m_uin[r], spu_h[i * 128:(i + 1) * 128,
                                       t * 512:(t + 1) * 512], dur[r],
                   waits=([(sv, O_V)] if jn < 2 else []))
            RDu = dur[r].n
            if m2_pend[r] is not None:
                m2_pend[r]()
                m2_pend[r] = None
            em.emit(V, lambda e, r=r: e.tensor_scalar(
                m_tf[r], m_gin[r], stc("i4"), -32768.0, ALU.mult, ALU.max),
                waits=[(dgr[r], RDg), (sv, S4_V)])
            em.emit(V, lambda e, r=r: e.tensor_scalar(
                m_i32[r], m_tf[r], 32767.0, None, ALU.min))
            em.emit(V, lambda e, r=r: e.tensor_copy(m_gf[r], m_i32[r]))
            em.emit(V, lambda e, r=r: e.tensor_scalar(
                m_tf[r], m_uin[r], stc("i2"), -32768.0, ALU.mult, ALU.max),
                waits=[(dur[r], RDu)])
            em.emit(V, lambda e, r=r: e.tensor_scalar(
                m_ui32[r], m_tf[r], 32767.0, None, ALU.min))
            em.emit(V, lambda e, r=r: e.tensor_copy(m_uf[r], m_ui32[r]))
            em.emit(V, lambda e, r=r: e.tensor_tensor(
                m_tf[r], m_gf[r], m_uf[r], ALU.mult))
            em.emit(V, lambda e, r=r: e.tensor_scalar(
                m_out[r], m_tf[r], stc("s42"), None, ALU.mult))
            em.emit(V, lambda e, r=r: e.tensor_reduce(
                stc("t1"), m_out[r], AX.X, ALU.max, apply_absolute_value=True))
            em.emit(V, lambda e: e.tensor_tensor(
                mc("o"), mc("o"), stc("t1"), ALU.max))
            m2["ec"][r] = sv.n
            EC = m2["ec"][r]

            def spill(r=r, i=i, t=t, EC=EC):
                em.dma(SY, spo_h[i * 128:(i + 1) * 128,
                                 t * 512:(t + 1) * 512],
                       m_out[r], dsp, waits=[(sv, EC)])
                m2["sp"][r] = dsp.n
            m2_pend[r] = spill
        for r in range(2):
            if m2_pend[r] is not None:
                m2_pend[r]()
                m2_pend[r] = None
        M2_WR = dsp.n

        O5_V = fold_scalar(mc("o"), "t0",
                           [(sv, m2["ec"][0]), (sv, m2["ec"][1])])
        CC4 = collective(3, [(sv, O5_V)])
        em.dma(P, st[0:1, COL["t0"]:COL["t0"] + 2], cc_o[3], dcc,
               waits=[(scc, CC4)])
        CC4D = dcc.n
        S5_V = bcast_scale("t0", "s5", "i5", [(dcc, CC4D)])

        m3 = {"ec": [0, 0], "sp": [0, 0]}
        m3_pend = [None, None]
        for jn in range(NI * NQ):
            i, t = divmod(jn, NQ)
            r = jn % 2
            w = [(dsp, M2_WR)]
            if m3["ec"][r]:
                w.append((sv, m3["ec"][r]))
            if m3["sp"][r]:
                w.append((dsp, m3["sp"][r]))
            em.dma(SY, m_gin[r], spo_h[i * 128:(i + 1) * 128,
                                       t * 512:(t + 1) * 512], dgr[r], waits=w)
            RD = dgr[r].n
            if m3_pend[r] is not None:
                m3_pend[r]()
                m3_pend[r] = None
            em.emit(V, lambda e, r=r: e.tensor_scalar(
                m_tf[r], m_gin[r], stc("i5"), -32768.0, ALU.mult, ALU.max),
                waits=[(dgr[r], RD), (sv, S5_V)])
            em.emit(V, lambda e, r=r: e.tensor_scalar(
                m_i32[r], m_tf[r], 32767.0, None, ALU.min))
            em.emit(V, lambda e, r=r: e.tensor_copy(m_gf[r], m_i32[r]))
            em.emit(V, lambda e, r=r: e.tensor_scalar(
                m_out[r], m_gf[r], stc("s5"), None, ALU.mult))
            m3["ec"][r] = sv.n
            EC = m3["ec"][r]

            def spill3(r=r, i=i, t=t, EC=EC):
                em.dma(SY, spq_h[i * 128:(i + 1) * 128,
                                 t * 512:(t + 1) * 512],
                       m_out[r], dsp, waits=[(sv, EC)])
                m3["sp"][r] = dsp.n
            m3_pend[r] = spill3
        for r in range(2):
            if m3_pend[r] is not None:
                m3_pend[r]()
                m3_pend[r] = None
        M3_WR = dsp.n

        WH_A = [0]
        p8mm = [0, 0]
        p8y = [0, 0]
        ydma = []
        for eh in range(2):
            for i in range(NI):
                wfirst = ([(stn, ATT_MM), (dqw, LPBQ_WR)]
                          if (i == 0 and eh == 0) else
                          ([(sa, WH_A[0]), (stn, stn.n)]
                           if i == 0 and eh == 1 else []))
                em.dma(SY, wdTh[:, i * 512:(i + 1) * 512],
                       wqd_h[eh * 512:(eh + 1) * 512,
                             i * 128:(i + 1) * 128], dwt,
                       waits=wfirst, transpose=True)
            for k in range(NE):
                wld = [(spl, spl.n), (sv, sv.n)] if (k == 0 and eh == 0) \
                    else ([(sa, WH_A[0]), (stn, stn.n)] if k == 0
                          else [(sa, WH_A[0])])
                em.dma(A, wo_st[:, :],
                       wout_d[k * 128:(k + 1) * 128,
                              eh * 512:(eh + 1) * 512], dwl[0], waits=wld)
                WD = dwl[0].n
                wca = [(dwl[0], WD)]
                if eh == 0 and k == 0:
                    wca.append((stn, ATT_MM))
                em.emit(A, lambda e, k=k: e.activation(
                    wouth[:, k * 512:(k + 1) * 512], wo_st[:, :],
                    AF.Identity), waits=wca)
                WH_A[0] = sa.n
            WDT = dwt.n
            WHA = sa.n

            def p8_load(t):
                r = t % 2
                n8 = eh * NT + t
                w = [(dsp, M3_WR)]
                if n8 == 0:
                    w.append((stn, ATT_MM))
                if n8 >= 2:
                    w.append((stn, p8mm[r]))
                for i in range(NI):
                    em.dma(SY, oqt[r][:, i * 128:(i + 1) * 128],
                           spq_h[i * 128:(i + 1) * 128,
                                 t * 128:(t + 1) * 128], doq[r],
                           waits=(w if i == 0 else []))
                wx = [(sv, p8y[r])] if n8 >= 2 else [(stn, ATT_MM)]
                em.dma(A, xrr[r], x_d[t * 128:(t + 1) * 128,
                                      eh * 512:(eh + 1) * 512], dxr[r],
                       waits=wx)
                return doq[r].n, dxr[r].n

            pend_y = [None]
            lded = {0: p8_load(0)}
            for t in range(NT):
                r = t % 2
                n8 = eh * NT + t
                if t + 1 < NT:
                    lded[t + 1] = p8_load(t + 1)
                DOQ, DXR = lded.pop(t)
                bk = r
                wl = [(sv, O_V), (sa, WHA), (dwt, WDT), (doq[r], DOQ),
                      (sa, OB_A)]
                if n8 >= 2:
                    wl.append((sv, p8y[r]))
                for k in range(NE):
                    em.emit(TN, lambda e, bk=bk, k=k, t=t: e.matmul(
                        pb[bk][:, :],
                        O_f[:, k * T + t * 128:k * T + (t + 1) * 128],
                        wouth[:, k * 512:(k + 1) * 512],
                        start=(k == 0), stop=False),
                        waits=(wl if k == 0 else ()))
                for i in range(NI):
                    em.emit(TN, lambda e, bk=bk, i=i, r=r: e.matmul(
                        pb[bk][:, :], oqt[r][:, i * 128:(i + 1) * 128],
                        wdTh[:, i * 512:(i + 1) * 512],
                        start=False, stop=False))
                em.emit(TN, lambda e, bk=bk, eh=eh: e.matmul(
                    pb[bk][:, :], ones16[0:1, 0:128],
                    ob16[0:1, eh * 512:(eh + 1) * 512],
                    start=False, stop=True), inc=stn)
                MM = stn.n
                p8mm[r] = MM
                wy = [(stn, MM), (dxr[r], DXR)]
                if len(ydma) >= 2:
                    wy.append((dy, ydma[-2]))
                em.emit(V, lambda e, bk=bk, r=r: e.tensor_tensor(
                    yt[r], pb[bk][:, :], xrr[r], ALU.add), waits=wy)
                YV = sv.n
                p8y[r] = YV
                em.dma(SY, y_d[t * 128:(t + 1) * 128,
                               eh * 512:(eh + 1) * 512], yt[r], dy,
                       waits=[(sv, YV)])
                ydma.append(dy.n)

        em.emit(P, lambda e: e.engine_nop(), waits=[(dy, dy.n)], no_chain=True)

        n_inst = len(em.steps)
        em.finalize(block)
    ends = []
    for a in nc.m.functions[0].allocations:
        if not hasattr(a, "memorylocations"):
            continue
        for ml in a.memorylocations:
            nm = getattr(ml, "name", "")
            if nm in ("arA", "arB", "arQ", "arK", "arV", "arW", "st", "nbE",
                      "nbO", "bcol", "bq8", "ob16", "id128", "ones32",
                      "ones16", "sel2", "selE", "selO", "qn2", "kn2",
                      "nbsrc", "mcol", "cst", "vland"):
                ends.append(ml.addr + ml.size() // 128)
    assert max(ends) <= 196608, f"SBUF overflow: {max(ends)}"
    if dbg:
        print(f"[build] {n_inst} emitted steps")
    return nc


_NC_CACHE = {}


def kernel(L0_attn_x, L0_attn_qkv_w, L0_attn_qkv_b, L0_attn_out_w,
           L0_attn_out_b, L0_ffn_w_gate, L0_ffn_w_up, L0_ffn_w_down):
    x = np.ascontiguousarray(L0_attn_x, dtype=np.float32)
    wqkv = np.ascontiguousarray(L0_attn_qkv_w, dtype=np.float32).reshape(E, I3)
    bqkv = np.ascontiguousarray(L0_attn_qkv_b, dtype=np.float32).reshape(1, I3)
    wout = np.ascontiguousarray(L0_attn_out_w, dtype=np.float32).reshape(E, E)
    bout = np.ascontiguousarray(L0_attn_out_b, dtype=np.float32).reshape(1, E)
    wg = np.ascontiguousarray(L0_ffn_w_gate, dtype=np.float32)
    wu = np.ascontiguousarray(L0_ffn_w_up, dtype=np.float32)
    wd = np.ascontiguousarray(L0_ffn_w_down, dtype=np.float32)

    if "nc" not in _NC_CACHE:
        _NC_CACHE["nc"] = build()
    nc = _NC_CACHE["nc"]
    in_maps = [{"x": np.ascontiguousarray(x[i]), "wqkv": wqkv, "bqkv": bqkv,
                "wout": wout, "bout": bout, "wg": wg, "wu": wu, "wd": wd}
               for i in range(8)]
    res = run_bass_kernel_spmd(nc, in_maps, core_ids=list(range(8)))
    return np.stack([res.results[i]["y"] for i in range(8)], axis=0)


# version 12
# speedup vs baseline: 1.8207x; 1.1347x over previous
import contextlib
import numpy as np
import concourse.bass as bass
import concourse.mybir as mybir
from concourse.bass_utils import run_bass_kernel_spmd

F32 = mybir.dt.float32
F16 = mybir.dt.float16
BF16 = mybir.dt.bfloat16
I32 = mybir.dt.int32
AF = mybir.ActivationFunctionType
ALU = mybir.AluOpType
AX = mybir.AxisListType

T = 2048
E = 1024
H = 16
D = 64
I3 = 3072
NT = T // 128
NQ = T // 512
NE = E // 128
NI = I3 // 128
QMAX = 32767.0
EXPOFF = float(np.log(1024.0))


class Sem:
    __slots__ = ("h", "n", "name")

    def __init__(self, h, name):
        self.h = h
        self.n = 0
        self.name = name


class Em:

    CHAINED = ("vector", "scalar", "gpsimd")

    def __init__(self, nc, es):
        self.nc = nc
        self.es = es
        self.steps = []
        self.chains = {}

    def sem(self, name):
        return Sem(self.es.enter_context(self.nc.semaphore(name)), name)

    def set_chains(self, v, a, p):
        self.chains = {"vector": v, "scalar": a, "gpsimd": p}

    def emit(self, eng, fn, waits=(), inc=None, k=1, no_chain=False):
        chain_prev = -1
        if eng in self.CHAINED and not no_chain:
            ch = self.chains[eng]
            if inc is None:
                inc = ch
            assert inc is ch
            chain_prev = ch.n
        incs = ()
        if inc is not None:
            inc.n += k
            incs = ((inc, k),)
        self.steps.append((eng, fn, tuple(waits), incs, chain_prev))

    def dma(self, eng, out, in_, sem, waits=(), **kw):
        self.emit(eng, lambda e, o=out, i=in_, kw=kw: e.dma_start(out=o, in_=i, **kw),
                  waits=waits, inc=sem, k=16, no_chain=True)

    def finalize(self, block):
        final = {}
        for _, _, _, incs, _ in self.steps:
            for s, k in incs:
                final[s.name] = final.get(s.name, 0) + k
        for _, _, waits, _, _ in self.steps:
            for s, v in waits:
                assert v <= final.get(s.name, 0), \
                    f"wait {s.name}>={v} but final={final.get(s.name, 0)}"

        def make(name):
            def body(eng):
                last_w = {}
                chain = self.chains.get(name)
                for ename, fn, waits, incs, chain_prev in self.steps:
                    if ename != name:
                        continue
                    for s, v in waits:
                        if v > 0 and last_w.get(s.name, -1) < v:
                            eng.wait_ge(s.h, v)
                            last_w[s.name] = v
                    inst = fn(eng)
                    if chain_prev > 0:
                        inst._wait_ge(chain.h, chain_prev)
                    for s, k in incs:
                        inst.then_inc(s.h, k)
            return body
        block.sync(make("sync"))
        block.scalar(make("scalar"))
        block.vector(make("vector"))
        block.tensor(make("tensor"))
        block.gpsimd(make("gpsimd"))


def build(dbg=False):
    nc = bass.Bass("TRN2", target_bir_lowering=False,
                   dynamic_dma_scratch_size=2048)

    x_d = nc.dram_tensor("x", [T, E], F32, kind="ExternalInput").ap()
    wqkv_d = nc.dram_tensor("wqkv", [E, I3], F32, kind="ExternalInput").ap()
    bqkv_d = nc.dram_tensor("bqkv", [1, I3], F32, kind="ExternalInput").ap()
    wout_d = nc.dram_tensor("wout", [E, E], F32, kind="ExternalInput").ap()
    bout_d = nc.dram_tensor("bout", [1, E], F32, kind="ExternalInput").ap()
    wg_d = nc.dram_tensor("wg", [I3, E], F32, kind="ExternalInput").ap()
    wu_d = nc.dram_tensor("wu", [I3, E], F32, kind="ExternalInput").ap()
    wd_d = nc.dram_tensor("wd", [E, I3], F32, kind="ExternalInput").ap()
    y_d = nc.dram_tensor("y", [T, E], F32, kind="ExternalOutput").ap()

    def dscr(name, shape, dt):
        if dbg:
            return nc.dram_tensor(name, shape, dt, kind="ExternalOutput").ap()
        return nc.dram_tensor(name, shape, dt).ap()

    xn_h = dscr("xn_h", [T, E], F16)
    v_h = dscr("v_h", [E, T], BF16)
    wqg_h = dscr("wqg_h", [I3, E], F16)
    wqu_h = dscr("wqu_h", [I3, E], F16)
    wqd_h = dscr("wqd_h", [E, I3], F16)
    spg_h = dscr("spg_h", [I3, T], F16)
    spu_h = dscr("spu_h", [I3, T], F16)
    spg2_h = dscr("spg2_h", [I3, T], F16)
    spo_h = dscr("spo_h", [I3, T], F16)
    spq_h = dscr("spq_h", [I3, T], F16)
    cc_i = [nc.dram_tensor(f"cci{i}", [1, 2], F32).ap() for i in range(4)]
    cc_o = [nc.dram_tensor(f"cco{i}", [1, 2], F32).ap() for i in range(4)]
    id128_d = nc.inline_tensor(np.eye(128, dtype=np.float32), "id128d").ap()
    xnf_dbg = dscr("xnf_dbg", [128, 16384], F16) if dbg else None

    es = contextlib.ExitStack()
    with es:
        sb = lambda n, s, d: es.enter_context(nc.sbuf_tensor(n, s, d))
        arA = sb("arA", [128, 16384], F16)
        arB = sb("arB", [128, 16384], F16)
        arQ = sb("arQ", [128, 16384], F16)
        arK = sb("arK", [128, 16384], F16)
        arV = sb("arV", [128, 16640], F16)
        arW = sb("arW", [128, 11392], F16)
        st = sb("st", [128, 32], F32)
        nbE = sb("nbE", [128, 8], F32)
        nbO = sb("nbO", [128, 8], F32)
        bcol = sb("bcol", [128, 24], F32)
        bq8 = sb("bq8", [128, 24], F32)
        ob16 = sb("ob16", [1, E], F16)
        id128 = sb("id128", [128, 128], F32)
        ones32 = sb("ones32", [1, 128], F32)
        ones16 = sb("ones16", [1, 128], F16)
        sel2 = sb("sel2", [128, 2], F32)
        selE = sb("selE", [2, 128], F32)
        selO = sb("selO", [2, 128], F32)
        qn2 = sb("qn2", [2, 8], F32)
        kn2 = sb("kn2", [2, 8], F32)
        nbsrc = sb("nbsrc", [2, 8], F32)
        mcol = sb("mcol", [128, 8], F32)
        cst = sb("cst", [128, 2], F32)
        vland = sb("vland", [128, 256], F16)
        nc.const_aps.register(F32, 1e-5, cst[:, 0:1])

        COL = {k: i for i, k in enumerate(
            ["m", "var", "rstd", "t0", "t1", "t2",
             "s1", "i1", "s2", "i2", "s3", "i3", "hs3", "q3",
             "s4", "i4", "s42", "s5", "i5"])}

        def stc(name, p0=0, p1=128):
            c = COL[name]
            return st[p0:p1, c:c + 1]

        MC = {"xn": 0, "g": 1, "u": 2, "g2": 3, "o": 4, "g2b": 5, "ob": 6}

        def mc(name):
            return mcol[:, MC[name]:MC[name] + 1]

        pb = [es.enter_context(nc.psum_tensor(f"pb{i}", [128, 512], F32))
              for i in range(8)]

        block = es.enter_context(nc.Block())
        em = Em(nc, es)
        sv = em.sem("sv")
        sa = em.sem("sa")
        spl = em.sem("spl")
        stn = em.sem("stn")
        em.set_chains(sv, sa, spl)
        dxa = [em.sem("dxa0"), em.sem("dxa1")]
        dxn = em.sem("dxn")
        dxf = em.sem("dxf")
        dwl = [em.sem("dwl0"), em.sem("dwl1")]
        dlq = [em.sem("dlq0"), em.sem("dlq1")]
        dqw = em.sem("dqw")
        dgt = [em.sem("dgt0"), em.sem("dgt1")]
        dvw = em.sem("dvw")
        dvr = em.sem("dvr")
        dsp = em.sem("dsp")
        dgr = [em.sem("dgr0"), em.sem("dgr1")]
        dur = [em.sem("dur0"), em.sem("dur1")]
        dcc = em.sem("dcc")
        scc = em.sem("scc")
        doq = [em.sem("doq0"), em.sem("doq1")]
        dwt = em.sem("dwt")
        dxr = [em.sem("dxr0"), em.sem("dxr1")]
        dy = em.sem("dy")
        dmi = em.sem("dmi")

        V, A, P, TN, SY = "vector", "scalar", "gpsimd", "tensor", "sync"

        f32c = lambda ar, c0, n: ar[:, c0:c0 + 2 * n].bitcast(F32)
        i32c = lambda ar, c0, n: ar[:, c0:c0 + 2 * n].bitcast(I32)
        bf16c = lambda ar, c0, n: ar[:, c0:c0 + n].bitcast(BF16)

        xn_f = arA
        sq_f32 = f32c(arA, 10240, 2048)
        pt_bf = [bf16c(arA, 14336 + i * 512, 512) for i in range(4)]
        m_gin = [arA[:, 0:512], arA[:, 512:1024]]
        m_tf = [f32c(arA, 1024, 512), f32c(arA, 2048, 512)]
        m_i32 = [i32c(arA, 3072, 512), i32c(arA, 4096, 512)]
        m_gf = [f32c(arA, 5120, 512), f32c(arA, 6144, 512)]
        m_th = [f32c(arA, 7168, 512), f32c(arA, 8192, 512)]
        m_out = [arA[:, 9216:9728], arA[:, 9728:10240]]
        m_uf = [f32c(arA, 10240, 512), f32c(arA, 11264, 512)]
        m_ui32 = [i32c(arA, 12288, 512), i32c(arA, 13312, 512)]
        m_uin = [arA[:, 14336:14848], arA[:, 14848:15360]]

        wst = [f32c(arB, 0, 1024), f32c(arB, 2048, 1024)]
        wh = [arB[:, 4096:5120], arB[:, 5120:6144]]
        vst = [arB[:, 6144 + i * 512:6656 + i * 512].bitcast(BF16)
               for i in range(4)]
        xq_f = arB
        O_f = arB

        Q_f = arQ
        wdTh = arQ[:, 0:12288]
        wouth = arQ[:, 12288:16384]

        K_f = arK
        oqt = [arK[:, 0:3072], arK[:, 3072:6144]]
        xrr = [f32c(arK, 6144, 512), f32c(arK, 7168, 512)]
        yt = [f32c(arK, 8192, 512), f32c(arK, 9216, 512)]

        v_tok = arV

        lw_f32 = f32c(arW, 0, 1024)
        wgT = [arW[:, 2048:3072], arW[:, 3072:4096]]
        lw_tf = f32c(arW, 4096, 1024)
        lw_i32 = i32c(arW, 4096, 1024)
        lw_int16 = arW[:, 6144:7168]
        lw_q = [arW[:, 7168:8192], arW[:, 8192:9216]]
        lw_s = f32c(arW, 9216, 64)
        wo_st = f32c(arW, 0, 512)
        xqf32_t = f32c(arW, 9344, 512)
        xqi32_t = i32c(arW, 10368, 512)
        rbb_t = xqf32_t
        rb = xqf32_t[0:1, :]
        brow = lw_f32[0:24, 0:128]
        obf = lw_tf[0:1, :]
        xa_t = f32c(arQ, 0, 2048)
        xsq_t = f32c(arQ, 4096, 1024)
        xnt_t = arQ[:, 6144:8192]

        em.emit(V, lambda e: e.memset(cst[:, 0:1], 1e-5))
        em.emit(V, lambda e: e.memset(ones32[:, :], 1.0))
        em.emit(V, lambda e: e.memset(ones16[:, :], 1.0))
        em.emit(V, lambda e: e.memset(st[:, :], 0.0))
        em.emit(V, lambda e: e.memset(mcol[:, :], 0.0))
        em.emit(V, lambda e: e.memset(qn2[:, :], 0.0))
        em.emit(V, lambda e: e.memset(kn2[:, :], 0.0))
        em.emit(V, lambda e: e.memset(sel2[:, :], 0.0))
        em.emit(V, lambda e: e.memset(sel2[0:64, 0:1], 1.0))
        em.emit(V, lambda e: e.memset(sel2[64:128, 1:2], 1.0))

        def vtok_ones(e):
            ap = v_tok[:, :].bitcast(BF16).rearrange("p (n c) -> p n c", c=65)
            return e.memset(ap[:, :, 64:65], 1.0)
        em.emit(V, vtok_ones)
        em.emit(V, lambda e: e.memset(selE[:, :], 1.0))
        em.emit(V, lambda e: e.memset(selO[:, :], 1.0))
        SELV = sv.n
        em.emit(P, lambda e: e.affine_select(
            selE[:, :], selE[:, :], [[1, 128]], ALU.is_ge, 0.0,
            base=0, channel_multiplier=-128), waits=[(sv, SELV)])
        em.emit(P, lambda e: e.affine_select(
            selO[:, :], selO[:, :], [[1, 128]], ALU.is_ge, 0.0,
            base=-128, channel_multiplier=128))
        INIT_V = sv.n
        INIT_P = spl.n
        em.dma(A, id128[:, :], id128_d, dmi)
        em.dma(A, brow[:, :], bqkv_d.rearrange("a (c p) -> (a c) p", p=128), dmi)
        em.dma(A, obf[:, :], bout_d, dmi)
        INIT_D = dmi.n
        em.emit(TN, lambda e: e.transpose(pb[7][:, 0:24], brow[:, :],
                                          id128[0:24, 0:24]),
                waits=[(dmi, INIT_D), (sv, INIT_V)], inc=stn)
        BT = stn.n
        em.emit(A, lambda e: e.activation(bcol[:, :], pb[7][:, 0:24],
                                          AF.Identity), waits=[(stn, BT)])
        em.emit(A, lambda e: e.activation(bq8[:, :], pb[7][:, 0:24],
                                          AF.Identity, scale=0.125))
        BC_A = sa.n
        em.emit(A, lambda e: e.activation(ob16[:, :], obf[:, :], AF.Identity),
                waits=[(dmi, INIT_D)])
        OB_A = sa.n

        xn_done_v = []
        xn_done_a = []
        for it in range(NT):
            bi = it % 2
            xt = xa_t[:, bi * 1024:(bi + 1) * 1024]
            xnt = xnt_t[:, bi * 1024:(bi + 1) * 1024]
            w = []
            if it >= 2:
                w.append((sa, xn_done_a[it - 2]))
            em.dma(A, xt, x_d[it * 128:(it + 1) * 128, :], dxa[bi], waits=w)
            DX = dxa[bi].n
            wv = [(dxa[bi], DX)]
            em.emit(A, lambda e, xt=xt: e.activation(xsq_t[:, :], xt, AF.Square),
                    waits=wv + ([(sv, xn_done_v[it - 1])] if it >= 1 else []))
            SQ = sa.n
            em.emit(V, lambda e, xt=xt: e.tensor_reduce(
                stc("m"), xt, AX.X, ALU.add), waits=[(dxa[bi], DX)])
            em.emit(V, lambda e: e.tensor_scalar_mul(stc("m"), stc("m"), 1.0 / E))
            em.emit(V, lambda e: e.tensor_reduce(
                stc("var"), xsq_t[:, :], AX.X, ALU.add), waits=[(sa, SQ)])
            em.emit(V, lambda e: e.tensor_scalar_mul(stc("var"), stc("var"),
                                                     1.0 / E))
            em.emit(V, lambda e: e.scalar_tensor_tensor(
                stc("t0"), stc("m"), stc("m"), stc("var"),
                ALU.mult, ALU.subtract))
            VW = sv.n
            em.emit(A, lambda e: e.activation(
                stc("t1"), stc("t0"), AF.Sqrt, bias=1e-5, scale=-1.0),
                waits=[(sv, VW)])
            SD = sa.n
            em.emit(V, lambda e: e.reciprocal(stc("rstd"), stc("t1")),
                    waits=[(sa, SD)])
            em.emit(V, lambda e, xt=xt: e.tensor_scalar(
                xsq_t[:, :], xt, stc("m"), None, ALU.subtract))
            em.emit(V, lambda e: e.tensor_scalar(
                xsq_t[:, :], xsq_t[:, :], stc("rstd"), None, ALU.mult))
            XN = sv.n
            em.emit(V, lambda e: e.tensor_reduce(
                stc("t1"), xsq_t[:, :], AX.X, ALU.max,
                apply_absolute_value=True))
            em.emit(V, lambda e: e.tensor_tensor(
                mc("xn"), mc("xn"), stc("t1"), ALU.max))
            xn_done_v.append(sv.n)
            em.emit(A, lambda e, xnt=xnt: e.activation(
                xnt, xsq_t[:, :], AF.Identity),
                waits=[(sv, XN)] + ([(dxn, 16 * (it - 1))] if it >= 2 else []))
            xn_done_a.append(sa.n)
            em.dma(SY, xn_h[it * 128:(it + 1) * 128, :], xnt, dxn,
                   waits=[(sa, xn_done_a[-1])])
            for c in range(NE):
                em.dma(SY, xn_f[:, c * T + it * 128:c * T + (it + 1) * 128],
                       xn_h[it * 128:(it + 1) * 128, c * 128:(c + 1) * 128],
                       dxf, waits=[(dxn, dxn.n)], transpose=True)
        LN_V = sv.n
        XNF_D = dxf.n

        def fold_scalar(src_col, dst, waits):
            em.emit(TN, lambda e: e.transpose(
                pb[7][0:1, 0:128], src_col, id128[:, :]),
                waits=list(waits) + [(dmi, INIT_D), (sv, sv.n), (sa, sa.n)],
                inc=stn)
            TW = stn.n
            em.emit(V, lambda e, dst=dst: e.tensor_reduce(
                stc(dst, 0, 1), pb[7][0:1, 0:128], AX.X, ALU.max),
                waits=[(stn, TW)])
            return sv.n

        MX_V = fold_scalar(mc("xn"), "t0", [(sv, LN_V)])

        def collective(ci, waits):
            em.dma(P, cc_i[ci], st[0:1, COL["t0"]:COL["t0"] + 2], dcc,
                   waits=waits)
            em.emit(P, lambda e, ci=ci: e.collective_compute(
                "AllReduce", ALU.max, [[0, 1, 2, 3, 4, 5, 6, 7]],
                [cc_i[ci].opt()], [cc_o[ci].opt()]),
                waits=[(dcc, dcc.n)], inc=scc, no_chain=True)
            return scc.n

        CC1 = collective(0, [(sv, MX_V)])
        em.dma(P, st[0:1, COL["t0"]:COL["t0"] + 2], cc_o[0], dcc,
               waits=[(scc, CC1)])
        CC1D = dcc.n

        def bcast_scale(srccol, scol, icol, waits, extra=()):
            em.emit(TN, lambda e: e.matmul(
                pb[7][:, 0:1], ones32[0:1, 0:128],
                st[0:1, COL[srccol]:COL[srccol] + 1], start=True, stop=True),
                waits=list(waits) + [(sv, sv.n), (sa, sa.n)], inc=stn)
            TW = stn.n
            em.emit(V, lambda e: e.tensor_scalar(
                stc(scol), pb[7][:, 0:1], 1.0 / QMAX, 1e-12, ALU.mult, ALU.max),
                waits=[(stn, TW)])
            em.emit(V, lambda e: e.reciprocal(stc(icol), stc(scol)))
            for f in extra:
                f()
            return sv.n

        S1_V = bcast_scale("t0", "s1", "i1", [(dcc, CC1D)])

        lqst = {"n": 0, "a": 0, "p": 0, "wr": 0, "q": [0, 0]}

        def lpbq_tile(src_ap, dst_ap):
            k = lqst["n"]
            lqst["n"] += 1
            w = [(dmi, INIT_D)]
            if k == 0:
                w += [(stn, BT)]
            if k >= 1:
                w += [(spl, lqst["p"])]
            em.dma(SY, lw_f32, src_ap, dlq[k % 2], waits=w)
            RD = dlq[k % 2].n
            wv3 = lw_f32.rearrange("p (nb e) -> p nb e", e=32)
            war = [(spl, lqst["p"])] if k >= 1 else []
            em.emit(V, lambda e, wv3=wv3: e.tensor_reduce(
                lw_s[:, 0:32], wv3, AX.X, ALU.max, apply_absolute_value=True),
                waits=[(dlq[k % 2], RD)] + war)
            em.emit(V, lambda e: e.tensor_scalar(
                lw_s[:, 32:64], lw_s[:, 0:32], 1.0 / 7.0, 1e-12,
                ALU.mult, ALU.max))
            em.emit(V, lambda e: e.reciprocal(lw_s[:, 0:32], lw_s[:, 32:64]))
            SV_ = sv.n
            inv_b = lw_s[:, 0:32].unsqueeze(2).broadcast_to([128, 32, 32])
            s_b = lw_s[:, 32:64].unsqueeze(2).broadcast_to([128, 32, 32])
            tfv = lw_tf.rearrange("p (nb e) -> p nb e", e=32)
            war_t = [(sa, lqst["a"])] if k >= 1 else []
            war_t2 = [(sa, OB_A)] if k == 0 else []
            em.emit(P, lambda e, wv3=wv3, inv_b=inv_b, tfv=tfv: e.tensor_tensor(
                tfv, wv3, inv_b, ALU.mult), waits=[(sv, SV_)] + war_t + war_t2)
            TM = spl.n
            lqst["p"] = TM
            em.emit(A, lambda e: e.activation(lw_i32[:, :], lw_tf, AF.Identity),
                    waits=[(spl, TM)])
            em.emit(A, lambda e: e.activation(lw_int16[:, :], lw_i32[:, :],
                                              AF.Identity))
            IA = sa.n
            lqst["a"] = IA
            qs = lw_q[k % 2]
            war_q = [(dqw, lqst["q"][k % 2])]
            em.emit(P, lambda e, qs=qs, s_b=s_b: e.tensor_tensor(
                qs.rearrange("p (nb e) -> p nb e", e=32),
                lw_int16.rearrange("p (nb e) -> p nb e", e=32),
                s_b, ALU.mult), waits=[(sa, IA)] + war_q)
            QP = spl.n
            em.dma(SY, dst_ap, qs, dqw, waits=[(spl, QP)])
            lqst["wr"] = dqw.n
            lqst["q"][k % 2] = dqw.n

        def lpbq_jobs(w_src, w_dst, nrows, ncols):
            out = []
            for rc in range(nrows // 128):
                for sc in range(ncols // 1024):
                    out.append((w_src[rc * 128:(rc + 1) * 128,
                                      sc * 1024:(sc + 1) * 1024],
                                w_dst[rc * 128:(rc + 1) * 128,
                                      sc * 1024:(sc + 1) * 1024]))
            return out

        side_jobs = (lpbq_jobs(wg_d, wqg_h, I3, E) + lpbq_jobs(wu_d, wqu_h, I3, E)
                     + lpbq_jobs(wd_d, wqd_h, E, I3))
        side_it = iter(side_jobs)

        def pop_side(n=1):
            for _ in range(n):
                job = next(side_it, None)
                if job is not None:
                    lpbq_tile(*job)

        wqkv_r = wqkv_d.rearrange("(c p) f -> p c f", p=128)
        qkv_conv_a = [0, 0]
        whmm = [0, 0]
        evq = [0, 0]
        vwd = [0, 0, 0, 0]
        vst_a = [0, 0, 0, 0]

        for m in range(24):
            b = m % 2
            j, r3 = divmod(m, 3)
            w = [(sa, qkv_conv_a[b])] if m >= 2 else []
            for k in range(NE):
                em.dma(SY, wst[b][:, k * 128:(k + 1) * 128],
                       wqkv_r[:, k, m * 128:(m + 1) * 128], dwl[b],
                       waits=(w if k == 0 else []))
            DW = dwl[b].n
            wconv = [(dwl[b], DW)]
            if m >= 2:
                wconv.append((stn, whmm[b]))
            em.emit(A, lambda e, b=b: e.activation(wh[b][:, :], wst[b][:, :],
                                                   AF.Identity), waits=wconv)
            WH = sa.n
            qkv_conv_a[b] = WH
            whv = wh[b].rearrange("p (c f) -> p c f", c=NE)
            for t in range(NQ):
                bk = (m * NQ + t) % 2
                wl = [(sa, WH), (dxf, XNF_D)]
                if m * NQ + t >= 2:
                    wl.append((sa, evq[bk]))
                for k in range(NE):
                    em.emit(TN, lambda e, bk=bk, k=k, t=t, whv=whv: e.matmul(
                        pb[bk][:, :], whv[:, k, :],
                        xn_f[:, k * T + t * 512:k * T + (t + 1) * 512],
                        start=(k == 0), stop=(k == NE - 1)),
                        waits=(wl if k == 0 else ()),
                        inc=(stn if k == NE - 1 else None))
                MM = stn.n
                if r3 == 0:
                    em.emit(A, lambda e, bk=bk, j=j, t=t, m=m: e.activation(
                        Q_f[0:64, j * T + t * 512:j * T + (t + 1) * 512],
                        pb[bk][0:64, :], AF.Identity,
                        bias=bq8[0:64, m:m + 1], scale=0.125),
                        waits=[(stn, MM), (sa, BC_A)])
                    em.emit(A, lambda e, bk=bk, j=j, t=t, m=m: e.activation(
                        K_f[0:64, j * T + t * 512:j * T + (t + 1) * 512],
                        pb[bk][64:128, :], AF.Identity,
                        bias=bcol[64:128, m:m + 1]))
                elif r3 == 1:
                    em.emit(A, lambda e, bk=bk, t=t, m=m: e.activation(
                        vst[t][0:64, :], pb[bk][0:64, :], AF.Identity,
                        bias=bcol[0:64, m:m + 1]),
                        waits=[(stn, MM), (sa, BC_A), (dvw, vwd[t])])
                    em.emit(A, lambda e, bk=bk, j=j, t=t, m=m: e.activation(
                        Q_f[64:128, j * T + t * 512:j * T + (t + 1) * 512],
                        pb[bk][64:128, :], AF.Identity,
                        bias=bq8[64:128, m:m + 1], scale=0.125))
                else:
                    em.emit(A, lambda e, bk=bk, j=j, t=t, m=m: e.activation(
                        K_f[64:128, j * T + t * 512:j * T + (t + 1) * 512],
                        pb[bk][0:64, :], AF.Identity,
                        bias=bcol[0:64, m:m + 1]),
                        waits=[(stn, MM), (sa, BC_A)])
                    em.emit(A, lambda e, bk=bk, t=t, m=m: e.activation(
                        vst[t][64:128, :], pb[bk][64:128, :], AF.Identity,
                        bias=bcol[64:128, m:m + 1]), waits=[(dvw, vwd[t])])
                    vst_a[t] = sa.n
                    em.dma(SY, v_h[j * 128:(j + 1) * 128,
                                   t * 512:(t + 1) * 512], vst[t], dvw,
                           waits=[(sa, vst_a[t])])
                    vwd[t] = dvw.n
                evq[bk] = sa.n
            whmm[b] = stn.n
            if m % 2 == 1:
                pop_side(2)
        QKV_MM = stn.n
        QKV_A = sa.n
        VW_D = dvw.n
        if dbg:
            em.dma(SY, xnf_dbg, arA[:, :], dmi, waits=[(stn, QKV_MM)])

        vcp = [0, 0]
        for idx in range(NT * NE):
            kt, j = divmod(idx, NE)
            r = idx % 2
            land = vland[:, r * 128:(r + 1) * 128].bitcast(BF16)
            w = [(dvw, VW_D)]
            if idx >= 2:
                w.append((spl, vcp[r]))
            em.dma(A, land, v_h[j * 128:(j + 1) * 128,
                                kt * 128:(kt + 1) * 128],
                   dvr, waits=w, transpose=True)
            DVT = dvr.n
            s0 = (kt * H + 2 * j) * 65
            s1 = (kt * H + 2 * j + 1) * 65
            em.emit(P, lambda e, land=land, s0=s0: e.tensor_copy(
                v_tok[:, s0:s0 + 64].bitcast(BF16), land[:, 0:64]),
                waits=[(dvr, DVT), (sv, INIT_V)])
            em.emit(P, lambda e, land=land, s1=s1: e.tensor_copy(
                v_tok[:, s1:s1 + 64].bitcast(BF16), land[:, 64:128]))
            vcp[r] = spl.n
        VT_P = spl.n
        VT_D = dvr.n

        xq_a = [0]
        first_xq = True
        for c in range(NE):
            for t in range(NQ):
                wv = [(sv, S1_V)]
                if first_xq:
                    wv += [(stn, QKV_MM), (sa, QKV_A), (dvw, VW_D)]
                    first_xq = False
                if xq_a[0]:
                    wv.append((sa, xq_a[0]))
                em.emit(V, lambda e, c=c, t=t: e.tensor_scalar(
                    xqf32_t, xn_f[:, c * T + t * 512:c * T + (t + 1) * 512],
                    stc("i1"), -32768.0, ALU.mult, ALU.max), waits=wv)
                em.emit(V, lambda e: e.tensor_scalar(
                    xqi32_t, xqf32_t, 32767.0, None, ALU.min))
                I32V = sv.n
                em.emit(A, lambda e, c=c, t=t: e.activation(
                    xq_f[:, c * T + t * 512:c * T + (t + 1) * 512],
                    xqi32_t, AF.Identity), waits=[(sv, I32V)])
                xq_a[0] = sa.n
        XQ_A = sa.n
        XQ_V = sv.n

        spill_cnt = {"g": 0, "u": 0}
        gumax_v = {"g": 0, "u": 0}
        gu_ev = {"ev0": 0, "ev1": 0}
        gumm = [0, 0]
        gusp = [0, 0]

        def gu_load(wq_h, i, wq_wr_need):
            b = i % 2
            w = [(dqw, wq_wr_need)]
            if i >= 2:
                w.append((stn, gumm[b]))
            for k in range(NE):
                em.dma(SY, wgT[b][:, k * 128:(k + 1) * 128],
                       wq_h[i * 128:(i + 1) * 128, k * 128:(k + 1) * 128],
                       dgt[b], waits=(w if k == 0 else []), transpose=True)
            return dgt[b].n

        def gu_stream(wq_h, sp_h, mx, wq_wr_need):
            gl = [0, 0]
            gl[0] = gu_load(wq_h, 0, wq_wr_need)
            for i in range(NI):
                b = i % 2
                if i + 1 < NI:
                    gl[(i + 1) % 2] = gu_load(wq_h, i + 1, wq_wr_need)
                DG = gl[b]
                for t in range(NQ):
                    bk = (i * NQ + t) % 2
                    wl = [(dgt[b], DG), (sa, XQ_A)]
                    if gu_ev[f"ev{bk}"]:
                        wl.append((sa, gu_ev[f"ev{bk}"]))
                    for k in range(NE):
                        em.emit(TN, lambda e, bk=bk, b=b, k=k, t=t: e.matmul(
                            pb[bk][:, :], wgT[b][:, k * 128:(k + 1) * 128],
                            xq_f[:, k * T + t * 512:k * T + (t + 1) * 512],
                            start=(k == 0), stop=(k == NE - 1)),
                            waits=(wl if k == 0 else ()),
                            inc=(stn if k == NE - 1 else None))
                    MM = stn.n
                    sl = m_out[bk]
                    em.emit(A, lambda e, sl=sl, bk=bk: e.activation(
                        sl, pb[bk][:, :], AF.Identity, scale=stc("s1")),
                        waits=[(stn, MM), (dsp, gusp[bk])])
                    EVA = sa.n
                    gu_ev[f"ev{bk}"] = EVA
                    em.emit(V, lambda e, sl=sl: e.tensor_reduce(
                        stc("t1"), sl, AX.X, ALU.max,
                        apply_absolute_value=True), waits=[(sa, EVA)])
                    em.emit(V, lambda e, mx=mx: e.tensor_tensor(
                        mc(mx), mc(mx), stc("t1"), ALU.max))
                    gumax_v[mx] = sv.n
                    em.dma(SY, sp_h[i * 128:(i + 1) * 128,
                                    t * 512:(t + 1) * 512], sl, dsp,
                           waits=[(sv, gumax_v[mx])])
                    gusp[bk] = dsp.n
                    spill_cnt[mx] += 1
                gumm[b] = stn.n
                if i % 2 == 1:
                    pop_side(1)

        pop_side(26)
        gu_stream(wqg_h, spg_h, "g", 16 * 24)
        pop_side(12)
        gu_stream(wqu_h, spu_h, "u", 16 * 48)
        pop_side(len(side_jobs))
        GU_MM = stn.n
        GU_SPILLS = 16 * (spill_cnt["g"] + spill_cnt["u"])
        LPBQ_WR = lqst["wr"]
        assert lqst["n"] == len(side_jobs)

        G_V = fold_scalar(mc("g"), "t0", [(sv, gumax_v["g"])])
        U_V = fold_scalar(mc("u"), "t1", [(sv, gumax_v["u"])])
        CC2 = collective(1, [(sv, U_V)])
        em.dma(P, st[0:1, COL["t0"]:COL["t0"] + 2], cc_o[1], dcc,
               waits=[(scc, CC2)])
        CC2D = dcc.n

        def s3_extra():
            em.emit(V, lambda e: e.tensor_scalar_mul(stc("hs3"), stc("s3"), 0.5))
            em.emit(V, lambda e: e.tensor_scalar_mul(stc("q3"), stc("s3"),
                                                     1.0 / 65536.0))
        S3_V = bcast_scale("t0", "s3", "i3", [(dcc, CC2D)], extra=(s3_extra,))
        S2_V = bcast_scale("t1", "s2", "i2", [(dcc, CC2D)])

        def norms(src, acc):
            for hc in range(NE):
                em.emit(A, lambda e, hc=hc, src=src: e.activation(
                    sq_f32[:, :], src[:, hc * T:(hc + 1) * T], AF.Square),
                    waits=[(stn, stn.n), (sv, XQ_V)])
                SQA = sa.n
                for t in range(NQ):
                    em.emit(TN, lambda e, t=t: e.matmul(
                        pb[6][0:2, :], sel2[:, :],
                        sq_f32[:, t * 512:(t + 1) * 512],
                        start=True, stop=True),
                        waits=[(sa, SQA), (sv, sv.n)], inc=stn)
                    TW = stn.n
                    em.emit(V, lambda e: e.tensor_reduce(
                        stc("t1", 0, 2), pb[6][0:2, :], AX.X, ALU.max),
                        waits=[(stn, TW)])
                    em.emit(V, lambda e, hc=hc, acc=acc: e.tensor_tensor(
                        acc[0:2, hc:hc + 1], acc[0:2, hc:hc + 1],
                        stc("t1", 0, 2), ALU.max))
            return sv.n

        QN_V = norms(Q_f, qn2)
        KN_V = norms(K_f, kn2)
        em.emit(V, lambda e: e.tensor_tensor(nbsrc[0:2, :], qn2[0:2, :],
                                             kn2[0:2, :], ALU.mult),
                waits=[(sv, KN_V)])
        NB1 = sv.n
        em.emit(A, lambda e: e.activation(nbsrc[0:2, :], nbsrc[0:2, :],
                                          AF.Sqrt), waits=[(sv, NB1)])
        NB2 = sa.n
        em.emit(V, lambda e: e.tensor_scalar(
            nbsrc[0:2, :], nbsrc[0:2, :], -1.0, EXPOFF, ALU.mult, ALU.add),
            waits=[(sa, NB2)])
        NB3 = sv.n
        em.emit(TN, lambda e: e.matmul(pb[6][:, 0:8], selE[:, :],
                                       nbsrc[0:2, :], start=True, stop=True),
                waits=[(sv, NB3), (spl, INIT_P), (sv, sv.n)], inc=stn)
        NBT1 = stn.n
        em.emit(A, lambda e: e.activation(nbE[:, :], pb[6][:, 0:8],
                                          AF.Identity), waits=[(stn, NBT1)])
        NBA1 = sa.n
        em.emit(TN, lambda e: e.matmul(pb[6][:, 0:8], selO[:, :],
                                       nbsrc[0:2, :], start=True, stop=True),
                waits=[(sa, NBA1)], inc=stn)
        NBT2 = stn.n
        em.emit(A, lambda e: e.activation(nbO[:, :], pb[6][:, 0:8],
                                          AF.Identity), waits=[(stn, NBT2)])
        NEGB_A = sa.n

        m1 = {"n": 0, "ec": [0, 0], "sp": [0, 0], "wr": 0, "th": [0, 0],
              "pend": [None, None]}

        def m1_pre(jn):
            i, t = divmod(jn, NQ)
            r = jn % 2
            w = [(dsp, GU_SPILLS)]
            if m1["ec"][r]:
                w.append((sv, m1["ec"][r]))
            if m1["sp"][r]:
                w.append((dsp, m1["sp"][r]))
            em.dma(SY, m_gin[r], spg_h[i * 128:(i + 1) * 128,
                                       t * 512:(t + 1) * 512], dgr[r], waits=w)
            RD = dgr[r].n
            em.emit(V, lambda e, r=r: e.tensor_scalar(
                m_tf[r], m_gin[r], stc("i3"), -32768.0, ALU.mult, ALU.max),
                waits=[(dgr[r], RD), (sv, S3_V)])
            em.emit(V, lambda e, r=r: e.tensor_scalar(
                m_i32[r], m_tf[r], 32767.0, None, ALU.min))
            em.emit(V, lambda e, r=r: e.tensor_copy(m_gf[r], m_i32[r]))
            GF = sv.n
            em.emit(A, lambda e, r=r: e.activation(m_th[r], m_gf[r], AF.Tanh,
                                                   scale=stc("hs3")),
                    waits=[(sv, GF)])
            m1["th"][r] = sa.n

        def m1_post(jn):
            i, t = divmod(jn, NQ)
            r = jn % 2
            em.emit(V, lambda e, r=r: e.tensor_scalar(
                m_tf[r], m_th[r], 32768.0, 32768.0, ALU.mult, ALU.add),
                waits=[(sa, m1["th"][r])])
            em.emit(V, lambda e, r=r: e.tensor_scalar(
                m_i32[r], m_tf[r], 65535.0, None, ALU.min))
            em.emit(V, lambda e, r=r: e.tensor_copy(m_th[r], m_i32[r]))
            em.emit(V, lambda e, r=r: e.tensor_tensor(
                m_tf[r], m_gf[r], m_th[r], ALU.mult))
            em.emit(V, lambda e, r=r: e.tensor_scalar(
                m_out[r], m_tf[r], stc("q3"), None, ALU.mult))
            em.emit(V, lambda e, r=r: e.tensor_reduce(
                stc("t1"), m_out[r], AX.X, ALU.max, apply_absolute_value=True))
            em.emit(V, lambda e: e.tensor_tensor(
                mc("g2"), mc("g2"), stc("t1"), ALU.max))
            m1["ec"][r] = sv.n
            em.dma(SY, spg2_h[i * 128:(i + 1) * 128, t * 512:(t + 1) * 512],
                   m_out[r], dsp, waits=[(sv, m1["ec"][r])])
            m1["sp"][r] = dsp.n
            m1["wr"] = dsp.n

        def m1_tile():
            jn = m1["n"]
            m1["n"] += 1
            m1_pre(jn)
            if jn >= 1:
                m1_post(jn - 1)
            if jn == NI * NQ - 1:
                m1_post(jn)

        ps_free = {0: 0, 1: 0, 2: 0, 3: 0}
        pt_free = [0, 0, 0, 0]
        o_ev = {4: 0, 5: 0}
        ep_a = [NEGB_A]
        si_ctr = [0]

        def attn_pair(hc):
            def scores_exp(qc, kt):
                diag = kt >= 4 * qc
                info = []
                for par, (sbk0, nb) in enumerate([(2, nbE), (3, nbO)]):
                    sbk = sbk0 if kt % 2 == 0 else par
                    si = si_ctr[0]
                    si_ctr[0] += 1
                    p0 = par * 64
                    w = [(sa, NEGB_A)]
                    if ps_free[sbk]:
                        w.append((sa, ps_free[sbk]))
                    em.emit(TN, lambda e, sbk=sbk, p0=p0, hc=hc, kt=kt,
                            qc=qc: e.matmul(
                        pb[sbk][:, :],
                        K_f[p0:p0 + 64,
                            hc * T + kt * 128:hc * T + (kt + 1) * 128],
                        Q_f[p0:p0 + 64,
                            hc * T + qc * 512:hc * T + (qc + 1) * 512],
                        start=True, stop=True), waits=w, inc=stn)
                    MMW = stn.n
                    slot = si % 4
                    we = [(stn, MMW)]
                    if pt_free[slot]:
                        we.append((stn, pt_free[slot]))
                    em.emit(A, lambda e, sbk=sbk, slot=slot, nb=nb, hc=hc:
                            e.activation(pt_bf[slot], pb[sbk][:, :],
                                         AF.Exp, bias=nb[:, hc:hc + 1]),
                            waits=we)
                    EXA = sa.n
                    ps_free[sbk] = EXA
                    if diag:
                        s = kt - 4 * qc
                        em.emit(P, lambda e, slot=slot, s=s:
                                e.affine_select(
                            pt_bf[slot], pt_bf[slot], [[1, 512]],
                            ALU.is_ge, 0.0, base=-128 * s,
                            channel_multiplier=-1), waits=[(sa, EXA)])
                        rdy = (spl, spl.n)
                    else:
                        rdy = (sa, EXA)
                    info.append((slot, rdy))
                return info

            def pv(qc, kt, nkt, info):
                for par, obk in ((0, 4), (1, 5)):
                    slot, rdy = info[par]
                    h = 2 * hc + par
                    wpv = [(spl, VT_P), rdy]
                    if kt == 0 and o_ev[obk]:
                        wpv.append((sv, o_ev[obk]))
                    em.emit(TN, lambda e, obk=obk, kt=kt, h=h, slot=slot,
                            nkt=nkt: e.matmul(
                        pb[obk][0:65, :],
                        v_tok[:, (kt * H + h) * 65:(kt * H + h + 1) * 65]
                        .bitcast(BF16),
                        pt_bf[slot], start=(kt == 0), stop=(kt == nkt - 1)),
                        waits=wpv, inc=stn)
                    pt_free[slot] = stn.n

            for qc in range(NQ):
                nkt = 4 * qc + 4
                pend = scores_exp(qc, 0)
                for kt in range(nkt):
                    nxt = scores_exp(qc, kt + 1) if kt + 1 < nkt else None
                    pv(qc, kt, nkt, pend)
                    pend = nxt
                for par, obk in ((0, 4), (1, 5)):
                    PVM = stn.n
                    em.emit(V, lambda e, obk=obk: e.tensor_scalar(
                        rb, pb[obk][64:65, :], 1e-12, None, ALU.add),
                        waits=[(stn, PVM)])
                    em.emit(V, lambda e: e.reciprocal(rb, rb))
                    RC = sv.n
                    em.emit(TN, lambda e: e.matmul(
                        pb[6][0:64, :], ones32[0:1, 0:64], rb,
                        start=True, stop=True),
                        waits=[(sv, RC), (sa, ep_a[0])], inc=stn)
                    BCW = stn.n
                    em.emit(A, lambda e: e.activation(
                        rbb_t[0:64, :], pb[6][0:64, :], AF.Identity),
                        waits=[(stn, BCW)])
                    ep_a[0] = sa.n
                    em.emit(V, lambda e, obk=obk, par=par, hc=hc, qc=qc:
                            e.tensor_tensor(
                        O_f[par * 64:par * 64 + 64,
                            hc * T + qc * 512:hc * T + (qc + 1) * 512],
                        pb[obk][0:64, :], rbb_t[0:64, :], ALU.mult),
                        waits=[(sa, ep_a[0])])
                    o_ev[obk] = sv.n

        for hc in range(NE):
            attn_pair(hc)
            if hc >= 2:
                for _ in range(16):
                    if m1["n"] < NI * NQ:
                        m1_tile()
        while m1["n"] < NI * NQ:
            m1_tile()
        ATT_MM = stn.n
        O_V = sv.n
        M1_WR = m1["wr"]

        G2_V = fold_scalar(mc("g2"), "t0",
                           [(sv, m1["ec"][0]), (sv, m1["ec"][1])])
        CC3 = collective(2, [(sv, G2_V)])
        em.dma(P, st[0:1, COL["t0"]:COL["t0"] + 2], cc_o[2], dcc,
               waits=[(scc, CC3)])
        CC3D = dcc.n

        def s4_extra():
            em.emit(V, lambda e: e.tensor_tensor(
                stc("s42"), stc("s4"), stc("s2"), ALU.mult))
        S4_V = bcast_scale("t0", "s4", "i4", [(dcc, CC3D), (sv, S2_V)],
                           extra=(s4_extra,))

        m2 = {"ec": [0, 0], "sp": [0, 0]}
        m2_pend = [None, None]
        for jn in range(NI * NQ):
            i, t = divmod(jn, NQ)
            r = jn % 2
            w = [(dsp, M1_WR)]
            if m2["ec"][r]:
                w.append((sv, m2["ec"][r]))
            if m2["sp"][r]:
                w.append((dsp, m2["sp"][r]))
            em.dma(SY, m_gin[r], spg2_h[i * 128:(i + 1) * 128,
                                        t * 512:(t + 1) * 512], dgr[r],
                   waits=w)
            RDg = dgr[r].n
            em.dma(SY, m_uin[r], spu_h[i * 128:(i + 1) * 128,
                                       t * 512:(t + 1) * 512], dur[r],
                   waits=([(sv, O_V)] if jn < 2 else []))
            RDu = dur[r].n
            if m2_pend[r] is not None:
                m2_pend[r]()
                m2_pend[r] = None
            em.emit(V, lambda e, r=r: e.tensor_scalar(
                m_tf[r], m_gin[r], stc("i4"), -32768.0, ALU.mult, ALU.max),
                waits=[(dgr[r], RDg), (sv, S4_V)])
            em.emit(V, lambda e, r=r: e.tensor_scalar(
                m_i32[r], m_tf[r], 32767.0, None, ALU.min))
            em.emit(V, lambda e, r=r: e.tensor_copy(m_gf[r], m_i32[r]))
            em.emit(V, lambda e, r=r: e.tensor_scalar(
                m_tf[r], m_uin[r], stc("i2"), -32768.0, ALU.mult, ALU.max),
                waits=[(dur[r], RDu)])
            em.emit(V, lambda e, r=r: e.tensor_scalar(
                m_ui32[r], m_tf[r], 32767.0, None, ALU.min))
            em.emit(V, lambda e, r=r: e.tensor_copy(m_uf[r], m_ui32[r]))
            em.emit(V, lambda e, r=r: e.tensor_tensor(
                m_tf[r], m_gf[r], m_uf[r], ALU.mult))
            em.emit(V, lambda e, r=r: e.tensor_scalar(
                m_out[r], m_tf[r], stc("s42"), None, ALU.mult))
            em.emit(V, lambda e, r=r: e.tensor_reduce(
                stc("t1"), m_out[r], AX.X, ALU.max, apply_absolute_value=True))
            em.emit(V, lambda e: e.tensor_tensor(
                mc("o"), mc("o"), stc("t1"), ALU.max))
            m2["ec"][r] = sv.n
            EC = m2["ec"][r]

            def spill(r=r, i=i, t=t, EC=EC):
                em.dma(SY, spo_h[i * 128:(i + 1) * 128,
                                 t * 512:(t + 1) * 512],
                       m_out[r], dsp, waits=[(sv, EC)])
                m2["sp"][r] = dsp.n
            m2_pend[r] = spill
        for r in range(2):
            if m2_pend[r] is not None:
                m2_pend[r]()
                m2_pend[r] = None
        M2_WR = dsp.n

        O5_V = fold_scalar(mc("o"), "t0",
                           [(sv, m2["ec"][0]), (sv, m2["ec"][1])])
        CC4 = collective(3, [(sv, O5_V)])
        em.dma(P, st[0:1, COL["t0"]:COL["t0"] + 2], cc_o[3], dcc,
               waits=[(scc, CC4)])
        CC4D = dcc.n
        S5_V = bcast_scale("t0", "s5", "i5", [(dcc, CC4D)])

        m3 = {"ec": [0, 0], "sp": [0, 0]}
        m3_pend = [None, None]
        for jn in range(NI * NQ):
            i, t = divmod(jn, NQ)
            r = jn % 2
            w = [(dsp, M2_WR)]
            if m3["ec"][r]:
                w.append((sv, m3["ec"][r]))
            if m3["sp"][r]:
                w.append((dsp, m3["sp"][r]))
            em.dma(SY, m_gin[r], spo_h[i * 128:(i + 1) * 128,
                                       t * 512:(t + 1) * 512], dgr[r], waits=w)
            RD = dgr[r].n
            if m3_pend[r] is not None:
                m3_pend[r]()
                m3_pend[r] = None
            em.emit(V, lambda e, r=r: e.tensor_scalar(
                m_tf[r], m_gin[r], stc("i5"), -32768.0, ALU.mult, ALU.max),
                waits=[(dgr[r], RD), (sv, S5_V)])
            em.emit(V, lambda e, r=r: e.tensor_scalar(
                m_i32[r], m_tf[r], 32767.0, None, ALU.min))
            em.emit(V, lambda e, r=r: e.tensor_copy(m_gf[r], m_i32[r]))
            em.emit(V, lambda e, r=r: e.tensor_scalar(
                m_out[r], m_gf[r], stc("s5"), None, ALU.mult))
            m3["ec"][r] = sv.n
            EC = m3["ec"][r]

            def spill3(r=r, i=i, t=t, EC=EC):
                em.dma(SY, spq_h[i * 128:(i + 1) * 128,
                                 t * 512:(t + 1) * 512],
                       m_out[r], dsp, waits=[(sv, EC)])
                m3["sp"][r] = dsp.n
            m3_pend[r] = spill3
        for r in range(2):
            if m3_pend[r] is not None:
                m3_pend[r]()
                m3_pend[r] = None
        M3_WR = dsp.n

        WH_A = [0]
        p8mm = [0, 0]
        p8y = [0, 0]
        ydma = []
        for eh in range(2):
            for i in range(NI):
                wfirst = ([(stn, ATT_MM), (dqw, LPBQ_WR)]
                          if (i == 0 and eh == 0) else
                          ([(sa, WH_A[0]), (stn, stn.n)]
                           if i == 0 and eh == 1 else []))
                em.dma(SY, wdTh[:, i * 512:(i + 1) * 512],
                       wqd_h[eh * 512:(eh + 1) * 512,
                             i * 128:(i + 1) * 128], dwt,
                       waits=wfirst, transpose=True)
            for k in range(NE):
                wld = [(spl, spl.n), (sv, sv.n)] if (k == 0 and eh == 0) \
                    else ([(sa, WH_A[0]), (stn, stn.n)] if k == 0
                          else [(sa, WH_A[0])])
                em.dma(A, wo_st[:, :],
                       wout_d[k * 128:(k + 1) * 128,
                              eh * 512:(eh + 1) * 512], dwl[0], waits=wld)
                WD = dwl[0].n
                wca = [(dwl[0], WD)]
                if eh == 0 and k == 0:
                    wca.append((stn, ATT_MM))
                em.emit(A, lambda e, k=k: e.activation(
                    wouth[:, k * 512:(k + 1) * 512], wo_st[:, :],
                    AF.Identity), waits=wca)
                WH_A[0] = sa.n
            WDT = dwt.n
            WHA = sa.n

            def p8_load(t):
                r = t % 2
                n8 = eh * NT + t
                w = [(dsp, M3_WR)]
                if n8 == 0:
                    w.append((stn, ATT_MM))
                if n8 >= 2:
                    w.append((stn, p8mm[r]))
                for i in range(NI):
                    em.dma(SY, oqt[r][:, i * 128:(i + 1) * 128],
                           spq_h[i * 128:(i + 1) * 128,
                                 t * 128:(t + 1) * 128], doq[r],
                           waits=(w if i == 0 else []))
                wx = [(sv, p8y[r])] if n8 >= 2 else [(stn, ATT_MM)]
                em.dma(A, xrr[r], x_d[t * 128:(t + 1) * 128,
                                      eh * 512:(eh + 1) * 512], dxr[r],
                       waits=wx)
                return doq[r].n, dxr[r].n

            pend_y = [None]
            lded = {0: p8_load(0)}
            for t in range(NT):
                r = t % 2
                n8 = eh * NT + t
                if t + 1 < NT:
                    lded[t + 1] = p8_load(t + 1)
                DOQ, DXR = lded.pop(t)
                bk = r
                wl = [(sv, O_V), (sa, WHA), (dwt, WDT), (doq[r], DOQ),
                      (sa, OB_A)]
                if n8 >= 2:
                    wl.append((sv, p8y[r]))
                for k in range(NE):
                    em.emit(TN, lambda e, bk=bk, k=k, t=t: e.matmul(
                        pb[bk][:, :],
                        O_f[:, k * T + t * 128:k * T + (t + 1) * 128],
                        wouth[:, k * 512:(k + 1) * 512],
                        start=(k == 0), stop=False),
                        waits=(wl if k == 0 else ()))
                for i in range(NI):
                    em.emit(TN, lambda e, bk=bk, i=i, r=r: e.matmul(
                        pb[bk][:, :], oqt[r][:, i * 128:(i + 1) * 128],
                        wdTh[:, i * 512:(i + 1) * 512],
                        start=False, stop=False))
                em.emit(TN, lambda e, bk=bk, eh=eh: e.matmul(
                    pb[bk][:, :], ones16[0:1, 0:128],
                    ob16[0:1, eh * 512:(eh + 1) * 512],
                    start=False, stop=True), inc=stn)
                MM = stn.n
                p8mm[r] = MM
                wy = [(stn, MM), (dxr[r], DXR)]
                if len(ydma) >= 2:
                    wy.append((dy, ydma[-2]))
                em.emit(V, lambda e, bk=bk, r=r: e.tensor_tensor(
                    yt[r], pb[bk][:, :], xrr[r], ALU.add), waits=wy)
                YV = sv.n
                p8y[r] = YV
                em.dma(SY, y_d[t * 128:(t + 1) * 128,
                               eh * 512:(eh + 1) * 512], yt[r], dy,
                       waits=[(sv, YV)])
                ydma.append(dy.n)

        em.emit(P, lambda e: e.engine_nop(), waits=[(dy, dy.n)], no_chain=True)

        n_inst = len(em.steps)
        em.finalize(block)
    ends = []
    for a in nc.m.functions[0].allocations:
        if not hasattr(a, "memorylocations"):
            continue
        for ml in a.memorylocations:
            nm = getattr(ml, "name", "")
            if nm in ("arA", "arB", "arQ", "arK", "arV", "arW", "st", "nbE",
                      "nbO", "bcol", "bq8", "ob16", "id128", "ones32",
                      "ones16", "sel2", "selE", "selO", "qn2", "kn2",
                      "nbsrc", "mcol", "cst", "vland"):
                ends.append(ml.addr + ml.size() // 128)
    assert max(ends) <= 196608, f"SBUF overflow: {max(ends)}"
    if dbg:
        print(f"[build] {n_inst} emitted steps")
    return nc


_NC_CACHE = {}


def kernel(L0_attn_x, L0_attn_qkv_w, L0_attn_qkv_b, L0_attn_out_w,
           L0_attn_out_b, L0_ffn_w_gate, L0_ffn_w_up, L0_ffn_w_down):
    x = np.ascontiguousarray(L0_attn_x, dtype=np.float32)
    wqkv = np.ascontiguousarray(L0_attn_qkv_w, dtype=np.float32).reshape(E, I3)
    bqkv = np.ascontiguousarray(L0_attn_qkv_b, dtype=np.float32).reshape(1, I3)
    wout = np.ascontiguousarray(L0_attn_out_w, dtype=np.float32).reshape(E, E)
    bout = np.ascontiguousarray(L0_attn_out_b, dtype=np.float32).reshape(1, E)
    wg = np.ascontiguousarray(L0_ffn_w_gate, dtype=np.float32)
    wu = np.ascontiguousarray(L0_ffn_w_up, dtype=np.float32)
    wd = np.ascontiguousarray(L0_ffn_w_down, dtype=np.float32)

    if "nc" not in _NC_CACHE:
        _NC_CACHE["nc"] = build()
    nc = _NC_CACHE["nc"]
    in_maps = [{"x": np.ascontiguousarray(x[i]), "wqkv": wqkv, "bqkv": bqkv,
                "wout": wout, "bout": bout, "wg": wg, "wu": wu, "wd": wd}
               for i in range(8)]
    res = run_bass_kernel_spmd(nc, in_maps, core_ids=list(range(8)))
    return np.stack([res.results[i]["y"] for i in range(8)], axis=0)


# version 13
# speedup vs baseline: 1.9803x; 1.0876x over previous
import contextlib
import numpy as np
import concourse.bass as bass
import concourse.mybir as mybir
from concourse.bass_utils import run_bass_kernel_spmd

F32 = mybir.dt.float32
F16 = mybir.dt.float16
BF16 = mybir.dt.bfloat16
I32 = mybir.dt.int32
AF = mybir.ActivationFunctionType
ALU = mybir.AluOpType
AX = mybir.AxisListType

T = 2048
E = 1024
H = 16
D = 64
I3 = 3072
NT = T // 128
NQ = T // 512
NE = E // 128
NI = I3 // 128
QMAX = 32767.0
EXPOFF = float(np.log(1024.0))


class Sem:
    __slots__ = ("h", "n", "name")

    def __init__(self, h, name):
        self.h = h
        self.n = 0
        self.name = name


class Em:

    CHAINED = ("vector", "scalar", "gpsimd")

    def __init__(self, nc, es):
        self.nc = nc
        self.es = es
        self.steps = []
        self.chains = {}

    def sem(self, name):
        return Sem(self.es.enter_context(self.nc.semaphore(name)), name)

    def set_chains(self, v, a, p):
        self.chains = {"vector": v, "scalar": a, "gpsimd": p}

    def emit(self, eng, fn, waits=(), inc=None, k=1, no_chain=False):
        chain_prev = -1
        if eng in self.CHAINED and not no_chain:
            ch = self.chains[eng]
            if inc is None:
                inc = ch
            assert inc is ch
            chain_prev = ch.n
        incs = ()
        if inc is not None:
            inc.n += k
            incs = ((inc, k),)
        self.steps.append((eng, fn, tuple(waits), incs, chain_prev))

    def dma(self, eng, out, in_, sem, waits=(), **kw):
        self.emit(eng, lambda e, o=out, i=in_, kw=kw: e.dma_start(out=o, in_=i, **kw),
                  waits=waits, inc=sem, k=16, no_chain=True)

    def finalize(self, block):
        final = {}
        for _, _, _, incs, _ in self.steps:
            for s, k in incs:
                final[s.name] = final.get(s.name, 0) + k
        for _, _, waits, _, _ in self.steps:
            for s, v in waits:
                assert v <= final.get(s.name, 0), \
                    f"wait {s.name}>={v} but final={final.get(s.name, 0)}"

        def make(name):
            def body(eng):
                last_w = {}
                chain = self.chains.get(name)
                for ename, fn, waits, incs, chain_prev in self.steps:
                    if ename != name:
                        continue
                    for s, v in waits:
                        if v > 0 and last_w.get(s.name, -1) < v:
                            eng.wait_ge(s.h, v)
                            last_w[s.name] = v
                    inst = fn(eng)
                    if chain_prev > 0:
                        inst._wait_ge(chain.h, chain_prev)
                    for s, k in incs:
                        inst.then_inc(s.h, k)
            return body
        block.sync(make("sync"))
        block.scalar(make("scalar"))
        block.vector(make("vector"))
        block.tensor(make("tensor"))
        block.gpsimd(make("gpsimd"))


def build(dbg=False):
    nc = bass.Bass("TRN2", target_bir_lowering=False,
                   dynamic_dma_scratch_size=2048)

    x_d = nc.dram_tensor("x", [T, E], F32, kind="ExternalInput").ap()
    wqkv_d = nc.dram_tensor("wqkv", [E, I3], F32, kind="ExternalInput").ap()
    bqkv_d = nc.dram_tensor("bqkv", [1, I3], F32, kind="ExternalInput").ap()
    wout_d = nc.dram_tensor("wout", [E, E], F32, kind="ExternalInput").ap()
    bout_d = nc.dram_tensor("bout", [1, E], F32, kind="ExternalInput").ap()
    wg_d = nc.dram_tensor("wg", [I3, E], F32, kind="ExternalInput").ap()
    wu_d = nc.dram_tensor("wu", [I3, E], F32, kind="ExternalInput").ap()
    wd_d = nc.dram_tensor("wd", [E, I3], F32, kind="ExternalInput").ap()
    y_d = nc.dram_tensor("y", [T, E], F32, kind="ExternalOutput").ap()

    def dscr(name, shape, dt):
        if dbg:
            return nc.dram_tensor(name, shape, dt, kind="ExternalOutput").ap()
        return nc.dram_tensor(name, shape, dt).ap()

    xn_h = dscr("xn_h", [T, E], F16)
    v_h = dscr("v_h", [E, T], BF16)
    wqg_h = dscr("wqg_h", [I3, E], F16)
    wqu_h = dscr("wqu_h", [I3, E], F16)
    wqd_h = dscr("wqd_h", [E, I3], F16)
    spg_h = dscr("spg_h", [I3, T], F16)
    spu_h = dscr("spu_h", [I3, T], F16)
    spg2_h = dscr("spg2_h", [I3, T], F16)
    spo_h = dscr("spo_h", [I3, T], F16)
    spq_h = dscr("spq_h", [I3, T], F16)
    cc_i = [nc.dram_tensor(f"cci{i}", [1, 2], F32).ap() for i in range(4)]
    cc_o = [nc.dram_tensor(f"cco{i}", [1, 2], F32).ap() for i in range(4)]
    id128_d = nc.inline_tensor(np.eye(128, dtype=np.float32), "id128d").ap()
    xnf_dbg = dscr("xnf_dbg", [128, 16384], F16) if dbg else None

    es = contextlib.ExitStack()
    with es:
        sb = lambda n, s, d: es.enter_context(nc.sbuf_tensor(n, s, d))
        arA = sb("arA", [128, 16384], F16)
        arB = sb("arB", [128, 16384], F16)
        arQ = sb("arQ", [128, 16384], F16)
        arK = sb("arK", [128, 16384], F16)
        arV = sb("arV", [128, 16640], F16)
        arW = sb("arW", [128, 11392], F16)
        st = sb("st", [128, 32], F32)
        nbE = sb("nbE", [128, 8], F32)
        nbO = sb("nbO", [128, 8], F32)
        bcol = sb("bcol", [128, 24], F32)
        bq8 = sb("bq8", [128, 24], F32)
        ob16 = sb("ob16", [1, E], F16)
        id128 = sb("id128", [128, 128], F32)
        ones32 = sb("ones32", [1, 128], F32)
        ones16 = sb("ones16", [1, 128], F16)
        sel2 = sb("sel2", [128, 2], F32)
        selE = sb("selE", [2, 128], F32)
        selO = sb("selO", [2, 128], F32)
        qn2 = sb("qn2", [2, 8], F32)
        kn2 = sb("kn2", [2, 8], F32)
        nbsrc = sb("nbsrc", [2, 8], F32)
        mcol = sb("mcol", [128, 8], F32)
        cst = sb("cst", [128, 2], F32)
        vland = sb("vland", [128, 256], F16)
        nc.const_aps.register(F32, 1e-5, cst[:, 0:1])

        COL = {k: i for i, k in enumerate(
            ["m", "var", "rstd", "t0", "t1", "t2",
             "s1", "i1", "s2", "i2", "s3", "i3", "hs3", "q3",
             "s4", "i4", "s42", "s5", "i5"])}

        def stc(name, p0=0, p1=128):
            c = COL[name]
            return st[p0:p1, c:c + 1]

        MC = {"xn": 0, "g": 1, "u": 2, "g2": 3, "o": 4, "g2b": 5, "ob": 6}

        def mc(name):
            return mcol[:, MC[name]:MC[name] + 1]

        pb = [es.enter_context(nc.psum_tensor(f"pb{i}", [128, 512], F32))
              for i in range(8)]

        block = es.enter_context(nc.Block())
        em = Em(nc, es)
        sv = em.sem("sv")
        sa = em.sem("sa")
        spl = em.sem("spl")
        stn = em.sem("stn")
        em.set_chains(sv, sa, spl)
        dxa = [em.sem("dxa0"), em.sem("dxa1")]
        dxn = em.sem("dxn")
        dxf = em.sem("dxf")
        dwl = [em.sem("dwl0"), em.sem("dwl1")]
        dlq = [em.sem("dlq0"), em.sem("dlq1")]
        dqw = em.sem("dqw")
        dgt = [em.sem("dgt0"), em.sem("dgt1")]
        dvw = em.sem("dvw")
        dvr = em.sem("dvr")
        dsp = em.sem("dsp")
        dgr = [em.sem("dgr0"), em.sem("dgr1")]
        dur = [em.sem("dur0"), em.sem("dur1")]
        dcc = em.sem("dcc")
        scc = em.sem("scc")
        doq = [em.sem("doq0"), em.sem("doq1")]
        dwt = em.sem("dwt")
        dxr = [em.sem("dxr0"), em.sem("dxr1")]
        dy = em.sem("dy")
        dmi = em.sem("dmi")

        V, A, P, TN, SY = "vector", "scalar", "gpsimd", "tensor", "sync"

        f32c = lambda ar, c0, n: ar[:, c0:c0 + 2 * n].bitcast(F32)
        i32c = lambda ar, c0, n: ar[:, c0:c0 + 2 * n].bitcast(I32)
        bf16c = lambda ar, c0, n: ar[:, c0:c0 + n].bitcast(BF16)

        xn_f = arA
        sq_f32 = f32c(arA, 10240, 2048)
        pt_bf = [bf16c(arA, 14336 + i * 512, 512) for i in range(4)]
        m_gin = [arA[:, 0:512], arA[:, 512:1024]]
        m_tf = [f32c(arA, 1024, 512), f32c(arA, 2048, 512)]
        m_i32 = [i32c(arA, 3072, 512), i32c(arA, 4096, 512)]
        m_gf = [f32c(arA, 5120, 512), f32c(arA, 6144, 512)]
        m_th = [f32c(arA, 7168, 512), f32c(arA, 8192, 512)]
        m_out = [arA[:, 9216:9728], arA[:, 9728:10240]]
        m_uf = [f32c(arA, 10240, 512), f32c(arA, 11264, 512)]
        m_ui32 = [i32c(arA, 12288, 512), i32c(arA, 13312, 512)]
        m_uin = [arA[:, 14336:14848], arA[:, 14848:15360]]

        wst = [f32c(arB, 0, 1024), f32c(arB, 2048, 1024)]
        wh = [arB[:, 4096:5120], arB[:, 5120:6144]]
        vst = [arB[:, 6144 + i * 512:6656 + i * 512].bitcast(BF16)
               for i in range(4)]
        xq_f = arB
        O_f = arB

        Q_f = arQ
        wdTh = arQ[:, 0:12288]
        wouth = arQ[:, 12288:16384]

        K_f = arK
        oqt = [arK[:, 0:3072], arK[:, 3072:6144]]
        xrr = [f32c(arK, 6144, 512), f32c(arK, 7168, 512)]
        yt = [f32c(arK, 8192, 512), f32c(arK, 9216, 512)]

        v_tok = arV

        lw_f32 = f32c(arW, 0, 1024)
        wgT = [arW[:, 2048:3072], arW[:, 3072:4096]]
        lw_tf = f32c(arW, 4096, 1024)
        lw_i32 = i32c(arW, 4096, 1024)
        lw_int16 = arW[:, 6144:7168]
        lw_q = [arW[:, 7168:8192], arW[:, 8192:9216]]
        lw_s = f32c(arW, 9216, 64)
        wo_st = f32c(arW, 0, 512)
        xqf32_t = f32c(arW, 9344, 512)
        xqi32_t = i32c(arW, 10368, 512)
        rbb_t = xqf32_t
        rb = xqf32_t[0:1, :]
        brow = lw_f32[0:24, 0:128]
        obf = lw_tf[0:1, :]
        xa_t = f32c(arQ, 0, 2048)
        xsq_t = f32c(arQ, 4096, 1024)
        xnt_t = arQ[:, 6144:8192]

        em.emit(V, lambda e: e.memset(cst[:, 0:1], 1e-5))
        em.emit(V, lambda e: e.memset(ones32[:, :], 1.0))
        em.emit(V, lambda e: e.memset(ones16[:, :], 1.0))
        em.emit(V, lambda e: e.memset(st[:, :], 0.0))
        em.emit(V, lambda e: e.memset(mcol[:, :], 0.0))
        em.emit(V, lambda e: e.memset(qn2[:, :], 0.0))
        em.emit(V, lambda e: e.memset(kn2[:, :], 0.0))
        em.emit(V, lambda e: e.memset(sel2[:, :], 0.0))
        em.emit(V, lambda e: e.memset(sel2[0:64, 0:1], 1.0))
        em.emit(V, lambda e: e.memset(sel2[64:128, 1:2], 1.0))

        def vtok_ones(e):
            ap = v_tok[:, :].bitcast(BF16).rearrange("p (n c) -> p n c", c=65)
            return e.memset(ap[:, :, 64:65], 1.0)
        em.emit(V, vtok_ones)
        em.emit(V, lambda e: e.memset(selE[:, :], 1.0))
        em.emit(V, lambda e: e.memset(selO[:, :], 1.0))
        SELV = sv.n
        em.emit(P, lambda e: e.affine_select(
            selE[:, :], selE[:, :], [[1, 128]], ALU.is_ge, 0.0,
            base=0, channel_multiplier=-128), waits=[(sv, SELV)])
        em.emit(P, lambda e: e.affine_select(
            selO[:, :], selO[:, :], [[1, 128]], ALU.is_ge, 0.0,
            base=-128, channel_multiplier=128))
        INIT_V = sv.n
        INIT_P = spl.n
        em.dma(A, id128[:, :], id128_d, dmi)
        em.dma(A, brow[:, :], bqkv_d.rearrange("a (c p) -> (a c) p", p=128), dmi)
        em.dma(A, obf[:, :], bout_d, dmi)
        INIT_D = dmi.n
        em.emit(TN, lambda e: e.transpose(pb[7][:, 0:24], brow[:, :],
                                          id128[0:24, 0:24]),
                waits=[(dmi, INIT_D), (sv, INIT_V)], inc=stn)
        BT = stn.n
        em.emit(A, lambda e: e.activation(bcol[:, :], pb[7][:, 0:24],
                                          AF.Identity), waits=[(stn, BT)])
        em.emit(A, lambda e: e.activation(bq8[:, :], pb[7][:, 0:24],
                                          AF.Identity, scale=0.125))
        BC_A = sa.n
        em.emit(A, lambda e: e.activation(ob16[:, :], obf[:, :], AF.Identity),
                waits=[(dmi, INIT_D)])
        OB_A = sa.n

        xn_done_v = []
        xn_done_a = []
        for it in range(NT):
            bi = it % 2
            xt = xa_t[:, bi * 1024:(bi + 1) * 1024]
            xnt = xnt_t[:, bi * 1024:(bi + 1) * 1024]
            w = []
            if it >= 2:
                w.append((sa, xn_done_a[it - 2]))
            em.dma(A, xt, x_d[it * 128:(it + 1) * 128, :], dxa[bi], waits=w)
            DX = dxa[bi].n
            wv = [(dxa[bi], DX)]
            em.emit(A, lambda e, xt=xt: e.activation(xsq_t[:, :], xt, AF.Square),
                    waits=wv + ([(sv, xn_done_v[it - 1])] if it >= 1 else []))
            SQ = sa.n
            em.emit(V, lambda e, xt=xt: e.tensor_reduce(
                stc("m"), xt, AX.X, ALU.add), waits=[(dxa[bi], DX)])
            em.emit(V, lambda e: e.tensor_scalar_mul(stc("m"), stc("m"), 1.0 / E))
            em.emit(V, lambda e: e.tensor_reduce(
                stc("var"), xsq_t[:, :], AX.X, ALU.add), waits=[(sa, SQ)])
            em.emit(V, lambda e: e.tensor_scalar_mul(stc("var"), stc("var"),
                                                     1.0 / E))
            em.emit(V, lambda e: e.scalar_tensor_tensor(
                stc("t0"), stc("m"), stc("m"), stc("var"),
                ALU.mult, ALU.subtract))
            VW = sv.n
            em.emit(A, lambda e: e.activation(
                stc("t1"), stc("t0"), AF.Sqrt, bias=1e-5, scale=-1.0),
                waits=[(sv, VW)])
            SD = sa.n
            em.emit(V, lambda e: e.reciprocal(stc("rstd"), stc("t1")),
                    waits=[(sa, SD)])
            em.emit(V, lambda e, xt=xt: e.tensor_scalar(
                xsq_t[:, :], xt, stc("m"), None, ALU.subtract))
            em.emit(V, lambda e: e.tensor_scalar(
                xsq_t[:, :], xsq_t[:, :], stc("rstd"), None, ALU.mult))
            XN = sv.n
            em.emit(V, lambda e: e.tensor_reduce(
                stc("t1"), xsq_t[:, :], AX.X, ALU.max,
                apply_absolute_value=True))
            em.emit(V, lambda e: e.tensor_tensor(
                mc("xn"), mc("xn"), stc("t1"), ALU.max))
            xn_done_v.append(sv.n)
            em.emit(A, lambda e, xnt=xnt: e.activation(
                xnt, xsq_t[:, :], AF.Identity),
                waits=[(sv, XN)] + ([(dxn, 16 * (it - 1))] if it >= 2 else []))
            xn_done_a.append(sa.n)
            em.dma(SY, xn_h[it * 128:(it + 1) * 128, :], xnt, dxn,
                   waits=[(sa, xn_done_a[-1])])
            for c in range(NE):
                em.dma(SY, xn_f[:, c * T + it * 128:c * T + (it + 1) * 128],
                       xn_h[it * 128:(it + 1) * 128, c * 128:(c + 1) * 128],
                       dxf, waits=[(dxn, dxn.n)], transpose=True)
        LN_V = sv.n
        XNF_D = dxf.n

        def fold_scalar(src_col, dst, waits):
            em.emit(TN, lambda e: e.transpose(
                pb[7][0:1, 0:128], src_col, id128[:, :]),
                waits=list(waits) + [(dmi, INIT_D), (sv, sv.n), (sa, sa.n)],
                inc=stn)
            TW = stn.n
            em.emit(V, lambda e, dst=dst: e.tensor_reduce(
                stc(dst, 0, 1), pb[7][0:1, 0:128], AX.X, ALU.max),
                waits=[(stn, TW)])
            return sv.n

        MX_V = fold_scalar(mc("xn"), "t0", [(sv, LN_V)])

        def collective(ci, waits):
            em.dma(P, cc_i[ci], st[0:1, COL["t0"]:COL["t0"] + 2], dcc,
                   waits=waits)
            em.emit(P, lambda e, ci=ci: e.collective_compute(
                "AllReduce", ALU.max, [[0, 1, 2, 3, 4, 5, 6, 7]],
                [cc_i[ci].opt()], [cc_o[ci].opt()]),
                waits=[(dcc, dcc.n)], inc=scc, no_chain=True)
            return scc.n

        CC1 = collective(0, [(sv, MX_V)])
        em.dma(P, st[0:1, COL["t0"]:COL["t0"] + 2], cc_o[0], dcc,
               waits=[(scc, CC1)])
        CC1D = dcc.n

        def bcast_scale(srccol, scol, icol, waits, extra=()):
            em.emit(TN, lambda e: e.matmul(
                pb[7][:, 0:1], ones32[0:1, 0:128],
                st[0:1, COL[srccol]:COL[srccol] + 1], start=True, stop=True),
                waits=list(waits) + [(sv, sv.n), (sa, sa.n)], inc=stn)
            TW = stn.n
            em.emit(V, lambda e: e.tensor_scalar(
                stc(scol), pb[7][:, 0:1], 1.0 / QMAX, 1e-12, ALU.mult, ALU.max),
                waits=[(stn, TW)])
            em.emit(V, lambda e: e.reciprocal(stc(icol), stc(scol)))
            for f in extra:
                f()
            return sv.n

        S1_V = bcast_scale("t0", "s1", "i1", [(dcc, CC1D)])

        lqst = {"n": 0, "a": 0, "p": 0, "wr": 0, "q": [0, 0]}

        def lpbq_tile(src_ap, dst_ap):
            k = lqst["n"]
            lqst["n"] += 1
            w = [(dmi, INIT_D)]
            if k == 0:
                w += [(stn, BT)]
            if k >= 1:
                w += [(sv, lqst["p"])]
            em.dma(SY, lw_f32, src_ap, dlq[k % 2], waits=w)
            RD = dlq[k % 2].n
            wv3 = lw_f32.rearrange("p (nb e) -> p nb e", e=32)
            em.emit(V, lambda e, wv3=wv3: e.tensor_reduce(
                lw_s[:, 0:32], wv3, AX.X, ALU.max, apply_absolute_value=True),
                waits=[(dlq[k % 2], RD)])
            em.emit(V, lambda e: e.tensor_scalar(
                lw_s[:, 32:64], lw_s[:, 0:32], 1.0 / 7.0, 1e-12,
                ALU.mult, ALU.max))
            em.emit(V, lambda e: e.reciprocal(lw_s[:, 0:32], lw_s[:, 32:64]))
            inv_b = lw_s[:, 0:32].unsqueeze(2).broadcast_to([128, 32, 32])
            s_b = lw_s[:, 32:64].unsqueeze(2).broadcast_to([128, 32, 32])
            tfv = lw_tf.rearrange("p (nb e) -> p nb e", e=32)
            war_t2 = [(sa, OB_A)] if k == 0 else []
            em.emit(V, lambda e, wv3=wv3, inv_b=inv_b, tfv=tfv: e.tensor_tensor(
                tfv, wv3, inv_b, ALU.mult), waits=war_t2)
            em.emit(V, lambda e: e.tensor_copy(lw_i32[:, :], lw_tf))
            em.emit(V, lambda e: e.tensor_copy(lw_int16[:, :], lw_i32[:, :]))
            lqst["p"] = sv.n
            qs = lw_q[k % 2]
            war_q = [(dqw, lqst["q"][k % 2])]
            em.emit(V, lambda e, qs=qs, s_b=s_b: e.tensor_tensor(
                qs.rearrange("p (nb e) -> p nb e", e=32),
                lw_int16.rearrange("p (nb e) -> p nb e", e=32),
                s_b, ALU.mult), waits=war_q)
            QP = sv.n
            lqst["a"] = sa.n
            em.dma(SY, dst_ap, qs, dqw, waits=[(sv, QP)])
            lqst["wr"] = dqw.n
            lqst["q"][k % 2] = dqw.n

        def lpbq_jobs(w_src, w_dst, nrows, ncols):
            out = []
            for rc in range(nrows // 128):
                for sc in range(ncols // 1024):
                    out.append((w_src[rc * 128:(rc + 1) * 128,
                                      sc * 1024:(sc + 1) * 1024],
                                w_dst[rc * 128:(rc + 1) * 128,
                                      sc * 1024:(sc + 1) * 1024]))
            return out

        side_jobs = (lpbq_jobs(wg_d, wqg_h, I3, E) + lpbq_jobs(wu_d, wqu_h, I3, E)
                     + lpbq_jobs(wd_d, wqd_h, E, I3))
        side_it = iter(side_jobs)

        def pop_side(n=1):
            for _ in range(n):
                job = next(side_it, None)
                if job is not None:
                    lpbq_tile(*job)

        wqkv_r = wqkv_d.rearrange("(c p) f -> p c f", p=128)
        qkv_conv_a = [0, 0]
        whmm = [0, 0]
        evq = [0, 0]
        vwd = [0, 0, 0, 0]
        vst_a = [0, 0, 0, 0]

        for m in range(24):
            b = m % 2
            j, r3 = divmod(m, 3)
            w = [(sa, qkv_conv_a[b])] if m >= 2 else []
            for k in range(NE):
                em.dma(SY, wst[b][:, k * 128:(k + 1) * 128],
                       wqkv_r[:, k, m * 128:(m + 1) * 128], dwl[b],
                       waits=(w if k == 0 else []))
            DW = dwl[b].n
            wconv = [(dwl[b], DW)]
            if m >= 2:
                wconv.append((stn, whmm[b]))
            em.emit(A, lambda e, b=b: e.activation(wh[b][:, :], wst[b][:, :],
                                                   AF.Identity), waits=wconv)
            WH = sa.n
            qkv_conv_a[b] = WH
            whv = wh[b].rearrange("p (c f) -> p c f", c=NE)
            for t in range(NQ):
                bk = (m * NQ + t) % 2
                wl = [(sa, WH), (dxf, XNF_D)]
                if m * NQ + t >= 2:
                    wl.append((sa, evq[bk]))
                for k in range(NE):
                    em.emit(TN, lambda e, bk=bk, k=k, t=t, whv=whv: e.matmul(
                        pb[bk][:, :], whv[:, k, :],
                        xn_f[:, k * T + t * 512:k * T + (t + 1) * 512],
                        start=(k == 0), stop=(k == NE - 1)),
                        waits=(wl if k == 0 else ()),
                        inc=(stn if k == NE - 1 else None))
                MM = stn.n
                if r3 == 0:
                    em.emit(A, lambda e, bk=bk, j=j, t=t, m=m: e.activation(
                        Q_f[0:64, j * T + t * 512:j * T + (t + 1) * 512],
                        pb[bk][0:64, :], AF.Identity,
                        bias=bq8[0:64, m:m + 1], scale=0.125),
                        waits=[(stn, MM), (sa, BC_A)])
                    em.emit(A, lambda e, bk=bk, j=j, t=t, m=m: e.activation(
                        K_f[0:64, j * T + t * 512:j * T + (t + 1) * 512],
                        pb[bk][64:128, :], AF.Identity,
                        bias=bcol[64:128, m:m + 1]))
                elif r3 == 1:
                    em.emit(A, lambda e, bk=bk, t=t, m=m: e.activation(
                        vst[t][0:64, :], pb[bk][0:64, :], AF.Identity,
                        bias=bcol[0:64, m:m + 1]),
                        waits=[(stn, MM), (sa, BC_A), (dvw, vwd[t])])
                    em.emit(A, lambda e, bk=bk, j=j, t=t, m=m: e.activation(
                        Q_f[64:128, j * T + t * 512:j * T + (t + 1) * 512],
                        pb[bk][64:128, :], AF.Identity,
                        bias=bq8[64:128, m:m + 1], scale=0.125))
                else:
                    em.emit(A, lambda e, bk=bk, j=j, t=t, m=m: e.activation(
                        K_f[64:128, j * T + t * 512:j * T + (t + 1) * 512],
                        pb[bk][0:64, :], AF.Identity,
                        bias=bcol[0:64, m:m + 1]),
                        waits=[(stn, MM), (sa, BC_A)])
                    em.emit(A, lambda e, bk=bk, t=t, m=m: e.activation(
                        vst[t][64:128, :], pb[bk][64:128, :], AF.Identity,
                        bias=bcol[64:128, m:m + 1]), waits=[(dvw, vwd[t])])
                    vst_a[t] = sa.n
                    em.dma(SY, v_h[j * 128:(j + 1) * 128,
                                   t * 512:(t + 1) * 512], vst[t], dvw,
                           waits=[(sa, vst_a[t])])
                    vwd[t] = dvw.n
                evq[bk] = sa.n
            whmm[b] = stn.n
            if m % 2 == 1:
                pop_side(2)
        QKV_MM = stn.n
        QKV_A = sa.n
        VW_D = dvw.n
        if dbg:
            em.dma(SY, xnf_dbg, arA[:, :], dmi, waits=[(stn, QKV_MM)])

        vcp = [0, 0]
        for idx in range(NT * NE):
            kt, j = divmod(idx, NE)
            r = idx % 2
            land = vland[:, r * 128:(r + 1) * 128].bitcast(BF16)
            w = [(dvw, VW_D)]
            if idx >= 2:
                w.append((spl, vcp[r]))
            em.dma(SY, land, v_h[j * 128:(j + 1) * 128,
                                 kt * 128:(kt + 1) * 128],
                   dvr, waits=w, transpose=True)
            DVT = dvr.n
            s0 = (kt * H + 2 * j) * 65
            s1 = (kt * H + 2 * j + 1) * 65
            em.emit(P, lambda e, land=land, s0=s0: e.tensor_copy(
                v_tok[:, s0:s0 + 64].bitcast(BF16), land[:, 0:64]),
                waits=[(dvr, DVT), (sv, INIT_V)])
            em.emit(P, lambda e, land=land, s1=s1: e.tensor_copy(
                v_tok[:, s1:s1 + 64].bitcast(BF16), land[:, 64:128]))
            vcp[r] = spl.n
        VT_P = spl.n
        VT_D = dvr.n

        xq_a = [0]
        first_xq = True
        for c in range(NE):
            for t in range(NQ):
                wv = [(sv, S1_V)]
                if first_xq:
                    wv += [(stn, QKV_MM), (sa, QKV_A), (dvw, VW_D)]
                    first_xq = False
                if xq_a[0]:
                    wv.append((sa, xq_a[0]))
                em.emit(V, lambda e, c=c, t=t: e.tensor_scalar(
                    xqf32_t, xn_f[:, c * T + t * 512:c * T + (t + 1) * 512],
                    stc("i1"), -32768.0, ALU.mult, ALU.max), waits=wv)
                em.emit(V, lambda e: e.tensor_scalar(
                    xqi32_t, xqf32_t, 32767.0, None, ALU.min))
                I32V = sv.n
                em.emit(A, lambda e, c=c, t=t: e.activation(
                    xq_f[:, c * T + t * 512:c * T + (t + 1) * 512],
                    xqi32_t, AF.Identity), waits=[(sv, I32V)])
                xq_a[0] = sa.n
        XQ_A = sa.n
        XQ_V = sv.n

        spill_cnt = {"g": 0, "u": 0}
        gumax_v = {"g": 0, "u": 0}
        gu_ev = {"ev0": 0, "ev1": 0}
        gumm = [0, 0]
        gusp = [0, 0]

        def gu_load(wq_h, i, wq_wr_need):
            b = i % 2
            w = [(dqw, wq_wr_need)]
            if i >= 2:
                w.append((stn, gumm[b]))
            for k in range(NE):
                em.dma(SY, wgT[b][:, k * 128:(k + 1) * 128],
                       wq_h[i * 128:(i + 1) * 128, k * 128:(k + 1) * 128],
                       dgt[b], waits=(w if k == 0 else []), transpose=True)
            return dgt[b].n

        def gu_stream(wq_h, sp_h, mx, wq_wr_need):
            gl = [0, 0]
            gl[0] = gu_load(wq_h, 0, wq_wr_need)
            for i in range(NI):
                b = i % 2
                if i + 1 < NI:
                    gl[(i + 1) % 2] = gu_load(wq_h, i + 1, wq_wr_need)
                DG = gl[b]
                for t in range(NQ):
                    bk = (i * NQ + t) % 2
                    wl = [(dgt[b], DG), (sa, XQ_A)]
                    if gu_ev[f"ev{bk}"]:
                        wl.append((sa, gu_ev[f"ev{bk}"]))
                    for k in range(NE):
                        em.emit(TN, lambda e, bk=bk, b=b, k=k, t=t: e.matmul(
                            pb[bk][:, :], wgT[b][:, k * 128:(k + 1) * 128],
                            xq_f[:, k * T + t * 512:k * T + (t + 1) * 512],
                            start=(k == 0), stop=(k == NE - 1)),
                            waits=(wl if k == 0 else ()),
                            inc=(stn if k == NE - 1 else None))
                    MM = stn.n
                    sl = m_out[bk]
                    em.emit(A, lambda e, sl=sl, bk=bk: e.activation(
                        sl, pb[bk][:, :], AF.Identity, scale=stc("s1")),
                        waits=[(stn, MM), (dsp, gusp[bk])])
                    EVA = sa.n
                    gu_ev[f"ev{bk}"] = EVA
                    em.emit(V, lambda e, sl=sl: e.tensor_reduce(
                        stc("t1"), sl, AX.X, ALU.max,
                        apply_absolute_value=True), waits=[(sa, EVA)])
                    em.emit(V, lambda e, mx=mx: e.tensor_tensor(
                        mc(mx), mc(mx), stc("t1"), ALU.max))
                    gumax_v[mx] = sv.n
                    em.dma(SY, sp_h[i * 128:(i + 1) * 128,
                                    t * 512:(t + 1) * 512], sl, dsp,
                           waits=[(sv, gumax_v[mx])])
                    gusp[bk] = dsp.n
                    spill_cnt[mx] += 1
                gumm[b] = stn.n
                if i % 2 == 1:
                    pop_side(1)

        pop_side(26)
        gu_stream(wqg_h, spg_h, "g", 16 * 24)
        pop_side(12)
        gu_stream(wqu_h, spu_h, "u", 16 * 48)
        pop_side(len(side_jobs))
        GU_MM = stn.n
        GU_SPILLS = 16 * (spill_cnt["g"] + spill_cnt["u"])
        LPBQ_WR = lqst["wr"]
        assert lqst["n"] == len(side_jobs)

        G_V = fold_scalar(mc("g"), "t0", [(sv, gumax_v["g"])])
        U_V = fold_scalar(mc("u"), "t1", [(sv, gumax_v["u"])])
        CC2 = collective(1, [(sv, U_V)])
        em.dma(P, st[0:1, COL["t0"]:COL["t0"] + 2], cc_o[1], dcc,
               waits=[(scc, CC2)])
        CC2D = dcc.n

        def s3_extra():
            em.emit(V, lambda e: e.tensor_scalar_mul(stc("hs3"), stc("s3"), 0.5))
            em.emit(V, lambda e: e.tensor_scalar_mul(stc("q3"), stc("s3"),
                                                     1.0 / 65536.0))
        S3_V = bcast_scale("t0", "s3", "i3", [(dcc, CC2D)], extra=(s3_extra,))
        S2_V = bcast_scale("t1", "s2", "i2", [(dcc, CC2D)])

        def norms(src, acc):
            for hc in range(NE):
                em.emit(A, lambda e, hc=hc, src=src: e.activation(
                    sq_f32[:, :], src[:, hc * T:(hc + 1) * T], AF.Square),
                    waits=[(stn, stn.n), (sv, XQ_V)])
                SQA = sa.n
                for t in range(NQ):
                    em.emit(TN, lambda e, t=t: e.matmul(
                        pb[6][0:2, :], sel2[:, :],
                        sq_f32[:, t * 512:(t + 1) * 512],
                        start=True, stop=True),
                        waits=[(sa, SQA), (sv, sv.n)], inc=stn)
                    TW = stn.n
                    em.emit(V, lambda e: e.tensor_reduce(
                        stc("t1", 0, 2), pb[6][0:2, :], AX.X, ALU.max),
                        waits=[(stn, TW)])
                    em.emit(V, lambda e, hc=hc, acc=acc: e.tensor_tensor(
                        acc[0:2, hc:hc + 1], acc[0:2, hc:hc + 1],
                        stc("t1", 0, 2), ALU.max))
            return sv.n

        QN_V = norms(Q_f, qn2)
        KN_V = norms(K_f, kn2)
        em.emit(V, lambda e: e.tensor_tensor(nbsrc[0:2, :], qn2[0:2, :],
                                             kn2[0:2, :], ALU.mult),
                waits=[(sv, KN_V)])
        NB1 = sv.n
        em.emit(A, lambda e: e.activation(nbsrc[0:2, :], nbsrc[0:2, :],
                                          AF.Sqrt), waits=[(sv, NB1)])
        NB2 = sa.n
        em.emit(V, lambda e: e.tensor_scalar(
            nbsrc[0:2, :], nbsrc[0:2, :], -1.0, EXPOFF, ALU.mult, ALU.add),
            waits=[(sa, NB2)])
        NB3 = sv.n
        em.emit(TN, lambda e: e.matmul(pb[6][:, 0:8], selE[:, :],
                                       nbsrc[0:2, :], start=True, stop=True),
                waits=[(sv, NB3), (spl, INIT_P), (sv, sv.n)], inc=stn)
        NBT1 = stn.n
        em.emit(A, lambda e: e.activation(nbE[:, :], pb[6][:, 0:8],
                                          AF.Identity), waits=[(stn, NBT1)])
        NBA1 = sa.n
        em.emit(TN, lambda e: e.matmul(pb[6][:, 0:8], selO[:, :],
                                       nbsrc[0:2, :], start=True, stop=True),
                waits=[(sa, NBA1)], inc=stn)
        NBT2 = stn.n
        em.emit(A, lambda e: e.activation(nbO[:, :], pb[6][:, 0:8],
                                          AF.Identity), waits=[(stn, NBT2)])
        NEGB_A = sa.n

        m1 = {"n": 0, "ec": [0, 0], "sp": [0, 0], "wr": 0, "th": [0, 0],
              "pend": [None, None]}

        def m1_pre(jn):
            i, t = divmod(jn, NQ)
            r = jn % 2
            w = [(dsp, GU_SPILLS)]
            if m1["ec"][r]:
                w.append((sv, m1["ec"][r]))
            if m1["sp"][r]:
                w.append((dsp, m1["sp"][r]))
            em.dma(SY, m_gin[r], spg_h[i * 128:(i + 1) * 128,
                                       t * 512:(t + 1) * 512], dgr[r], waits=w)
            RD = dgr[r].n
            em.emit(V, lambda e, r=r: e.tensor_scalar(
                m_tf[r], m_gin[r], stc("i3"), -32768.0, ALU.mult, ALU.max),
                waits=[(dgr[r], RD), (sv, S3_V)])
            em.emit(V, lambda e, r=r: e.tensor_scalar(
                m_i32[r], m_tf[r], 32767.0, None, ALU.min))
            em.emit(V, lambda e, r=r: e.tensor_copy(m_gf[r], m_i32[r]))
            GF = sv.n
            em.emit(A, lambda e, r=r: e.activation(m_th[r], m_gf[r], AF.Tanh,
                                                   scale=stc("hs3")),
                    waits=[(sv, GF)])
            m1["th"][r] = sa.n

        def m1_post(jn):
            i, t = divmod(jn, NQ)
            r = jn % 2
            em.emit(V, lambda e, r=r: e.tensor_scalar(
                m_tf[r], m_th[r], 32768.0, 32768.0, ALU.mult, ALU.add),
                waits=[(sa, m1["th"][r])])
            em.emit(V, lambda e, r=r: e.tensor_scalar(
                m_i32[r], m_tf[r], 65535.0, None, ALU.min))
            em.emit(V, lambda e, r=r: e.tensor_copy(m_th[r], m_i32[r]))
            em.emit(V, lambda e, r=r: e.tensor_tensor(
                m_tf[r], m_gf[r], m_th[r], ALU.mult))
            em.emit(V, lambda e, r=r: e.tensor_scalar(
                m_out[r], m_tf[r], stc("q3"), None, ALU.mult))
            em.emit(V, lambda e, r=r: e.tensor_reduce(
                stc("t1"), m_out[r], AX.X, ALU.max, apply_absolute_value=True))
            em.emit(V, lambda e: e.tensor_tensor(
                mc("g2"), mc("g2"), stc("t1"), ALU.max))
            m1["ec"][r] = sv.n
            em.dma(SY, spg2_h[i * 128:(i + 1) * 128, t * 512:(t + 1) * 512],
                   m_out[r], dsp, waits=[(sv, m1["ec"][r])])
            m1["sp"][r] = dsp.n
            m1["wr"] = dsp.n

        def m1_tile():
            jn = m1["n"]
            m1["n"] += 1
            m1_pre(jn)
            if jn >= 1:
                m1_post(jn - 1)
            if jn == NI * NQ - 1:
                m1_post(jn)

        ps_free = {0: 0, 1: 0, 2: 0, 3: 0}
        pt_free = [0, 0, 0, 0]
        o_ev = {4: 0, 5: 0}
        ep_a = [NEGB_A]
        si_ctr = [0]

        def attn_pair(hc):
            def scores_exp(qc, kt):
                diag = kt >= 4 * qc
                info = []
                for par, (sbk0, nb) in enumerate([(2, nbE), (3, nbO)]):
                    sbk = sbk0 if kt % 2 == 0 else par
                    si = si_ctr[0]
                    si_ctr[0] += 1
                    p0 = par * 64
                    w = [(sa, NEGB_A)]
                    if ps_free[sbk]:
                        w.append((sa, ps_free[sbk]))
                    em.emit(TN, lambda e, sbk=sbk, p0=p0, hc=hc, kt=kt,
                            qc=qc: e.matmul(
                        pb[sbk][:, :],
                        K_f[p0:p0 + 64,
                            hc * T + kt * 128:hc * T + (kt + 1) * 128],
                        Q_f[p0:p0 + 64,
                            hc * T + qc * 512:hc * T + (qc + 1) * 512],
                        start=True, stop=True), waits=w, inc=stn)
                    MMW = stn.n
                    slot = si % 4
                    we = [(stn, MMW)]
                    if pt_free[slot]:
                        we.append((stn, pt_free[slot]))
                    em.emit(A, lambda e, sbk=sbk, slot=slot, nb=nb, hc=hc:
                            e.activation(pt_bf[slot], pb[sbk][:, :],
                                         AF.Exp, bias=nb[:, hc:hc + 1]),
                            waits=we)
                    EXA = sa.n
                    ps_free[sbk] = EXA
                    if diag:
                        s = kt - 4 * qc
                        em.emit(P, lambda e, slot=slot, s=s:
                                e.affine_select(
                            pt_bf[slot], pt_bf[slot], [[1, 512]],
                            ALU.is_ge, 0.0, base=-128 * s,
                            channel_multiplier=-1), waits=[(sa, EXA)])
                        rdy = (spl, spl.n)
                    else:
                        rdy = (sa, EXA)
                    info.append((slot, rdy))
                return info

            def pv(qc, kt, nkt, info):
                for par, obk in ((0, 4), (1, 5)):
                    slot, rdy = info[par]
                    h = 2 * hc + par
                    wpv = [(spl, VT_P), rdy]
                    if kt == 0 and o_ev[obk]:
                        wpv.append((sv, o_ev[obk]))
                    em.emit(TN, lambda e, obk=obk, kt=kt, h=h, slot=slot,
                            nkt=nkt: e.matmul(
                        pb[obk][0:65, :],
                        v_tok[:, (kt * H + h) * 65:(kt * H + h + 1) * 65]
                        .bitcast(BF16),
                        pt_bf[slot], start=(kt == 0), stop=(kt == nkt - 1)),
                        waits=wpv, inc=stn)
                    pt_free[slot] = stn.n

            for qc in range(NQ):
                nkt = 4 * qc + 4
                pend = scores_exp(qc, 0)
                for kt in range(nkt):
                    nxt = scores_exp(qc, kt + 1) if kt + 1 < nkt else None
                    pv(qc, kt, nkt, pend)
                    pend = nxt
                for par, obk in ((0, 4), (1, 5)):
                    PVM = stn.n
                    em.emit(V, lambda e, obk=obk: e.tensor_scalar(
                        rb, pb[obk][64:65, :], 1e-12, None, ALU.add),
                        waits=[(stn, PVM)])
                    em.emit(V, lambda e: e.reciprocal(rb, rb))
                    RC = sv.n
                    em.emit(TN, lambda e: e.matmul(
                        pb[6][0:64, :], ones32[0:1, 0:64], rb,
                        start=True, stop=True),
                        waits=[(sv, RC), (sa, ep_a[0])], inc=stn)
                    BCW = stn.n
                    em.emit(A, lambda e: e.activation(
                        rbb_t[0:64, :], pb[6][0:64, :], AF.Identity),
                        waits=[(stn, BCW)])
                    ep_a[0] = sa.n
                    em.emit(V, lambda e, obk=obk, par=par, hc=hc, qc=qc:
                            e.tensor_tensor(
                        O_f[par * 64:par * 64 + 64,
                            hc * T + qc * 512:hc * T + (qc + 1) * 512],
                        pb[obk][0:64, :], rbb_t[0:64, :], ALU.mult),
                        waits=[(sa, ep_a[0])])
                    o_ev[obk] = sv.n

        for hc in range(NE):
            attn_pair(hc)
            if hc >= 2:
                for _ in range(16):
                    if m1["n"] < NI * NQ:
                        m1_tile()
        while m1["n"] < NI * NQ:
            m1_tile()
        ATT_MM = stn.n
        O_V = sv.n
        M1_WR = m1["wr"]

        G2_V = fold_scalar(mc("g2"), "t0",
                           [(sv, m1["ec"][0]), (sv, m1["ec"][1])])
        CC3 = collective(2, [(sv, G2_V)])
        em.dma(P, st[0:1, COL["t0"]:COL["t0"] + 2], cc_o[2], dcc,
               waits=[(scc, CC3)])
        CC3D = dcc.n

        def s4_extra():
            em.emit(V, lambda e: e.tensor_tensor(
                stc("s42"), stc("s4"), stc("s2"), ALU.mult))
        S4_V = bcast_scale("t0", "s4", "i4", [(dcc, CC3D), (sv, S2_V)],
                           extra=(s4_extra,))

        m2 = {"ec": [0, 0], "sp": [0, 0]}
        m2_pend = [None, None]
        for jn in range(NI * NQ):
            i, t = divmod(jn, NQ)
            r = jn % 2
            w = [(dsp, M1_WR)]
            if m2["ec"][r]:
                w.append((sv, m2["ec"][r]))
            if m2["sp"][r]:
                w.append((dsp, m2["sp"][r]))
            em.dma(SY, m_gin[r], spg2_h[i * 128:(i + 1) * 128,
                                        t * 512:(t + 1) * 512], dgr[r],
                   waits=w)
            RDg = dgr[r].n
            em.dma(SY, m_uin[r], spu_h[i * 128:(i + 1) * 128,
                                       t * 512:(t + 1) * 512], dur[r],
                   waits=([(sv, O_V)] if jn < 2 else []))
            RDu = dur[r].n
            if m2_pend[r] is not None:
                m2_pend[r]()
                m2_pend[r] = None
            em.emit(V, lambda e, r=r: e.tensor_scalar(
                m_tf[r], m_gin[r], stc("i4"), -32768.0, ALU.mult, ALU.max),
                waits=[(dgr[r], RDg), (sv, S4_V)])
            em.emit(V, lambda e, r=r: e.tensor_scalar(
                m_i32[r], m_tf[r], 32767.0, None, ALU.min))
            em.emit(V, lambda e, r=r: e.tensor_copy(m_gf[r], m_i32[r]))
            em.emit(V, lambda e, r=r: e.tensor_scalar(
                m_tf[r], m_uin[r], stc("i2"), -32768.0, ALU.mult, ALU.max),
                waits=[(dur[r], RDu)])
            em.emit(V, lambda e, r=r: e.tensor_scalar(
                m_ui32[r], m_tf[r], 32767.0, None, ALU.min))
            em.emit(V, lambda e, r=r: e.tensor_copy(m_uf[r], m_ui32[r]))
            em.emit(V, lambda e, r=r: e.tensor_tensor(
                m_tf[r], m_gf[r], m_uf[r], ALU.mult))
            em.emit(V, lambda e, r=r: e.tensor_scalar(
                m_out[r], m_tf[r], stc("s42"), None, ALU.mult))
            em.emit(V, lambda e, r=r: e.tensor_reduce(
                stc("t1"), m_out[r], AX.X, ALU.max, apply_absolute_value=True))
            em.emit(V, lambda e: e.tensor_tensor(
                mc("o"), mc("o"), stc("t1"), ALU.max))
            m2["ec"][r] = sv.n
            EC = m2["ec"][r]

            def spill(r=r, i=i, t=t, EC=EC):
                em.dma(SY, spo_h[i * 128:(i + 1) * 128,
                                 t * 512:(t + 1) * 512],
                       m_out[r], dsp, waits=[(sv, EC)])
                m2["sp"][r] = dsp.n
            m2_pend[r] = spill
        for r in range(2):
            if m2_pend[r] is not None:
                m2_pend[r]()
                m2_pend[r] = None
        M2_WR = dsp.n

        O5_V = fold_scalar(mc("o"), "t0",
                           [(sv, m2["ec"][0]), (sv, m2["ec"][1])])
        CC4 = collective(3, [(sv, O5_V)])
        em.dma(P, st[0:1, COL["t0"]:COL["t0"] + 2], cc_o[3], dcc,
               waits=[(scc, CC4)])
        CC4D = dcc.n
        S5_V = bcast_scale("t0", "s5", "i5", [(dcc, CC4D)])

        m3 = {"ec": [0, 0], "sp": [0, 0]}
        m3_pend = [None, None]
        for jn in range(NI * NQ):
            i, t = divmod(jn, NQ)
            r = jn % 2
            w = [(dsp, M2_WR)]
            if m3["ec"][r]:
                w.append((sv, m3["ec"][r]))
            if m3["sp"][r]:
                w.append((dsp, m3["sp"][r]))
            em.dma(SY, m_gin[r], spo_h[i * 128:(i + 1) * 128,
                                       t * 512:(t + 1) * 512], dgr[r], waits=w)
            RD = dgr[r].n
            if m3_pend[r] is not None:
                m3_pend[r]()
                m3_pend[r] = None
            em.emit(V, lambda e, r=r: e.tensor_scalar(
                m_tf[r], m_gin[r], stc("i5"), -32768.0, ALU.mult, ALU.max),
                waits=[(dgr[r], RD), (sv, S5_V)])
            em.emit(V, lambda e, r=r: e.tensor_scalar(
                m_i32[r], m_tf[r], 32767.0, None, ALU.min))
            em.emit(V, lambda e, r=r: e.tensor_copy(m_gf[r], m_i32[r]))
            em.emit(V, lambda e, r=r: e.tensor_scalar(
                m_out[r], m_gf[r], stc("s5"), None, ALU.mult))
            m3["ec"][r] = sv.n
            EC = m3["ec"][r]

            def spill3(r=r, i=i, t=t, EC=EC):
                em.dma(SY, spq_h[i * 128:(i + 1) * 128,
                                 t * 512:(t + 1) * 512],
                       m_out[r], dsp, waits=[(sv, EC)])
                m3["sp"][r] = dsp.n
            m3_pend[r] = spill3
        for r in range(2):
            if m3_pend[r] is not None:
                m3_pend[r]()
                m3_pend[r] = None
        M3_WR = dsp.n

        WH_A = [0]
        p8mm = [0, 0]
        p8y = [0, 0]
        ydma = []
        for eh in range(2):
            for i in range(NI):
                wfirst = ([(stn, ATT_MM), (dqw, LPBQ_WR)]
                          if (i == 0 and eh == 0) else
                          ([(sa, WH_A[0]), (stn, stn.n)]
                           if i == 0 and eh == 1 else []))
                em.dma(SY, wdTh[:, i * 512:(i + 1) * 512],
                       wqd_h[eh * 512:(eh + 1) * 512,
                             i * 128:(i + 1) * 128], dwt,
                       waits=wfirst, transpose=True)
            for k in range(NE):
                wld = [(spl, spl.n), (sv, sv.n)] if (k == 0 and eh == 0) \
                    else ([(sa, WH_A[0]), (stn, stn.n)] if k == 0
                          else [(sa, WH_A[0])])
                em.dma(A, wo_st[:, :],
                       wout_d[k * 128:(k + 1) * 128,
                              eh * 512:(eh + 1) * 512], dwl[0], waits=wld)
                WD = dwl[0].n
                wca = [(dwl[0], WD)]
                if eh == 0 and k == 0:
                    wca.append((stn, ATT_MM))
                em.emit(A, lambda e, k=k: e.activation(
                    wouth[:, k * 512:(k + 1) * 512], wo_st[:, :],
                    AF.Identity), waits=wca)
                WH_A[0] = sa.n
            WDT = dwt.n
            WHA = sa.n

            def p8_load(t):
                r = t % 2
                n8 = eh * NT + t
                w = [(dsp, M3_WR)]
                if n8 == 0:
                    w.append((stn, ATT_MM))
                if n8 >= 2:
                    w.append((stn, p8mm[r]))
                for i in range(NI):
                    em.dma(SY if i % 2 == 0 else A,
                           oqt[r][:, i * 128:(i + 1) * 128],
                           spq_h[i * 128:(i + 1) * 128,
                                 t * 128:(t + 1) * 128], doq[r],
                           waits=(w if i < 2 else []))
                wx = [(sv, p8y[r])] if n8 >= 2 else [(stn, ATT_MM)]
                em.dma(A, xrr[r], x_d[t * 128:(t + 1) * 128,
                                      eh * 512:(eh + 1) * 512], dxr[r],
                       waits=wx)
                return doq[r].n, dxr[r].n

            pend_y = [None]
            lded = {0: p8_load(0)}
            for t in range(NT):
                r = t % 2
                n8 = eh * NT + t
                if t + 1 < NT:
                    lded[t + 1] = p8_load(t + 1)
                DOQ, DXR = lded.pop(t)
                bk = r
                wl = [(sv, O_V), (sa, WHA), (dwt, WDT), (doq[r], DOQ),
                      (sa, OB_A)]
                if n8 >= 2:
                    wl.append((sv, p8y[r]))
                for k in range(NE):
                    em.emit(TN, lambda e, bk=bk, k=k, t=t: e.matmul(
                        pb[bk][:, :],
                        O_f[:, k * T + t * 128:k * T + (t + 1) * 128],
                        wouth[:, k * 512:(k + 1) * 512],
                        start=(k == 0), stop=False),
                        waits=(wl if k == 0 else ()))
                for i in range(NI):
                    em.emit(TN, lambda e, bk=bk, i=i, r=r: e.matmul(
                        pb[bk][:, :], oqt[r][:, i * 128:(i + 1) * 128],
                        wdTh[:, i * 512:(i + 1) * 512],
                        start=False, stop=False))
                em.emit(TN, lambda e, bk=bk, eh=eh: e.matmul(
                    pb[bk][:, :], ones16[0:1, 0:128],
                    ob16[0:1, eh * 512:(eh + 1) * 512],
                    start=False, stop=True), inc=stn)
                MM = stn.n
                p8mm[r] = MM
                wy = [(stn, MM), (dxr[r], DXR)]
                if len(ydma) >= 2:
                    wy.append((dy, ydma[-2]))
                em.emit(V, lambda e, bk=bk, r=r: e.tensor_tensor(
                    yt[r], pb[bk][:, :], xrr[r], ALU.add), waits=wy)
                YV = sv.n
                p8y[r] = YV
                em.dma(SY, y_d[t * 128:(t + 1) * 128,
                               eh * 512:(eh + 1) * 512], yt[r], dy,
                       waits=[(sv, YV)])
                ydma.append(dy.n)

        em.emit(P, lambda e: e.engine_nop(), waits=[(dy, dy.n)], no_chain=True)

        n_inst = len(em.steps)
        em.finalize(block)
    ends = []
    for a in nc.m.functions[0].allocations:
        if not hasattr(a, "memorylocations"):
            continue
        for ml in a.memorylocations:
            nm = getattr(ml, "name", "")
            if nm in ("arA", "arB", "arQ", "arK", "arV", "arW", "st", "nbE",
                      "nbO", "bcol", "bq8", "ob16", "id128", "ones32",
                      "ones16", "sel2", "selE", "selO", "qn2", "kn2",
                      "nbsrc", "mcol", "cst", "vland"):
                ends.append(ml.addr + ml.size() // 128)
    assert max(ends) <= 196608, f"SBUF overflow: {max(ends)}"
    if dbg:
        print(f"[build] {n_inst} emitted steps")
    return nc


_NC_CACHE = {}


def kernel(L0_attn_x, L0_attn_qkv_w, L0_attn_qkv_b, L0_attn_out_w,
           L0_attn_out_b, L0_ffn_w_gate, L0_ffn_w_up, L0_ffn_w_down):
    x = np.ascontiguousarray(L0_attn_x, dtype=np.float32)
    wqkv = np.ascontiguousarray(L0_attn_qkv_w, dtype=np.float32).reshape(E, I3)
    bqkv = np.ascontiguousarray(L0_attn_qkv_b, dtype=np.float32).reshape(1, I3)
    wout = np.ascontiguousarray(L0_attn_out_w, dtype=np.float32).reshape(E, E)
    bout = np.ascontiguousarray(L0_attn_out_b, dtype=np.float32).reshape(1, E)
    wg = np.ascontiguousarray(L0_ffn_w_gate, dtype=np.float32)
    wu = np.ascontiguousarray(L0_ffn_w_up, dtype=np.float32)
    wd = np.ascontiguousarray(L0_ffn_w_down, dtype=np.float32)

    if "nc" not in _NC_CACHE:
        _NC_CACHE["nc"] = build()
    nc = _NC_CACHE["nc"]
    in_maps = [{"x": np.ascontiguousarray(x[i]), "wqkv": wqkv, "bqkv": bqkv,
                "wout": wout, "bout": bout, "wg": wg, "wu": wu, "wd": wd}
               for i in range(8)]
    res = run_bass_kernel_spmd(nc, in_maps, core_ids=list(range(8)))
    return np.stack([res.results[i]["y"] for i in range(8)], axis=0)


# version 15 (final)
# speedup vs baseline: 2.0717x; 1.0462x over previous
import contextlib
import numpy as np
import concourse.bass as bass
import concourse.mybir as mybir
from concourse.bass_utils import run_bass_kernel_spmd

F32 = mybir.dt.float32
F16 = mybir.dt.float16
BF16 = mybir.dt.bfloat16
I32 = mybir.dt.int32
AF = mybir.ActivationFunctionType
ALU = mybir.AluOpType
AX = mybir.AxisListType

T = 2048
E = 1024
H = 16
D = 64
I3 = 3072
NT = T // 128
NQ = T // 512
NE = E // 128
NI = I3 // 128
QMAX = 32767.0
EXPOFF = float(np.log(1024.0))


class Sem:
    __slots__ = ("h", "n", "name")

    def __init__(self, h, name):
        self.h = h
        self.n = 0
        self.name = name


class Em:

    CHAINED = ("vector", "scalar", "gpsimd")

    def __init__(self, nc, es):
        self.nc = nc
        self.es = es
        self.steps = []
        self.chains = {}

    def sem(self, name):
        return Sem(self.es.enter_context(self.nc.semaphore(name)), name)

    def set_chains(self, v, a, p):
        self.chains = {"vector": v, "scalar": a, "gpsimd": p}

    def emit(self, eng, fn, waits=(), inc=None, k=1, no_chain=False):
        chain_prev = -1
        if eng in self.CHAINED and not no_chain:
            ch = self.chains[eng]
            if inc is None:
                inc = ch
            assert inc is ch
            chain_prev = ch.n
        incs = ()
        if inc is not None:
            inc.n += k
            incs = ((inc, k),)
        self.steps.append((eng, fn, tuple(waits), incs, chain_prev))

    def dma(self, eng, out, in_, sem, waits=(), **kw):
        self.emit(eng, lambda e, o=out, i=in_, kw=kw: e.dma_start(out=o, in_=i, **kw),
                  waits=waits, inc=sem, k=16, no_chain=True)

    def finalize(self, block):
        final = {}
        for _, _, _, incs, _ in self.steps:
            for s, k in incs:
                final[s.name] = final.get(s.name, 0) + k
        for _, _, waits, _, _ in self.steps:
            for s, v in waits:
                assert v <= final.get(s.name, 0), \
                    f"wait {s.name}>={v} but final={final.get(s.name, 0)}"

        def make(name):
            def body(eng):
                last_w = {}
                chain = self.chains.get(name)
                for ename, fn, waits, incs, chain_prev in self.steps:
                    if ename != name:
                        continue
                    for s, v in waits:
                        if v > 0 and last_w.get(s.name, -1) < v:
                            eng.wait_ge(s.h, v)
                            last_w[s.name] = v
                    inst = fn(eng)
                    if chain_prev > 0:
                        inst._wait_ge(chain.h, chain_prev)
                    for s, k in incs:
                        inst.then_inc(s.h, k)
            return body
        block.sync(make("sync"))
        block.scalar(make("scalar"))
        block.vector(make("vector"))
        block.tensor(make("tensor"))
        block.gpsimd(make("gpsimd"))


def build(dbg=False):
    nc = bass.Bass("TRN2", target_bir_lowering=False,
                   dynamic_dma_scratch_size=2048)

    x_d = nc.dram_tensor("x", [T, E], F32, kind="ExternalInput").ap()
    wqkv_d = nc.dram_tensor("wqkv", [E, I3], F32, kind="ExternalInput").ap()
    bqkv_d = nc.dram_tensor("bqkv", [1, I3], F32, kind="ExternalInput").ap()
    wout_d = nc.dram_tensor("wout", [E, E], F32, kind="ExternalInput").ap()
    bout_d = nc.dram_tensor("bout", [1, E], F32, kind="ExternalInput").ap()
    wg_d = nc.dram_tensor("wg", [I3, E], F32, kind="ExternalInput").ap()
    wu_d = nc.dram_tensor("wu", [I3, E], F32, kind="ExternalInput").ap()
    wd_d = nc.dram_tensor("wd", [E, I3], F32, kind="ExternalInput").ap()
    y_d = nc.dram_tensor("y", [T, E], F32, kind="ExternalOutput").ap()

    def dscr(name, shape, dt):
        if dbg:
            return nc.dram_tensor(name, shape, dt, kind="ExternalOutput").ap()
        return nc.dram_tensor(name, shape, dt).ap()

    xn_h = dscr("xn_h", [T, E], F16)
    v_h = dscr("v_h", [E, T], BF16)
    wqg_h = dscr("wqg_h", [I3, E], F16)
    wqu_h = dscr("wqu_h", [I3, E], F16)
    wqd_h = dscr("wqd_h", [E, I3], F16)
    spg_h = dscr("spg_h", [I3, T], F16)
    spu_h = dscr("spu_h", [I3, T], F16)
    spg2_h = dscr("spg2_h", [I3, T], F16)
    spo_h = dscr("spo_h", [I3, T], F16)
    spq_h = dscr("spq_h", [I3, T], F16)
    cc_i = [nc.dram_tensor(f"cci{i}", [1, 2], F32).ap() for i in range(4)]
    cc_o = [nc.dram_tensor(f"cco{i}", [1, 2], F32).ap() for i in range(4)]
    id128_d = nc.inline_tensor(np.eye(128, dtype=np.float32), "id128d").ap()
    xnf_dbg = dscr("xnf_dbg", [128, 16384], F16) if dbg else None

    es = contextlib.ExitStack()
    with es:
        sb = lambda n, s, d: es.enter_context(nc.sbuf_tensor(n, s, d))
        arA = sb("arA", [128, 16384], F16)
        arB = sb("arB", [128, 16384], F16)
        arQ = sb("arQ", [128, 16384], F16)
        arK = sb("arK", [128, 16384], F16)
        arV = sb("arV", [128, 16640], F16)
        arW = sb("arW", [128, 11392], F16)
        st = sb("st", [128, 32], F32)
        nbE = sb("nbE", [128, 8], F32)
        nbO = sb("nbO", [128, 8], F32)
        bcol = sb("bcol", [128, 24], F32)
        bq8 = sb("bq8", [128, 24], F32)
        ob16 = sb("ob16", [1, E], F16)
        id128 = sb("id128", [128, 128], F32)
        ones32 = sb("ones32", [1, 128], F32)
        ones16 = sb("ones16", [1, 128], F16)
        sel2 = sb("sel2", [128, 2], F32)
        selE = sb("selE", [2, 128], F32)
        selO = sb("selO", [2, 128], F32)
        qn2 = sb("qn2", [2, 8], F32)
        kn2 = sb("kn2", [2, 8], F32)
        nbsrc = sb("nbsrc", [2, 8], F32)
        mcol = sb("mcol", [128, 8], F32)
        cst = sb("cst", [128, 2], F32)
        vland = sb("vland", [128, 256], F16)
        nc.const_aps.register(F32, 1e-5, cst[:, 0:1])

        COL = {k: i for i, k in enumerate(
            ["m", "var", "rstd", "t0", "t1", "t2",
             "s1", "i1", "s2", "i2", "s3", "i3", "hs3", "q3",
             "s4", "i4", "s42", "s5", "i5"])}

        def stc(name, p0=0, p1=128):
            c = COL[name]
            return st[p0:p1, c:c + 1]

        MC = {"xn": 0, "g": 1, "u": 2, "g2": 3, "o": 4, "g2b": 5, "ob": 6}

        def mc(name):
            return mcol[:, MC[name]:MC[name] + 1]

        pb = [es.enter_context(nc.psum_tensor(f"pb{i}", [128, 512], F32))
              for i in range(8)]

        block = es.enter_context(nc.Block())
        em = Em(nc, es)
        sv = em.sem("sv")
        sa = em.sem("sa")
        spl = em.sem("spl")
        stn = em.sem("stn")
        em.set_chains(sv, sa, spl)
        dxa = [em.sem("dxa0"), em.sem("dxa1")]
        dxn = em.sem("dxn")
        dxf = em.sem("dxf")
        dwl = [em.sem("dwl0"), em.sem("dwl1")]
        dlq = [em.sem("dlq0"), em.sem("dlq1")]
        dqw = em.sem("dqw")
        dgt = [em.sem("dgt0"), em.sem("dgt1")]
        dvw = em.sem("dvw")
        dvr = em.sem("dvr")
        dsp = em.sem("dsp")
        dgr = [em.sem("dgr0"), em.sem("dgr1")]
        dur = [em.sem("dur0"), em.sem("dur1")]
        dcc = em.sem("dcc")
        scc = em.sem("scc")
        doq = [em.sem("doq0"), em.sem("doq1")]
        dwt = em.sem("dwt")
        dxr = [em.sem("dxr0"), em.sem("dxr1")]
        dy = em.sem("dy")
        dmi = em.sem("dmi")

        V, A, P, TN, SY = "vector", "scalar", "gpsimd", "tensor", "sync"

        f32c = lambda ar, c0, n: ar[:, c0:c0 + 2 * n].bitcast(F32)
        i32c = lambda ar, c0, n: ar[:, c0:c0 + 2 * n].bitcast(I32)
        bf16c = lambda ar, c0, n: ar[:, c0:c0 + n].bitcast(BF16)

        xn_f = arA
        sq_f32 = f32c(arA, 10240, 2048)
        pt_bf = [bf16c(arA, 14336 + i * 512, 512) for i in range(4)]
        m_gin = [arA[:, 0:512], arA[:, 512:1024]]
        m_tf = [f32c(arA, 1024, 512), f32c(arA, 2048, 512)]
        m_i32 = [i32c(arA, 3072, 512), i32c(arA, 4096, 512)]
        m_gf = [f32c(arA, 5120, 512), f32c(arA, 6144, 512)]
        m_th = [f32c(arA, 7168, 512), f32c(arA, 8192, 512)]
        m_out = [arA[:, 9216:9728], arA[:, 9728:10240]]
        m_uf = [f32c(arA, 10240, 512), f32c(arA, 11264, 512)]
        m_ui32 = [i32c(arA, 12288, 512), i32c(arA, 13312, 512)]
        m_uin = [arA[:, 14336:14848], arA[:, 14848:15360]]

        wst = [f32c(arB, 0, 1024), f32c(arB, 2048, 1024)]
        wh = [arB[:, 4096:5120], arB[:, 5120:6144]]
        vst = [arB[:, 6144 + i * 512:6656 + i * 512].bitcast(BF16)
               for i in range(4)]
        xq_f = arB
        O_f = arB

        Q_f = arQ
        wdTh = arQ[:, 0:12288]
        wouth = arQ[:, 12288:16384]

        K_f = arK
        oqt = [arK[:, 0:3072], arK[:, 3072:6144]]
        xrr = [f32c(arK, 6144, 512), f32c(arK, 7168, 512)]
        yt = [f32c(arK, 8192, 512), f32c(arK, 9216, 512)]

        v_tok = arV

        lw_f32 = f32c(arW, 0, 1024)
        wgT = [arW[:, 2048:3072], arW[:, 3072:4096]]
        lw_tf = f32c(arW, 4096, 1024)
        lw_i32 = i32c(arW, 4096, 1024)
        lw_int16 = arW[:, 6144:7168]
        lw_q = [arW[:, 7168:8192], arW[:, 8192:9216]]
        lw_s = f32c(arW, 9216, 64)
        wo_st = f32c(arW, 0, 512)
        xqf32_t = f32c(arW, 9344, 512)
        xqi32_t = i32c(arW, 10368, 512)
        rbb_t = xqf32_t
        rb = xqf32_t[0:1, :]
        brow = lw_f32[0:24, 0:128]
        obf = lw_tf[0:1, :]
        xa_t = f32c(arQ, 0, 2048)
        xsq_t = f32c(arQ, 4096, 1024)
        xnt_t = arQ[:, 6144:8192]

        em.emit(V, lambda e: e.memset(cst[:, 0:1], 1e-5))
        em.emit(V, lambda e: e.memset(ones32[:, :], 1.0))
        em.emit(V, lambda e: e.memset(ones16[:, :], 1.0))
        em.emit(V, lambda e: e.memset(st[:, :], 0.0))
        em.emit(V, lambda e: e.memset(mcol[:, :], 0.0))
        em.emit(V, lambda e: e.memset(qn2[:, :], 0.0))
        em.emit(V, lambda e: e.memset(kn2[:, :], 0.0))
        em.emit(V, lambda e: e.memset(sel2[:, :], 0.0))
        em.emit(V, lambda e: e.memset(sel2[0:64, 0:1], 1.0))
        em.emit(V, lambda e: e.memset(sel2[64:128, 1:2], 1.0))

        def vtok_ones(e):
            ap = v_tok[:, :].bitcast(BF16).rearrange("p (n c) -> p n c", c=65)
            return e.memset(ap[:, :, 64:65], 1.0)
        em.emit(V, vtok_ones)
        em.emit(V, lambda e: e.memset(selE[:, :], 1.0))
        em.emit(V, lambda e: e.memset(selO[:, :], 1.0))
        SELV = sv.n
        em.emit(P, lambda e: e.affine_select(
            selE[:, :], selE[:, :], [[1, 128]], ALU.is_ge, 0.0,
            base=0, channel_multiplier=-128), waits=[(sv, SELV)])
        em.emit(P, lambda e: e.affine_select(
            selO[:, :], selO[:, :], [[1, 128]], ALU.is_ge, 0.0,
            base=-128, channel_multiplier=128))
        INIT_V = sv.n
        INIT_P = spl.n
        em.dma(A, id128[:, :], id128_d, dmi)
        em.dma(A, brow[:, :], bqkv_d.rearrange("a (c p) -> (a c) p", p=128), dmi)
        em.dma(A, obf[:, :], bout_d, dmi)
        INIT_D = dmi.n
        em.emit(TN, lambda e: e.transpose(pb[7][:, 0:24], brow[:, :],
                                          id128[0:24, 0:24]),
                waits=[(dmi, INIT_D), (sv, INIT_V)], inc=stn)
        BT = stn.n
        em.emit(A, lambda e: e.activation(bcol[:, :], pb[7][:, 0:24],
                                          AF.Identity), waits=[(stn, BT)])
        em.emit(A, lambda e: e.activation(bq8[:, :], pb[7][:, 0:24],
                                          AF.Identity, scale=0.125))
        BC_A = sa.n
        em.emit(A, lambda e: e.activation(ob16[:, :], obf[:, :], AF.Identity),
                waits=[(dmi, INIT_D)])
        OB_A = sa.n

        xn_done_v = []
        xn_done_a = []
        for it in range(NT):
            bi = it % 2
            xt = xa_t[:, bi * 1024:(bi + 1) * 1024]
            xnt = xnt_t[:, bi * 1024:(bi + 1) * 1024]
            w = []
            if it >= 2:
                w.append((sa, xn_done_a[it - 2]))
            em.dma(A, xt, x_d[it * 128:(it + 1) * 128, :], dxa[bi], waits=w)
            DX = dxa[bi].n
            wv = [(dxa[bi], DX)]
            em.emit(A, lambda e, xt=xt: e.activation(xsq_t[:, :], xt, AF.Square),
                    waits=wv + ([(sv, xn_done_v[it - 1])] if it >= 1 else []))
            SQ = sa.n
            em.emit(V, lambda e, xt=xt: e.tensor_reduce(
                stc("m"), xt, AX.X, ALU.add), waits=[(dxa[bi], DX)])
            em.emit(V, lambda e: e.tensor_scalar_mul(stc("m"), stc("m"), 1.0 / E))
            em.emit(V, lambda e: e.tensor_reduce(
                stc("var"), xsq_t[:, :], AX.X, ALU.add), waits=[(sa, SQ)])
            em.emit(V, lambda e: e.tensor_scalar_mul(stc("var"), stc("var"),
                                                     1.0 / E))
            em.emit(V, lambda e: e.scalar_tensor_tensor(
                stc("t0"), stc("m"), stc("m"), stc("var"),
                ALU.mult, ALU.subtract))
            VW = sv.n
            em.emit(A, lambda e: e.activation(
                stc("t1"), stc("t0"), AF.Sqrt, bias=1e-5, scale=-1.0),
                waits=[(sv, VW)])
            SD = sa.n
            em.emit(V, lambda e: e.reciprocal(stc("rstd"), stc("t1")),
                    waits=[(sa, SD)])
            em.emit(V, lambda e, xt=xt: e.tensor_scalar(
                xsq_t[:, :], xt, stc("m"), None, ALU.subtract))
            em.emit(V, lambda e: e.tensor_scalar(
                xsq_t[:, :], xsq_t[:, :], stc("rstd"), None, ALU.mult))
            XN = sv.n
            em.emit(V, lambda e: e.tensor_reduce(
                stc("t1"), xsq_t[:, :], AX.X, ALU.max,
                apply_absolute_value=True))
            em.emit(V, lambda e: e.tensor_tensor(
                mc("xn"), mc("xn"), stc("t1"), ALU.max))
            xn_done_v.append(sv.n)
            em.emit(A, lambda e, xnt=xnt: e.activation(
                xnt, xsq_t[:, :], AF.Identity),
                waits=[(sv, XN)] + ([(dxn, 16 * (it - 1))] if it >= 2 else []))
            xn_done_a.append(sa.n)
            em.dma(SY, xn_h[it * 128:(it + 1) * 128, :], xnt, dxn,
                   waits=[(sa, xn_done_a[-1])])
            for c in range(NE):
                em.dma(SY, xn_f[:, c * T + it * 128:c * T + (it + 1) * 128],
                       xn_h[it * 128:(it + 1) * 128, c * 128:(c + 1) * 128],
                       dxf, waits=[(dxn, dxn.n)], transpose=True)
        LN_V = sv.n
        XNF_D = dxf.n

        def fold_scalar(src_col, dst, waits):
            em.emit(TN, lambda e: e.transpose(
                pb[7][0:1, 0:128], src_col, id128[:, :]),
                waits=list(waits) + [(dmi, INIT_D), (sv, sv.n), (sa, sa.n)],
                inc=stn)
            TW = stn.n
            em.emit(V, lambda e, dst=dst: e.tensor_reduce(
                stc(dst, 0, 1), pb[7][0:1, 0:128], AX.X, ALU.max),
                waits=[(stn, TW)])
            return sv.n

        MX_V = fold_scalar(mc("xn"), "t0", [(sv, LN_V)])

        def collective(ci, waits):
            em.dma(P, cc_i[ci], st[0:1, COL["t0"]:COL["t0"] + 2], dcc,
                   waits=waits)
            em.emit(P, lambda e, ci=ci: e.collective_compute(
                "AllReduce", ALU.max, [[0, 1, 2, 3, 4, 5, 6, 7]],
                [cc_i[ci].opt()], [cc_o[ci].opt()]),
                waits=[(dcc, dcc.n)], inc=scc, no_chain=True)
            return scc.n

        CC1 = collective(0, [(sv, MX_V)])
        em.dma(P, st[0:1, COL["t0"]:COL["t0"] + 2], cc_o[0], dcc,
               waits=[(scc, CC1)])
        CC1D = dcc.n

        def bcast_scale(srccol, scol, icol, waits, extra=()):
            em.emit(TN, lambda e: e.matmul(
                pb[7][:, 0:1], ones32[0:1, 0:128],
                st[0:1, COL[srccol]:COL[srccol] + 1], start=True, stop=True),
                waits=list(waits) + [(sv, sv.n), (sa, sa.n)], inc=stn)
            TW = stn.n
            em.emit(V, lambda e: e.tensor_scalar(
                stc(scol), pb[7][:, 0:1], 1.0 / QMAX, 1e-12, ALU.mult, ALU.max),
                waits=[(stn, TW)])
            em.emit(V, lambda e: e.reciprocal(stc(icol), stc(scol)))
            for f in extra:
                f()
            return sv.n

        S1_V = bcast_scale("t0", "s1", "i1", [(dcc, CC1D)])

        lqst = {"n": 0, "a": 0, "p": 0, "wr": 0, "q": [0, 0]}

        def lpbq_tile(src_ap, dst_ap):
            k = lqst["n"]
            lqst["n"] += 1
            w = [(dmi, INIT_D)]
            if k == 0:
                w += [(stn, BT)]
            if k >= 1:
                w += [(sv, lqst["p"])]
            em.dma(SY, lw_f32, src_ap, dlq[k % 2], waits=w)
            RD = dlq[k % 2].n
            wv3 = lw_f32.rearrange("p (nb e) -> p nb e", e=32)
            em.emit(V, lambda e, wv3=wv3: e.tensor_reduce(
                lw_s[:, 0:32], wv3, AX.X, ALU.max, apply_absolute_value=True),
                waits=[(dlq[k % 2], RD)])
            em.emit(V, lambda e: e.tensor_scalar(
                lw_s[:, 32:64], lw_s[:, 0:32], 1.0 / 7.0, 1e-12,
                ALU.mult, ALU.max))
            em.emit(V, lambda e: e.reciprocal(lw_s[:, 0:32], lw_s[:, 32:64]))
            inv_b = lw_s[:, 0:32].unsqueeze(2).broadcast_to([128, 32, 32])
            s_b = lw_s[:, 32:64].unsqueeze(2).broadcast_to([128, 32, 32])
            tfv = lw_tf.rearrange("p (nb e) -> p nb e", e=32)
            war_t2 = [(sa, OB_A)] if k == 0 else []
            em.emit(V, lambda e, wv3=wv3, inv_b=inv_b, tfv=tfv: e.tensor_tensor(
                tfv, wv3, inv_b, ALU.mult), waits=war_t2)
            em.emit(V, lambda e: e.tensor_copy(lw_i32[:, :], lw_tf))
            em.emit(V, lambda e: e.tensor_copy(lw_int16[:, :], lw_i32[:, :]))
            lqst["p"] = sv.n
            qs = lw_q[k % 2]
            war_q = [(dqw, lqst["q"][k % 2])]
            em.emit(V, lambda e, qs=qs, s_b=s_b: e.tensor_tensor(
                qs.rearrange("p (nb e) -> p nb e", e=32),
                lw_int16.rearrange("p (nb e) -> p nb e", e=32),
                s_b, ALU.mult), waits=war_q)
            QP = sv.n
            lqst["a"] = sa.n
            em.dma(SY, dst_ap, qs, dqw, waits=[(sv, QP)])
            lqst["wr"] = dqw.n
            lqst["q"][k % 2] = dqw.n

        def lpbq_jobs(w_src, w_dst, nrows, ncols):
            out = []
            for rc in range(nrows // 128):
                for sc in range(ncols // 1024):
                    out.append((w_src[rc * 128:(rc + 1) * 128,
                                      sc * 1024:(sc + 1) * 1024],
                                w_dst[rc * 128:(rc + 1) * 128,
                                      sc * 1024:(sc + 1) * 1024]))
            return out

        side_jobs = (lpbq_jobs(wg_d, wqg_h, I3, E) + lpbq_jobs(wu_d, wqu_h, I3, E)
                     + lpbq_jobs(wd_d, wqd_h, E, I3))
        side_it = iter(side_jobs)

        def pop_side(n=1):
            for _ in range(n):
                job = next(side_it, None)
                if job is not None:
                    lpbq_tile(*job)

        wqkv_r = wqkv_d.rearrange("(c p) f -> p c f", p=128)
        qkv_conv_a = [0, 0]
        whmm = [0, 0]
        evq = [0, 0]
        vwd = [0, 0, 0, 0]
        vst_a = [0, 0, 0, 0]

        def qkv_load(m):
            b = m % 2
            w = [(sa, qkv_conv_a[b])] if m >= 2 else []
            for k in range(NE):
                em.dma(SY, wst[b][:, k * 128:(k + 1) * 128],
                       wqkv_r[:, k, m * 128:(m + 1) * 128], dwl[b],
                       waits=(w if k == 0 else []))
            return dwl[b].n

        qdw = [0, 0]
        qdw[0] = qkv_load(0)
        for m in range(24):
            b = m % 2
            j, r3 = divmod(m, 3)
            if m + 1 < 24:
                qdw[(m + 1) % 2] = qkv_load(m + 1)
            DW = qdw[b]
            wconv = [(dwl[b], DW)]
            if m >= 2:
                wconv.append((stn, whmm[b]))
            em.emit(A, lambda e, b=b: e.activation(wh[b][:, :], wst[b][:, :],
                                                   AF.Identity), waits=wconv)
            WH = sa.n
            qkv_conv_a[b] = WH
            whv = wh[b].rearrange("p (c f) -> p c f", c=NE)
            for t in range(NQ):
                bk = (m * NQ + t) % 2
                wl = [(sa, WH), (dxf, XNF_D)]
                if m * NQ + t >= 2:
                    wl.append((sa, evq[bk]))
                for k in range(NE):
                    em.emit(TN, lambda e, bk=bk, k=k, t=t, whv=whv: e.matmul(
                        pb[bk][:, :], whv[:, k, :],
                        xn_f[:, k * T + t * 512:k * T + (t + 1) * 512],
                        start=(k == 0), stop=(k == NE - 1)),
                        waits=(wl if k == 0 else ()),
                        inc=(stn if k == NE - 1 else None))
                MM = stn.n
                if r3 == 0:
                    em.emit(A, lambda e, bk=bk, j=j, t=t, m=m: e.activation(
                        Q_f[0:64, j * T + t * 512:j * T + (t + 1) * 512],
                        pb[bk][0:64, :], AF.Identity,
                        bias=bq8[0:64, m:m + 1], scale=0.125),
                        waits=[(stn, MM), (sa, BC_A)])
                    em.emit(A, lambda e, bk=bk, j=j, t=t, m=m: e.activation(
                        K_f[0:64, j * T + t * 512:j * T + (t + 1) * 512],
                        pb[bk][64:128, :], AF.Identity,
                        bias=bcol[64:128, m:m + 1]))
                elif r3 == 1:
                    em.emit(A, lambda e, bk=bk, t=t, m=m: e.activation(
                        vst[t][0:64, :], pb[bk][0:64, :], AF.Identity,
                        bias=bcol[0:64, m:m + 1]),
                        waits=[(stn, MM), (sa, BC_A), (dvw, vwd[t])])
                    em.emit(A, lambda e, bk=bk, j=j, t=t, m=m: e.activation(
                        Q_f[64:128, j * T + t * 512:j * T + (t + 1) * 512],
                        pb[bk][64:128, :], AF.Identity,
                        bias=bq8[64:128, m:m + 1], scale=0.125))
                else:
                    em.emit(A, lambda e, bk=bk, j=j, t=t, m=m: e.activation(
                        K_f[64:128, j * T + t * 512:j * T + (t + 1) * 512],
                        pb[bk][0:64, :], AF.Identity,
                        bias=bcol[0:64, m:m + 1]),
                        waits=[(stn, MM), (sa, BC_A)])
                    em.emit(A, lambda e, bk=bk, t=t, m=m: e.activation(
                        vst[t][64:128, :], pb[bk][64:128, :], AF.Identity,
                        bias=bcol[64:128, m:m + 1]), waits=[(dvw, vwd[t])])
                    vst_a[t] = sa.n
                    em.dma(SY, v_h[j * 128:(j + 1) * 128,
                                   t * 512:(t + 1) * 512], vst[t], dvw,
                           waits=[(sa, vst_a[t])])
                    vwd[t] = dvw.n
                evq[bk] = sa.n
            whmm[b] = stn.n
            if m % 2 == 1:
                pop_side(2)
        QKV_MM = stn.n
        QKV_A = sa.n
        VW_D = dvw.n
        if dbg:
            em.dma(SY, xnf_dbg, arA[:, :], dmi, waits=[(stn, QKV_MM)])

        xq_a = [0]
        first_xq = True
        for c in range(NE):
            for t in range(NQ):
                wv = [(sv, S1_V)]
                if first_xq:
                    wv += [(stn, QKV_MM), (sa, QKV_A), (dvw, VW_D)]
                    first_xq = False
                if xq_a[0]:
                    wv.append((sa, xq_a[0]))
                em.emit(V, lambda e, c=c, t=t: e.tensor_scalar(
                    xqf32_t, xn_f[:, c * T + t * 512:c * T + (t + 1) * 512],
                    stc("i1"), -32768.0, ALU.mult, ALU.max), waits=wv)
                em.emit(V, lambda e: e.tensor_scalar(
                    xqi32_t, xqf32_t, 32767.0, None, ALU.min))
                I32V = sv.n
                em.emit(A, lambda e, c=c, t=t: e.activation(
                    xq_f[:, c * T + t * 512:c * T + (t + 1) * 512],
                    xqi32_t, AF.Identity), waits=[(sv, I32V)])
                xq_a[0] = sa.n
        XQ_A = sa.n
        XQ_V = sv.n

        spill_cnt = {"g": 0, "u": 0}
        gumax_v = {"g": 0, "u": 0}
        gu_ev = {"ev0": 0, "ev1": 0}
        gumm = [0, 0]
        gusp = [0, 0]

        def gu_load(wq_h, i, wq_wr_need):
            b = i % 2
            w = [(dqw, wq_wr_need)]
            if i >= 2:
                w.append((stn, gumm[b]))
            for k in range(NE):
                em.dma(SY, wgT[b][:, k * 128:(k + 1) * 128],
                       wq_h[i * 128:(i + 1) * 128, k * 128:(k + 1) * 128],
                       dgt[b], waits=(w if k == 0 else []), transpose=True)
            return dgt[b].n

        def gu_stream(wq_h, sp_h, mx, wq_wr_need):
            gl = [0, 0]
            gl[0] = gu_load(wq_h, 0, wq_wr_need)
            for i in range(NI):
                b = i % 2
                if i + 1 < NI:
                    gl[(i + 1) % 2] = gu_load(wq_h, i + 1, wq_wr_need)
                DG = gl[b]
                for t in range(NQ):
                    bk = (i * NQ + t) % 2
                    wl = [(dgt[b], DG), (sa, XQ_A)]
                    if gu_ev[f"ev{bk}"]:
                        wl.append((sa, gu_ev[f"ev{bk}"]))
                    for k in range(NE):
                        em.emit(TN, lambda e, bk=bk, b=b, k=k, t=t: e.matmul(
                            pb[bk][:, :], wgT[b][:, k * 128:(k + 1) * 128],
                            xq_f[:, k * T + t * 512:k * T + (t + 1) * 512],
                            start=(k == 0), stop=(k == NE - 1)),
                            waits=(wl if k == 0 else ()),
                            inc=(stn if k == NE - 1 else None))
                    MM = stn.n
                    sl = m_out[bk]
                    em.emit(A, lambda e, sl=sl, bk=bk: e.activation(
                        sl, pb[bk][:, :], AF.Identity, scale=stc("s1")),
                        waits=[(stn, MM), (dsp, gusp[bk])])
                    EVA = sa.n
                    gu_ev[f"ev{bk}"] = EVA
                    em.emit(V, lambda e, sl=sl: e.tensor_reduce(
                        stc("t1"), sl, AX.X, ALU.max,
                        apply_absolute_value=True), waits=[(sa, EVA)])
                    em.emit(V, lambda e, mx=mx: e.tensor_tensor(
                        mc(mx), mc(mx), stc("t1"), ALU.max))
                    gumax_v[mx] = sv.n
                    em.dma(SY, sp_h[i * 128:(i + 1) * 128,
                                    t * 512:(t + 1) * 512], sl, dsp,
                           waits=[(sv, gumax_v[mx])])
                    gusp[bk] = dsp.n
                    spill_cnt[mx] += 1
                gumm[b] = stn.n
                if i % 2 == 1:
                    pop_side(1)

        pop_side(26)
        gu_stream(wqg_h, spg_h, "g", 16 * 24)
        pop_side(12)
        gu_stream(wqu_h, spu_h, "u", 16 * 48)
        pop_side(len(side_jobs))
        GU_MM = stn.n
        GU_SPILLS = 16 * (spill_cnt["g"] + spill_cnt["u"])
        LPBQ_WR = lqst["wr"]
        assert lqst["n"] == len(side_jobs)

        G_V = fold_scalar(mc("g"), "t0", [(sv, gumax_v["g"])])
        U_V = fold_scalar(mc("u"), "t1", [(sv, gumax_v["u"])])
        CC2 = collective(1, [(sv, U_V)])
        em.dma(P, st[0:1, COL["t0"]:COL["t0"] + 2], cc_o[1], dcc,
               waits=[(scc, CC2)])
        CC2D = dcc.n

        def s3_extra():
            em.emit(V, lambda e: e.tensor_scalar_mul(stc("hs3"), stc("s3"), 0.5))
            em.emit(V, lambda e: e.tensor_scalar_mul(stc("q3"), stc("s3"),
                                                     1.0 / 65536.0))
        S3_V = bcast_scale("t0", "s3", "i3", [(dcc, CC2D)], extra=(s3_extra,))
        S2_V = bcast_scale("t1", "s2", "i2", [(dcc, CC2D)])

        vcp = [0, 0]
        for idx in range(NT * NE):
            kt, j = divmod(idx, NE)
            r = idx % 2
            land = vland[:, r * 128:(r + 1) * 128].bitcast(BF16)
            w = [(dvw, VW_D)]
            if idx >= 2:
                w.append((spl, vcp[r]))
            em.dma(SY, land, v_h[j * 128:(j + 1) * 128,
                                 kt * 128:(kt + 1) * 128],
                   dvr, waits=w, transpose=True)
            DVT = dvr.n
            s0 = (kt * H + 2 * j) * 65
            s1 = (kt * H + 2 * j + 1) * 65
            em.emit(P, lambda e, land=land, s0=s0: e.tensor_copy(
                v_tok[:, s0:s0 + 64].bitcast(BF16), land[:, 0:64]),
                waits=[(dvr, DVT), (sv, INIT_V)])
            em.emit(P, lambda e, land=land, s1=s1: e.tensor_copy(
                v_tok[:, s1:s1 + 64].bitcast(BF16), land[:, 64:128]))
            vcp[r] = spl.n
        VT_P = spl.n
        VT_D = dvr.n

        def norms(src, acc):
            for hc in range(NE):
                em.emit(A, lambda e, hc=hc, src=src: e.activation(
                    sq_f32[:, :], src[:, hc * T:(hc + 1) * T], AF.Square),
                    waits=[(stn, stn.n), (sv, XQ_V)])
                SQA = sa.n
                for t in range(NQ):
                    em.emit(TN, lambda e, t=t: e.matmul(
                        pb[6][0:2, :], sel2[:, :],
                        sq_f32[:, t * 512:(t + 1) * 512],
                        start=True, stop=True),
                        waits=[(sa, SQA), (sv, sv.n)], inc=stn)
                    TW = stn.n
                    em.emit(V, lambda e: e.tensor_reduce(
                        stc("t1", 0, 2), pb[6][0:2, :], AX.X, ALU.max),
                        waits=[(stn, TW)])
                    em.emit(V, lambda e, hc=hc, acc=acc: e.tensor_tensor(
                        acc[0:2, hc:hc + 1], acc[0:2, hc:hc + 1],
                        stc("t1", 0, 2), ALU.max))
            return sv.n

        QN_V = norms(Q_f, qn2)
        KN_V = norms(K_f, kn2)
        em.emit(V, lambda e: e.tensor_tensor(nbsrc[0:2, :], qn2[0:2, :],
                                             kn2[0:2, :], ALU.mult),
                waits=[(sv, KN_V)])
        NB1 = sv.n
        em.emit(A, lambda e: e.activation(nbsrc[0:2, :], nbsrc[0:2, :],
                                          AF.Sqrt), waits=[(sv, NB1)])
        NB2 = sa.n
        em.emit(V, lambda e: e.tensor_scalar(
            nbsrc[0:2, :], nbsrc[0:2, :], -1.0, EXPOFF, ALU.mult, ALU.add),
            waits=[(sa, NB2)])
        NB3 = sv.n
        em.emit(TN, lambda e: e.matmul(pb[6][:, 0:8], selE[:, :],
                                       nbsrc[0:2, :], start=True, stop=True),
                waits=[(sv, NB3), (spl, INIT_P), (sv, sv.n)], inc=stn)
        NBT1 = stn.n
        em.emit(A, lambda e: e.activation(nbE[:, :], pb[6][:, 0:8],
                                          AF.Identity), waits=[(stn, NBT1)])
        NBA1 = sa.n
        em.emit(TN, lambda e: e.matmul(pb[6][:, 0:8], selO[:, :],
                                       nbsrc[0:2, :], start=True, stop=True),
                waits=[(sa, NBA1)], inc=stn)
        NBT2 = stn.n
        em.emit(A, lambda e: e.activation(nbO[:, :], pb[6][:, 0:8],
                                          AF.Identity), waits=[(stn, NBT2)])
        NEGB_A = sa.n

        m1 = {"n": 0, "ec": [0, 0], "sp": [0, 0], "wr": 0, "th": [0, 0],
              "pend": [None, None]}

        def m1_pre(jn):
            i, t = divmod(jn, NQ)
            r = jn % 2
            w = [(dsp, GU_SPILLS)]
            if m1["ec"][r]:
                w.append((sv, m1["ec"][r]))
            if m1["sp"][r]:
                w.append((dsp, m1["sp"][r]))
            em.dma(SY, m_gin[r], spg_h[i * 128:(i + 1) * 128,
                                       t * 512:(t + 1) * 512], dgr[r], waits=w)
            RD = dgr[r].n
            em.emit(V, lambda e, r=r: e.tensor_scalar(
                m_tf[r], m_gin[r], stc("i3"), -32768.0, ALU.mult, ALU.max),
                waits=[(dgr[r], RD), (sv, S3_V)])
            em.emit(V, lambda e, r=r: e.tensor_scalar(
                m_i32[r], m_tf[r], 32767.0, None, ALU.min))
            em.emit(V, lambda e, r=r: e.tensor_copy(m_gf[r], m_i32[r]))
            GF = sv.n
            em.emit(A, lambda e, r=r: e.activation(m_th[r], m_gf[r], AF.Tanh,
                                                   scale=stc("hs3")),
                    waits=[(sv, GF)])
            m1["th"][r] = sa.n

        def m1_post(jn):
            i, t = divmod(jn, NQ)
            r = jn % 2
            em.emit(V, lambda e, r=r: e.tensor_scalar(
                m_tf[r], m_th[r], 32768.0, 32768.0, ALU.mult, ALU.add),
                waits=[(sa, m1["th"][r])])
            em.emit(V, lambda e, r=r: e.tensor_scalar(
                m_i32[r], m_tf[r], 65535.0, None, ALU.min))
            em.emit(V, lambda e, r=r: e.tensor_copy(m_th[r], m_i32[r]))
            em.emit(V, lambda e, r=r: e.tensor_tensor(
                m_tf[r], m_gf[r], m_th[r], ALU.mult))
            em.emit(V, lambda e, r=r: e.tensor_scalar(
                m_out[r], m_tf[r], stc("q3"), None, ALU.mult))
            em.emit(V, lambda e, r=r: e.tensor_reduce(
                stc("t1"), m_out[r], AX.X, ALU.max, apply_absolute_value=True))
            em.emit(V, lambda e: e.tensor_tensor(
                mc("g2"), mc("g2"), stc("t1"), ALU.max))
            m1["ec"][r] = sv.n
            em.dma(SY, spg2_h[i * 128:(i + 1) * 128, t * 512:(t + 1) * 512],
                   m_out[r], dsp, waits=[(sv, m1["ec"][r])])
            m1["sp"][r] = dsp.n
            m1["wr"] = dsp.n

        def m1_tile():
            jn = m1["n"]
            m1["n"] += 1
            m1_pre(jn)
            if jn >= 1:
                m1_post(jn - 1)
            if jn == NI * NQ - 1:
                m1_post(jn)

        ps_free = {0: 0, 1: 0, 2: 0, 3: 0}
        pt_free = [0, 0, 0, 0]
        o_ev = {4: 0, 5: 0}
        ep_a = [NEGB_A]
        si_ctr = [0]

        def attn_pair(hc):
            def scores_exp(qc, kt):
                diag = kt >= 4 * qc
                info = []
                for par, (sbk0, nb) in enumerate([(2, nbE), (3, nbO)]):
                    sbk = sbk0 if kt % 2 == 0 else par
                    si = si_ctr[0]
                    si_ctr[0] += 1
                    p0 = par * 64
                    w = [(sa, NEGB_A)]
                    if ps_free[sbk]:
                        w.append((sa, ps_free[sbk]))
                    em.emit(TN, lambda e, sbk=sbk, p0=p0, hc=hc, kt=kt,
                            qc=qc: e.matmul(
                        pb[sbk][:, :],
                        K_f[p0:p0 + 64,
                            hc * T + kt * 128:hc * T + (kt + 1) * 128],
                        Q_f[p0:p0 + 64,
                            hc * T + qc * 512:hc * T + (qc + 1) * 512],
                        start=True, stop=True), waits=w, inc=stn)
                    MMW = stn.n
                    slot = si % 4
                    we = [(stn, MMW)]
                    if pt_free[slot]:
                        we.append((stn, pt_free[slot]))
                    em.emit(A, lambda e, sbk=sbk, slot=slot, nb=nb, hc=hc:
                            e.activation(pt_bf[slot], pb[sbk][:, :],
                                         AF.Exp, bias=nb[:, hc:hc + 1]),
                            waits=we)
                    EXA = sa.n
                    ps_free[sbk] = EXA
                    if diag:
                        s = kt - 4 * qc
                        em.emit(P, lambda e, slot=slot, s=s:
                                e.affine_select(
                            pt_bf[slot], pt_bf[slot], [[1, 512]],
                            ALU.is_ge, 0.0, base=-128 * s,
                            channel_multiplier=-1), waits=[(sa, EXA)])
                        rdy = (spl, spl.n)
                    else:
                        rdy = (sa, EXA)
                    info.append((slot, rdy))
                return info

            def pv(qc, kt, nkt, info):
                for par, obk in ((0, 4), (1, 5)):
                    slot, rdy = info[par]
                    h = 2 * hc + par
                    wpv = [(spl, VT_P), rdy]
                    if kt == 0 and o_ev[obk]:
                        wpv.append((sv, o_ev[obk]))
                    em.emit(TN, lambda e, obk=obk, kt=kt, h=h, slot=slot,
                            nkt=nkt: e.matmul(
                        pb[obk][0:65, :],
                        v_tok[:, (kt * H + h) * 65:(kt * H + h + 1) * 65]
                        .bitcast(BF16),
                        pt_bf[slot], start=(kt == 0), stop=(kt == nkt - 1)),
                        waits=wpv, inc=stn)
                    pt_free[slot] = stn.n

            for qc in range(NQ):
                nkt = 4 * qc + 4
                pend = scores_exp(qc, 0)
                for kt in range(nkt):
                    nxt = scores_exp(qc, kt + 1) if kt + 1 < nkt else None
                    pv(qc, kt, nkt, pend)
                    pend = nxt
                for par, obk in ((0, 4), (1, 5)):
                    PVM = stn.n
                    em.emit(V, lambda e, obk=obk: e.tensor_scalar(
                        rb, pb[obk][64:65, :], 1e-12, None, ALU.add),
                        waits=[(stn, PVM)])
                    em.emit(V, lambda e: e.reciprocal(rb, rb))
                    RC = sv.n
                    em.emit(TN, lambda e: e.matmul(
                        pb[6][0:64, :], ones32[0:1, 0:64], rb,
                        start=True, stop=True),
                        waits=[(sv, RC), (sa, ep_a[0])], inc=stn)
                    BCW = stn.n
                    em.emit(A, lambda e: e.activation(
                        rbb_t[0:64, :], pb[6][0:64, :], AF.Identity),
                        waits=[(stn, BCW)])
                    ep_a[0] = sa.n
                    em.emit(V, lambda e, obk=obk, par=par, hc=hc, qc=qc:
                            e.tensor_tensor(
                        O_f[par * 64:par * 64 + 64,
                            hc * T + qc * 512:hc * T + (qc + 1) * 512],
                        pb[obk][0:64, :], rbb_t[0:64, :], ALU.mult),
                        waits=[(sa, ep_a[0])])
                    o_ev[obk] = sv.n

        for hc in range(NE):
            attn_pair(hc)
            if hc >= 2:
                for _ in range(16):
                    if m1["n"] < NI * NQ:
                        m1_tile()
        while m1["n"] < NI * NQ:
            m1_tile()
        ATT_MM = stn.n
        O_V = sv.n
        M1_WR = m1["wr"]

        G2_V = fold_scalar(mc("g2"), "t0",
                           [(sv, m1["ec"][0]), (sv, m1["ec"][1])])
        CC3 = collective(2, [(sv, G2_V)])
        em.dma(P, st[0:1, COL["t0"]:COL["t0"] + 2], cc_o[2], dcc,
               waits=[(scc, CC3)])
        CC3D = dcc.n

        def s4_extra():
            em.emit(V, lambda e: e.tensor_tensor(
                stc("s42"), stc("s4"), stc("s2"), ALU.mult))
        S4_V = bcast_scale("t0", "s4", "i4", [(dcc, CC3D), (sv, S2_V)],
                           extra=(s4_extra,))

        m2 = {"ec": [0, 0], "sp": [0, 0]}
        m2_pend = [None, None]
        for jn in range(NI * NQ):
            i, t = divmod(jn, NQ)
            r = jn % 2
            w = [(dsp, M1_WR)]
            if m2["ec"][r]:
                w.append((sv, m2["ec"][r]))
            if m2["sp"][r]:
                w.append((dsp, m2["sp"][r]))
            em.dma(SY, m_gin[r], spg2_h[i * 128:(i + 1) * 128,
                                        t * 512:(t + 1) * 512], dgr[r],
                   waits=w)
            RDg = dgr[r].n
            em.dma(SY, m_uin[r], spu_h[i * 128:(i + 1) * 128,
                                       t * 512:(t + 1) * 512], dur[r],
                   waits=([(sv, O_V)] if jn < 2 else []))
            RDu = dur[r].n
            if m2_pend[r] is not None:
                m2_pend[r]()
                m2_pend[r] = None
            em.emit(V, lambda e, r=r: e.tensor_scalar(
                m_tf[r], m_gin[r], stc("i4"), -32768.0, ALU.mult, ALU.max),
                waits=[(dgr[r], RDg), (sv, S4_V)])
            em.emit(V, lambda e, r=r: e.tensor_scalar(
                m_i32[r], m_tf[r], 32767.0, None, ALU.min))
            em.emit(V, lambda e, r=r: e.tensor_copy(m_gf[r], m_i32[r]))
            em.emit(V, lambda e, r=r: e.tensor_scalar(
                m_tf[r], m_uin[r], stc("i2"), -32768.0, ALU.mult, ALU.max),
                waits=[(dur[r], RDu)])
            em.emit(V, lambda e, r=r: e.tensor_scalar(
                m_ui32[r], m_tf[r], 32767.0, None, ALU.min))
            em.emit(V, lambda e, r=r: e.tensor_copy(m_uf[r], m_ui32[r]))
            em.emit(V, lambda e, r=r: e.tensor_tensor(
                m_tf[r], m_gf[r], m_uf[r], ALU.mult))
            em.emit(V, lambda e, r=r: e.tensor_scalar(
                m_out[r], m_tf[r], stc("s42"), None, ALU.mult))
            em.emit(V, lambda e, r=r: e.tensor_reduce(
                stc("t1"), m_out[r], AX.X, ALU.max, apply_absolute_value=True))
            em.emit(V, lambda e: e.tensor_tensor(
                mc("o"), mc("o"), stc("t1"), ALU.max))
            m2["ec"][r] = sv.n
            EC = m2["ec"][r]

            def spill(r=r, i=i, t=t, EC=EC):
                em.dma(SY, spo_h[i * 128:(i + 1) * 128,
                                 t * 512:(t + 1) * 512],
                       m_out[r], dsp, waits=[(sv, EC)])
                m2["sp"][r] = dsp.n
            m2_pend[r] = spill
        for r in range(2):
            if m2_pend[r] is not None:
                m2_pend[r]()
                m2_pend[r] = None
        M2_WR = dsp.n

        O5_V = fold_scalar(mc("o"), "t0",
                           [(sv, m2["ec"][0]), (sv, m2["ec"][1])])
        CC4 = collective(3, [(sv, O5_V)])
        em.dma(P, st[0:1, COL["t0"]:COL["t0"] + 2], cc_o[3], dcc,
               waits=[(scc, CC4)])
        CC4D = dcc.n
        S5_V = bcast_scale("t0", "s5", "i5", [(dcc, CC4D)])

        m3 = {"ec": [0, 0], "sp": [0, 0]}
        m3_pend = [None, None]
        for jn in range(NI * NQ):
            i, t = divmod(jn, NQ)
            r = jn % 2
            w = [(dsp, M2_WR)]
            if m3["ec"][r]:
                w.append((sv, m3["ec"][r]))
            if m3["sp"][r]:
                w.append((dsp, m3["sp"][r]))
            em.dma(SY, m_gin[r], spo_h[i * 128:(i + 1) * 128,
                                       t * 512:(t + 1) * 512], dgr[r], waits=w)
            RD = dgr[r].n
            if m3_pend[r] is not None:
                m3_pend[r]()
                m3_pend[r] = None
            em.emit(V, lambda e, r=r: e.tensor_scalar(
                m_tf[r], m_gin[r], stc("i5"), -32768.0, ALU.mult, ALU.max),
                waits=[(dgr[r], RD), (sv, S5_V)])
            em.emit(V, lambda e, r=r: e.tensor_scalar(
                m_i32[r], m_tf[r], 32767.0, None, ALU.min))
            em.emit(V, lambda e, r=r: e.tensor_copy(m_gf[r], m_i32[r]))
            em.emit(V, lambda e, r=r: e.tensor_scalar(
                m_out[r], m_gf[r], stc("s5"), None, ALU.mult))
            m3["ec"][r] = sv.n
            EC = m3["ec"][r]

            def spill3(r=r, i=i, t=t, EC=EC):
                em.dma(SY, spq_h[i * 128:(i + 1) * 128,
                                 t * 512:(t + 1) * 512],
                       m_out[r], dsp, waits=[(sv, EC)])
                m3["sp"][r] = dsp.n
            m3_pend[r] = spill3
        for r in range(2):
            if m3_pend[r] is not None:
                m3_pend[r]()
                m3_pend[r] = None
        M3_WR = dsp.n

        WH_A = [0]
        p8mm = [0, 0]
        p8y = [0, 0]
        ydma = []
        for eh in range(2):
            for i in range(NI):
                wfirst = ([(stn, ATT_MM), (dqw, LPBQ_WR)]
                          if (i == 0 and eh == 0) else
                          ([(sa, WH_A[0]), (stn, stn.n)]
                           if i == 0 and eh == 1 else []))
                em.dma(SY, wdTh[:, i * 512:(i + 1) * 512],
                       wqd_h[eh * 512:(eh + 1) * 512,
                             i * 128:(i + 1) * 128], dwt,
                       waits=wfirst, transpose=True)
            for k in range(NE):
                wld = [(spl, spl.n), (sv, sv.n)] if (k == 0 and eh == 0) \
                    else ([(sa, WH_A[0]), (stn, stn.n)] if k == 0
                          else [(sa, WH_A[0])])
                em.dma(A, wo_st[:, :],
                       wout_d[k * 128:(k + 1) * 128,
                              eh * 512:(eh + 1) * 512], dwl[0], waits=wld)
                WD = dwl[0].n
                wca = [(dwl[0], WD)]
                if eh == 0 and k == 0:
                    wca.append((stn, ATT_MM))
                em.emit(A, lambda e, k=k: e.activation(
                    wouth[:, k * 512:(k + 1) * 512], wo_st[:, :],
                    AF.Identity), waits=wca)
                WH_A[0] = sa.n
            WDT = dwt.n
            WHA = sa.n

            def p8_load(t):
                r = t % 2
                n8 = eh * NT + t
                w = [(dsp, M3_WR)]
                if n8 == 0:
                    w.append((stn, ATT_MM))
                if n8 >= 2:
                    w.append((stn, p8mm[r]))
                for i in range(NI):
                    em.dma(SY if i % 2 == 0 else A,
                           oqt[r][:, i * 128:(i + 1) * 128],
                           spq_h[i * 128:(i + 1) * 128,
                                 t * 128:(t + 1) * 128], doq[r],
                           waits=(w if i < 2 else []))
                wx = [(sv, p8y[r])] if n8 >= 2 else [(stn, ATT_MM)]
                em.dma(A, xrr[r], x_d[t * 128:(t + 1) * 128,
                                      eh * 512:(eh + 1) * 512], dxr[r],
                       waits=wx)
                return doq[r].n, dxr[r].n

            pend_y = [None]
            lded = {0: p8_load(0)}
            for t in range(NT):
                r = t % 2
                n8 = eh * NT + t
                if t + 1 < NT:
                    lded[t + 1] = p8_load(t + 1)
                DOQ, DXR = lded.pop(t)
                bk = r
                wl = [(sv, O_V), (sa, WHA), (dwt, WDT), (doq[r], DOQ),
                      (sa, OB_A)]
                if n8 >= 2:
                    wl.append((sv, p8y[r]))
                for k in range(NE):
                    em.emit(TN, lambda e, bk=bk, k=k, t=t: e.matmul(
                        pb[bk][:, :],
                        O_f[:, k * T + t * 128:k * T + (t + 1) * 128],
                        wouth[:, k * 512:(k + 1) * 512],
                        start=(k == 0), stop=False),
                        waits=(wl if k == 0 else ()))
                for i in range(NI):
                    em.emit(TN, lambda e, bk=bk, i=i, r=r: e.matmul(
                        pb[bk][:, :], oqt[r][:, i * 128:(i + 1) * 128],
                        wdTh[:, i * 512:(i + 1) * 512],
                        start=False, stop=False))
                em.emit(TN, lambda e, bk=bk, eh=eh: e.matmul(
                    pb[bk][:, :], ones16[0:1, 0:128],
                    ob16[0:1, eh * 512:(eh + 1) * 512],
                    start=False, stop=True), inc=stn)
                MM = stn.n
                p8mm[r] = MM
                wy = [(stn, MM), (dxr[r], DXR)]
                if len(ydma) >= 2:
                    wy.append((dy, ydma[-2]))
                em.emit(V, lambda e, bk=bk, r=r: e.tensor_tensor(
                    yt[r], pb[bk][:, :], xrr[r], ALU.add), waits=wy)
                YV = sv.n
                p8y[r] = YV
                em.dma(SY, y_d[t * 128:(t + 1) * 128,
                               eh * 512:(eh + 1) * 512], yt[r], dy,
                       waits=[(sv, YV)])
                ydma.append(dy.n)

        em.emit(P, lambda e: e.engine_nop(), waits=[(dy, dy.n)], no_chain=True)

        n_inst = len(em.steps)
        em.finalize(block)
    ends = []
    for a in nc.m.functions[0].allocations:
        if not hasattr(a, "memorylocations"):
            continue
        for ml in a.memorylocations:
            nm = getattr(ml, "name", "")
            if nm in ("arA", "arB", "arQ", "arK", "arV", "arW", "st", "nbE",
                      "nbO", "bcol", "bq8", "ob16", "id128", "ones32",
                      "ones16", "sel2", "selE", "selO", "qn2", "kn2",
                      "nbsrc", "mcol", "cst", "vland"):
                ends.append(ml.addr + ml.size() // 128)
    assert max(ends) <= 196608, f"SBUF overflow: {max(ends)}"
    if dbg:
        print(f"[build] {n_inst} emitted steps")
    return nc


_NC_CACHE = {}


def kernel(L0_attn_x, L0_attn_qkv_w, L0_attn_qkv_b, L0_attn_out_w,
           L0_attn_out_b, L0_ffn_w_gate, L0_ffn_w_up, L0_ffn_w_down):
    x = np.ascontiguousarray(L0_attn_x, dtype=np.float32)
    wqkv = np.ascontiguousarray(L0_attn_qkv_w, dtype=np.float32).reshape(E, I3)
    bqkv = np.ascontiguousarray(L0_attn_qkv_b, dtype=np.float32).reshape(1, I3)
    wout = np.ascontiguousarray(L0_attn_out_w, dtype=np.float32).reshape(E, E)
    bout = np.ascontiguousarray(L0_attn_out_b, dtype=np.float32).reshape(1, E)
    wg = np.ascontiguousarray(L0_ffn_w_gate, dtype=np.float32)
    wu = np.ascontiguousarray(L0_ffn_w_up, dtype=np.float32)
    wd = np.ascontiguousarray(L0_ffn_w_down, dtype=np.float32)

    if "nc" not in _NC_CACHE:
        _NC_CACHE["nc"] = build()
    nc = _NC_CACHE["nc"]
    in_maps = [{"x": np.ascontiguousarray(x[i]), "wqkv": wqkv, "bqkv": bqkv,
                "wout": wout, "bout": bout, "wg": wg, "wu": wu, "wd": wd}
               for i in range(8)]
    res = run_bass_kernel_spmd(nc, in_maps, core_ids=list(range(8)))
    return np.stack([res.results[i]["y"] for i in range(8)], axis=0)
